# version 05
# baseline (speedup 1.0000x reference)
import numpy as np
import ml_dtypes
from contextlib import ExitStack
from types import SimpleNamespace

import concourse.bass as bass
import concourse.mybir as mybir
from concourse.bass_utils import run_bass_kernel_spmd

F32 = mybir.dt.float32
BF16 = mybir.dt.bfloat16
AF = mybir.ActivationFunctionType
ALU = mybir.AluOpType
AX = mybir.AxisListType
BF = ml_dtypes.bfloat16

T = 2048
D = 1024
NTB = 4
TB = 512
HID = 2730
NJ = 22
NEG = -1.0e30
C_EIGHTH, C_EPS, C_TINY, C_INV1024, C_TWO, C_LN8 = 0, 1, 2, 3, 4, 5


def host_consts():
    c = {}
    c["identin"] = np.eye(128, dtype=np.float32)
    sel8 = np.zeros((8, 1024), np.float32)
    for h in range(8):
        sel8[h, 128 * h:128 * (h + 1)] = 1.0
    c["sel8in"] = sel8
    i2t = np.zeros((2, 128), np.float32)
    i2t[0, 0:64] = 1.0
    i2t[1, 64:128] = 1.0
    c["ind2tin"] = i2t
    r = np.arange(128)
    c["maskLEin"] = np.where(r[None, :] <= r[:, None], 0.0, NEG).astype(np.float32)
    c["maskGEin"] = np.where(r[None, :] >= r[:, None], 0.0, NEG).astype(np.float32)
    half = np.arange(0, 64, 2, dtype=np.float64)
    inv_freq = 1.0 / 10000.0 ** (half / 64)
    t = np.arange(T, dtype=np.float64)
    freqs = t[:, None] * inv_freq[None, :]
    base = (half + 0.4 * 64) / (1.4 * 64)
    power = (t - T // 2) / 256.0
    sc = base[None, :] ** power[:, None]
    cos, sin = np.cos(freqs), np.sin(freqs)
    sgn = np.concatenate([-np.ones(32), np.ones(32)])[:, None]

    def fm(tab):
        d64 = np.concatenate([tab.T, tab.T], axis=0)
        return np.concatenate([d64, d64], axis=0).astype(np.float32)

    def fmsgn(tab):
        d64 = np.concatenate([tab.T, tab.T], axis=0) * sgn
        return np.concatenate([d64, d64], axis=0).astype(np.float32)

    c["Tcq"] = fm(cos * sc)
    c["Tsq"] = fmsgn(sin * sc)
    c["Tck"] = fm(cos / sc)
    c["Tsk"] = fmsgn(sin / sc)
    return c


def host_layout(ins):
    d = {}
    d["A_wqT"] = np.ascontiguousarray(np.asarray(ins["L0_attn_Wq"]).T)
    d["A_wkT"] = np.ascontiguousarray(np.asarray(ins["L0_attn_Wk"]).T)
    d["A_wvT"] = np.ascontiguousarray(np.asarray(ins["L0_attn_Wv"]).T)
    d["A_woT"] = np.ascontiguousarray(np.asarray(ins["L0_attn_Wo"]).T)
    d["A_bq"] = np.ascontiguousarray(np.asarray(ins["L0_attn_bq"]).reshape(8, 128).T)
    d["A_bk"] = np.ascontiguousarray(np.asarray(ins["L0_attn_bk"]).reshape(8, 128).T)
    d["A_bvrow"] = np.asarray(ins["L0_attn_bv"]).reshape(1, 1024).copy()
    d["A_borow"] = np.asarray(ins["L0_attn_bo"]).reshape(1, 1024).copy()
    perm = np.concatenate([np.arange(32, 64), np.arange(32)])
    permfull = np.concatenate([64 * h + perm for h in range(8)])
    for l in range(2):
        w = np.asarray(ins["L0_ffn_wqkv"][l])
        d[f"B{l}_wqkv"] = np.ascontiguousarray(w)
        d[f"B{l}_wsh"] = np.ascontiguousarray(
            np.concatenate([w[:, permfull], w[:, 512 + permfull]], axis=1))
        d[f"B{l}_wo"] = np.ascontiguousarray(np.asarray(ins["L0_ffn_wo"][l]))
        d[f"B{l}_w1"] = np.ascontiguousarray(np.asarray(ins["L0_ffn_w1"][l]))
        w2p = np.zeros((NJ * 128, 1024), np.float32)
        w2p[:HID] = np.asarray(ins["L0_ffn_w2"][l])
        d[f"B{l}_w2"] = w2p
        qs = np.asarray(ins["L0_ffn_q_scale"][l]).astype(np.float32)
        ks = np.asarray(ins["L0_ffn_k_scale"][l]).astype(np.float32)
        d[f"B{l}_qs"] = np.tile(qs, 2).reshape(128, 1).copy()
        d[f"B{l}_qssh"] = np.tile(qs[perm], 2).reshape(128, 1).copy()
        d[f"B{l}_ks"] = np.tile(ks, 2).reshape(128, 1).copy()
        d[f"B{l}_kssh"] = np.tile(ks[perm], 2).reshape(128, 1).copy()
        d[f"B{l}_lng"] = np.ascontiguousarray(
            np.asarray(ins["L0_ffn_ln_g"][l]).reshape(8, 128).T)
        d[f"B{l}_lnb"] = np.ascontiguousarray(
            np.asarray(ins["L0_ffn_ln_b"][l]).reshape(8, 128).T)
        b1 = np.asarray(ins["L0_ffn_b1"][l])
        b1a = np.zeros(NJ * 128, np.float32); b1a[:HID] = b1[:HID]
        b1g = np.zeros(NJ * 128, np.float32); b1g[:HID] = b1[HID:]
        d[f"B{l}_b1a"] = np.ascontiguousarray(b1a.reshape(NJ, 128).T)
        d[f"B{l}_b1g"] = np.ascontiguousarray(b1g.reshape(NJ, 128).T)
        d[f"B{l}_b2"] = np.ascontiguousarray(
            np.asarray(ins["L0_ffn_b2"][l]).reshape(8, 128).T)
    return d


SEC = {"pe": "tensor", "act": "scalar", "dve": "vector",
       "pool": "gpsimd", "gq": "gpsimd", "sq": "sync"}


class Prog:
    def __init__(self, nc, ctx):
        self.nc = nc
        self.sem = {k: ctx.enter_context(nc.semaphore(f"s_{k}")) for k in SEC}
        self.cnt = {k: 0 for k in SEC}
        self.steps = []
        self.wm = {s: {} for s in set(SEC.values())}
        self.bank_state = [[] for _ in range(8)]

    def step(self, key, emit, deps=(), n=1, tag=""):
        flat = []
        for d in deps:
            if d is None:
                continue
            if isinstance(d, list):
                flat.extend(x for x in d if x is not None)
            else:
                flat.append(d)
        s = SimpleNamespace(key=key, emit=emit, tag=tag, n=n, deps=flat,
                            section=SEC[key])
        self.cnt[key] += n
        s.val = self.cnt[key]
        self.steps.append(s)
        return s

    def dma(self, key, transfers, deps=(), tag=""):
        assert key in ("sq", "gq")
        sem = self.sem[key]
        tr = list(transfers)

        def emit(eng):
            last = None
            for (o, i) in tr:
                last = eng.dma_start(o, i).then_inc(sem, 16)
            return last
        return self.step(key, emit, deps, n=16 * len(tr), tag=tag)

    def wait_only2(self, key, deps, tag=""):
        return self.wait_only(key, deps, tag)

    def wait_only(self, key, deps, tag=""):
        return self.step(key, (lambda eng: None), deps, n=0, tag=tag)

    def flush(self):
        steps, self.steps = self.steps, []
        if not steps:
            return
        secs = {}
        for s in steps:
            secs.setdefault(s.section, []).append(s)
        with self.nc.Block() as block:
            for secname, slist in secs.items():
                def make(slist=slist, secname=secname):
                    def body(eng):
                        for s in slist:
                            for d in s.deps:
                                if d.key == s.key and d.key not in ("sq", "gq"):
                                    continue
                                wm = self.wm[secname]
                                if wm.get(d.key, 0) >= d.val:
                                    continue
                                eng.wait_ge(self.sem[d.key], d.val)
                                wm[d.key] = d.val
                            last = s.emit(eng)
                            if s.key not in ("sq", "gq") and last is not None:
                                last.then_inc(self.sem[s.key], 1)
                    return body
                getattr(block, secname)(make())

    def use_bank(self, b):
        return list(self.bank_state[b] or [])

    def set_bank(self, b, step):
        self.bank_state[b] = step if isinstance(step, list) else [step]

    def add_bank_reader(self, b, step):
        if self.bank_state[b] is None:
            self.bank_state[b] = []
        self.bank_state[b].append(step)


def build_nc(upto="full"):
    nc = bass.Bass("TRN2", target_bir_lowering=False)
    E = {}

    def din(name, shape, dt=F32):
        E[name] = nc.dram_tensor(name, shape, dt, kind="ExternalInput")
        return E[name]

    din("xT", [D, T])
    din("identin", [128, 128]); din("sel8in", [8, 1024]); din("ind2tin", [2, 128])
    din("maskLEin", [128, 128]); din("maskGEin", [128, 128])
    for n in ["Tcq", "Tsq", "Tck", "Tsk"]:
        din(n, [128, T])
    din("A_wqT", [D, D]); din("A_wkT", [D, D]); din("A_wvT", [D, D]); din("A_woT", [D, D])
    din("A_bq", [128, 8]); din("A_bk", [128, 8])
    din("A_bvrow", [1, D]); din("A_borow", [1, D])
    for l in range(2):
        din(f"B{l}_wqkv", [D, 1536]); din(f"B{l}_wsh", [D, D])
        din(f"B{l}_wo", [512, D]); din(f"B{l}_w1", [D, 5460])
        din(f"B{l}_w2", [NJ * 128, D])
        for n in ["qs", "qssh", "ks", "kssh"]:
            din(f"B{l}_{n}", [128, 1])
        din(f"B{l}_lng", [128, 8]); din(f"B{l}_lnb", [128, 8])
        din(f"B{l}_b1a", [128, NJ]); din(f"B{l}_b1g", [128, NJ]); din(f"B{l}_b2", [128, 8])
    out = nc.dram_tensor("yT", [D, T], F32, kind="ExternalOutput")
    xa = nc.dram_tensor("xa_scr", [D, T], F32)
    xb = nc.dram_tensor("xb_scr", [D, T], F32)
    xc = nc.dram_tensor("xc_scr", [D, T], F32)
    yb = nc.dram_tensor("yb_scr", [D, T], BF16)

    def dram3(dten, col0, width=512):
        return bass.AP(dten, col0, [[T, 128], [128 * T, 8], [1, width]])

    ctx = ExitStack()
    with ctx:
        P = Prog(nc, ctx)
        sb = lambda name, shape, dt: ctx.enter_context(nc.sbuf_tensor(name, shape, dt))
        pb = [ctx.enter_context(nc.psum_tensor(f"pb{i}", [128, 512], F32))
              for i in range(8)]

        cst = sb("cst", [128, 8], F32)
        ident = sb("ident", [128, 128], F32)
        ident_bf = sb("ident_bf", [128, 128], BF16)
        sel8 = sb("sel8", [8, 1024], BF16)
        maskLE = sb("maskLE", [128, 128], BF16)
        maskGE = sb("maskGE", [128, 128], BF16)
        ones_row = sb("ones_row", [1, 512], BF16)
        ones_col = sb("ones_col", [128, 1], BF16)
        ones_row32 = sb("ones_row32", [1, 128], F32)
        ind2 = sb("ind2", [128, 2], BF16)
        ind2t = sb("ind2t", [2, 128], BF16)
        xroll = sb("xroll", [128, 2 * 8 * 516], F32)
        xrs = lambda s, ic: 516 * (8 * s + ic)
        negmax8 = sb("negmax8", [128, 8], F32)
        negtmp = sb("negtmp", [128, 2], F32)
        nm8 = sb("nm8", [8, 128], BF16)
        mrow = sb("mrow", [1, 512], F32)
        vrow = sb("vrow", [1, 512], F32)
        rstdr = sb("rstdr", [1, 512], F32)
        rrow = sb("rrow", [1, 256], F32)
        sspair = sb("sspair", [2, 512], F32)
        fpair = sb("fpair", [2, 512], BF16)
        Ab32 = sb("Ab32", [128, 512], F32)
        Bb32 = sb("Bb32", [128, 512], F32)
        rb32 = sb("rb32", [128, 256], F32)
        bqs = sb("bqs", [128, 8], F32)
        bkA = sb("bkA", [128, 8], F32)
        zbuf = sb("zbuf", [128, 8 * 513], F32)
        zc = lambda oc: 513 * oc + 1
        zsqs = sb("zsqs", [128, 2 * 512], BF16)
        zbfs = sb("zbfs", [128, 2 * 512], BF16)
        params = {}
        for l in range(2):
            for n in ["qs", "qssh", "ks", "kssh"]:
                params[(l, n)] = sb(f"p{l}{n}", [128, 1], F32)
            for n in ["lng", "lnb", "b2"]:
                params[(l, n)] = sb(f"p{l}{n}", [128, 8], F32)
            for n in ["b1a", "b1g"]:
                params[(l, n)] = sb(f"p{l}{n}", [128, NJ], F32)

        def init_pool(g):
            g.memset(ones_row[:, :], 1.0)
            g.memset(ones_col[:, :], 1.0)
            g.memset(ones_row32[:, :], 1.0)
            g.memset(ind2[:, :], 0.0)
            g.memset(ind2[0:64, 0:1], 1.0)
            g.memset(ind2[64:128, 1:2], 1.0)
            g.memset(cst[:, C_EIGHTH:C_EIGHTH + 1], 0.125)
            g.memset(cst[:, C_EPS:C_EPS + 1], 1e-5)
            g.memset(cst[:, C_TINY:C_TINY + 1], 1e-24)
            g.memset(cst[:, C_INV1024:C_INV1024 + 1], 1.0 / 1024.0)
            g.memset(cst[:, C_TWO:C_TWO + 1], 2.0)
            return g.memset(cst[:, C_LN8:C_LN8 + 1], 2.0794415416798357)
        st_ms = P.step("pool", init_pool, tag="init_memset")
        st_cst = P.dma("gq", [
            (ident_bf[:, :], E["identin"].ap()),
            (sel8[:, :], E["sel8in"].ap()),
            (ind2t[:, :], E["ind2tin"].ap()),
            (maskLE[:, :], E["maskLEin"].ap()),
            (maskGE[:, :], E["maskGEin"].ap()),
        ], tag="cast_consts")
        st_cf = P.dma("sq", [
            (ident[:, :], E["identin"].ap()),
            (bkA[:, :], E["A_bk"].ap()),
        ] + [(params[(l, n)][:, :], E[f"B{l}_{n}"].ap())
             for l in range(2)
             for n in ["qs", "qssh", "ks", "kssh", "lng", "lnb", "b2", "b1a", "b1g"]],
            tag="f32_consts")
        st_bq = P.dma("sq", [(bqs[:, :], E["A_bq"].ap())], tag="bq_load")
        st_bqs = P.step("dve", lambda e: e.tensor_scalar(
            bqs[:, :], bqs[:, :], cst[:, C_EIGHTH:C_EIGHTH + 1], None, ALU.mult),
            deps=[st_bq, st_ms], tag="bq_scale")
        P.wait_only("pool", [st_cst, st_cf, st_bq], tag="init_bar")
        P.flush()

        P.ln_ab = []
        P.ln_zr = []

        def ln_stats(dep8, zsrc):
            s1s = []
            csteps = []
            for oc in range(8):
                src = zsrc(oc)
                war = [csteps[oc - 2]] if oc >= 2 else [P.use_bank(6), P.use_bank(7)]
                s1 = P.step("act", lambda s, src=src, oc=oc: s.activation(
                    zsqs[:, 512 * (oc % 2): 512 * (oc % 2 + 1)], src, AF.Square),
                    deps=[dep8[oc]] + war, tag="lnsq")
                s1s.append(s1)
                s2 = P.step("dve", lambda e, src=src, oc=oc: e.tensor_copy(
                    zbfs[:, 512 * (oc % 2): 512 * (oc % 2 + 1)], src),
                    deps=[dep8[oc]] + war, tag="lnbf")

                def mmst(t, oc=oc):
                    t.matmul(pb[6][0:1, 0:512], ones_col[:, :],
                             zbfs[:, 512 * (oc % 2): 512 * (oc % 2 + 1)],
                             start=(oc == 0), stop=(oc == 7))
                    return t.matmul(pb[7][0:1, 0:512], ones_col[:, :],
                                    zsqs[:, 512 * (oc % 2): 512 * (oc % 2 + 1)],
                                    start=(oc == 0), stop=(oc == 7))
                csteps.append(P.step("pe", mmst, deps=[s1, s2], tag="lnmm"))

            def rows(e):
                e.tensor_scalar(mrow[:, :], pb[6][0:1, 0:512],
                                cst[0:1, C_INV1024:C_INV1024 + 1], None, ALU.mult)
                e.tensor_scalar(vrow[:, :], pb[7][0:1, 0:512],
                                cst[0:1, C_INV1024:C_INV1024 + 1], None, ALU.mult)
                e.tensor_mul(rstdr[:, :], mrow[:, :], mrow[:, :])
                return e.tensor_sub(vrow[:, :], vrow[:, :], rstdr[:, :])
            s4 = P.step("dve", rows, deps=[csteps[-1]], tag="lnrows")
            P.set_bank(6, s4)
            P.set_bank(7, s4)

            def rstd(s):
                s.activation(rstdr[:, :], vrow[:, :], AF.Ln,
                             bias=cst[0:1, C_EPS:C_EPS + 1])
                return s.activation(rstdr[:, :], rstdr[:, :], AF.Exp, scale=-0.5)
            s5 = P.step("act", rstd, deps=[s4], tag="lnrstd")

            def bc(t):
                t.matmul(pb[6][:, 0:512], ones_row32[:, :], rstdr[:, :],
                         start=True, stop=True)
                return t.matmul(pb[7][:, 0:512], ones_row32[:, :], mrow[:, :],
                                start=True, stop=True)
            s6 = P.step("pe", bc, deps=[s5, P.use_bank(6), P.use_bank(7)], tag="lnbc")

            def bce(s):
                s.copy(Ab32[:, :], pb[6][:, 0:512])
                return s.copy(Bb32[:, :], pb[7][:, 0:512])
            s7 = P.step("act", bce, deps=[s6] + P.ln_ab, tag="lnbce")
            P.set_bank(6, s7)
            P.set_bank(7, s7)
            return s7, s1s

        GA = ExitStack()
        with GA:
            asb = lambda name, shape, dt: GA.enter_context(
                nc.sbuf_tensor(name, shape, dt))
            wA = {n: asb(f"wA_{n}", [128, 8 * 1024], BF16) for n in "qkvo"}
            xbfA = asb("xbfA", [128, 2 * 8 * 516], BF16)
            xbs = lambda s, ic: 516 * (8 * s + ic)
            qA = asb("qA", [128, 8 * 512], BF16)
            kA = asb("kA", [128, 8 * 512], BF16)
            vA = asb("vA", [128, 4 * 1024], BF16)
            o_fmA = asb("o_fmA", [128, 8 * 512], BF16)
            PTA = asb("PTA", [128, 512], BF16)
            nm2 = asb("nm2", [2, 128], BF16)
            bvrow = asb("bvrow", [1, D], BF16)
            borow = asb("borow", [1, D], BF16)

            wsteps = {}
            for n, src in [("q", "A_wqT"), ("k", "A_wkT"), ("v", "A_wvT"),
                           ("o", "A_woT")]:
                sap = bass.AP(E[src], 0, [[1024, 128], [128 * 1024, 8], [1, 1024]])
                dap = wA[n].ap().rearrange("p (ic n) -> p ic n", ic=8)
                wsteps[n] = P.dma("gq", [(dap, sap)], tag=f"wA_{n}")
            st_brows = P.dma("gq", [(bvrow[:, :], E["A_bvrow"].ap()),
                                    (borow[:, :], E["A_borow"].ap())], tag="brows")
            P.wait_only("pool", [st_brows], tag="wA_bar")
            P.flush()

            prev_spill = None
            for tb in range(NTB):
                sl = tb % 2
                if tb == 0:
                    st_g = P.step("pool", lambda g: g.memset(
                        bass.AP(xbfA, xbs(0, 0),
                                [[2 * 8 * 516, 128], [516, 8], [1, 1]]), 0.0),
                        tag="xbf_guard")
                    st_xbf = P.dma("gq", [(
                        bass.AP(xbfA, xbs(sl, 0) + 1,
                                [[2 * 8 * 516, 128], [516, 8], [1, 512]]),
                        bass.AP(E["xT"], 0, [[T, 128], [128 * T, 8], [1, 512]]))],
                        deps=[st_g], tag="xbfA0")
                else:
                    st_xbf = P.dma("gq", [(
                        bass.AP(xbfA, xbs(sl, 0),
                                [[2 * 8 * 516, 128], [516, 8], [1, 513]]),
                        bass.AP(E["xT"], TB * tb - 1,
                                [[T, 128], [128 * T, 8], [1, 513]]))],
                        tag=f"xbfA{tb}")

                def xb_data(ic, sl=sl):
                    return xbfA[:, xbs(sl, ic) + 1: xbs(sl, ic) + 513]

                last_ev = None
                for which, wt, dst, brow, scale, wst in [
                        ("q", wA["q"], qA, bqs, 0.125, wsteps["q"]),
                        ("k", wA["k"], kA, bkA, 1.0, wsteps["k"])]:
                    for oc in range(8):
                        b = oc % 4

                        def mm_qk(t, wt=wt, oc=oc, b=b):
                            last = None
                            for ic in range(8):
                                last = t.matmul(
                                    pb[b][:, 0:512],
                                    wt[:, 1024 * ic + 128 * oc:
                                       1024 * ic + 128 * (oc + 1)],
                                    xb_data(ic), start=(ic == 0), stop=(ic == 7))
                            return last
                        stm = P.step("pe", mm_qk,
                                     deps=[st_xbf, wst, P.use_bank(b)],
                                     tag=f"A1{which}{oc}")

                        def ev(s, dst=dst, oc=oc, b=b, brow=brow, scale=scale):
                            return s.activation(
                                dst[:, 512 * oc: 512 * (oc + 1)],
                                pb[b][:, 0:512], AF.Identity,
                                bias=brow[:, oc:oc + 1], scale=scale)
                        ste = P.step("act", ev, deps=[stm, st_cf, st_bqs],
                                     tag=f"A1{which}e{oc}")
                        P.set_bank(b, ste)
                        last_ev = ste
                for tc in range(4):
                    for half in range(2):
                        b = 4 + (2 * tc + half) % 4

                        def mm_v(t, tc=tc, half=half, b=b, sl=sl):
                            for ic in range(8):
                                t.matmul(
                                    pb[b][:, 0:512],
                                    xbfA[:, xbs(sl, ic) + 1 + 128 * tc:
                                         xbs(sl, ic) + 1 + 128 * (tc + 1)],
                                    wA["v"][:, 1024 * ic + 512 * half:
                                            1024 * ic + 512 * (half + 1)],
                                    start=(ic == 0), stop=False)
                            return t.matmul(
                                pb[b][:, 0:512], ones_row[0:1, 0:128],
                                bvrow[0:1, 512 * half: 512 * (half + 1)],
                                start=False, stop=True)
                        stm = P.step("pe", mm_v,
                                     deps=[st_xbf, wsteps["v"], st_brows, st_ms,
                                           P.use_bank(b)], tag=f"A1v{tc}{half}")

                        def ev(s, tc=tc, half=half, b=b):
                            return s.copy(vA[:, 1024 * tc + 512 * half:
                                             1024 * tc + 512 * (half + 1)],
                                          pb[b][:, 0:512])
                        ste = P.step("act", ev, deps=[stm], tag=f"A1ve{tc}{half}")
                        P.set_bank(b, ste)
                        last_ev = ste

                pta_reader = None
                rb_reader = None
                for wi in range(2):
                    nmsteps = []
                    for j in range(2):
                        qoff = 256 * wi + 128 * j
                        bS = 4

                        def mm_s(t, wi=wi, j=j, qoff=qoff, bS=bS):
                            span = 128 * (j + 1)
                            for ic in range(8):
                                t.matmul(
                                    pb[bS][:, 0:span],
                                    qA[:, 512 * ic + qoff: 512 * ic + qoff + 128],
                                    kA[:, 512 * ic + 256 * wi:
                                       512 * ic + 256 * wi + span],
                                    start=(ic == 0), stop=False)
                            return t.matmul(
                                pb[bS][:, 128 * j: 128 * (j + 1)],
                                ident_bf[:, :], maskLE[:, :],
                                start=False, stop=True)
                        stm = P.step("pe", mm_s,
                                     deps=[last_ev, st_cst, P.use_bank(bS)],
                                     tag=f"A2s{wi}{j}")

                        def red(e, j=j, wi=wi, bS=bS):
                            return e.reduce_max(
                                negmax8[:, 2 * wi + j: 2 * wi + j + 1],
                                pb[bS][:, 0:128 * (j + 1)], axis=AX.X, negate=True)
                        str_ = P.step("dve", red, deps=[stm], tag=f"A2r{wi}{j}")
                        P.set_bank(bS, str_)
                        nmsteps.append(str_)

                    def tp(t, wi=wi):
                        return t.transpose(
                            bass.AP(pb[5], 0, [[512, 2], [1, 128]]),
                            negmax8[:, 2 * wi: 2 * wi + 2], ident[:, :])
                    sttp = P.step("pe", tp, deps=nmsteps + [st_cf, P.use_bank(5)],
                                  tag=f"A2tp{wi}")
                    stnm = P.step("act", lambda s: s.copy(
                        nm2[:, :], bass.AP(pb[5], 0, [[512, 2], [1, 128]])),
                        deps=[sttp], tag=f"A2nm{wi}")
                    P.set_bank(5, stnm)

                    for j in range(2):
                        qoff = 256 * wi + 128 * j
                        bT = 6

                        def mm_st(t, wi=wi, j=j, qoff=qoff, bT=bT):
                            for kc in range(j + 1):
                                for ic in range(8):
                                    t.matmul(
                                        pb[bT][:, 128 * kc: 128 * (kc + 1)],
                                        kA[:, 512 * ic + 256 * wi + 128 * kc:
                                           512 * ic + 256 * wi + 128 * (kc + 1)],
                                        qA[:, 512 * ic + qoff: 512 * ic + qoff + 128],
                                        start=(ic == 0 and kc == 0), stop=False)
                            t.matmul(
                                pb[bT][:, 128 * j: 128 * (j + 1)],
                                ident_bf[:, :], maskGE[:, :],
                                start=False, stop=False)
                            nrep = bass.AP(nm2, 0, [[128, 2], [0, j + 1], [1, 128]])
                            return t.matmul(
                                pb[bT][:, 0:128 * (j + 1)],
                                sel8[0:2, 128 * j: 128 * (j + 1)],
                                nrep, start=False, stop=True)
                        stm = P.step("pe", mm_st, deps=[stnm, P.use_bank(bT)],
                                     tag=f"A2t{wi}{j}")
                        stexp = P.step("act", lambda s, j=j, bT=bT: s.activation(
                            PTA[:, 0:128 * (j + 1)], pb[bT][:, 0:128 * (j + 1)],
                            AF.Exp), deps=[stm, pta_reader], tag=f"A2e{wi}{j}")
                        P.set_bank(bT, stexp)
                        bU = 7

                        def mm_u(t, j=j, bU=bU):
                            last = None
                            for kc in range(j + 1):
                                last = t.matmul(
                                    pb[bU][0:1, 0:128], ones_col[:, :],
                                    PTA[:, 128 * kc: 128 * (kc + 1)],
                                    start=(kc == 0), stop=(kc == j))
                            return last
                        stu = P.step("pe", mm_u, deps=[stexp, P.use_bank(bU)],
                                     tag=f"A2u{wi}{j}")
                        strc = P.step("dve", lambda e, bU=bU: e.reciprocal(
                            rrow[0:1, 0:128], pb[bU][0:1, 0:128]), deps=[stu],
                            tag=f"A2rc{wi}{j}")
                        P.set_bank(bU, strc)
                        bR = 5
                        strb = P.step("pe", lambda t, bR=bR: t.matmul(
                            pb[bR][:, 0:128], ones_row32[:, :], rrow[0:1, 0:128],
                            start=True, stop=True),
                            deps=[strc, P.use_bank(bR)], tag=f"A2rb{wi}{j}")
                        strbe = P.step("act", lambda s, bR=bR: s.copy(
                            rb32[:, 0:128], pb[bR][:, 0:128]), deps=[strb, rb_reader],
                            tag=f"A2rbe{wi}{j}")
                        P.set_bank(bR, strbe)

                        def mm_pv(t, wi=wi, j=j):
                            last = None
                            for oc in range(8):
                                bV = oc // 4
                                for kc in range(j + 1):
                                    tc = 2 * wi + kc
                                    last = t.matmul(
                                        pb[bV][:, 128 * (oc % 4):
                                               128 * (oc % 4 + 1)],
                                        vA[:, 1024 * tc + 128 * oc:
                                           1024 * tc + 128 * (oc + 1)],
                                        PTA[:, 128 * kc: 128 * (kc + 1)],
                                        start=(kc == 0), stop=(kc == j))
                            return last
                        stpv = P.step("pe", mm_pv,
                                      deps=[stexp, P.use_bank(0), P.use_bank(1)],
                                      tag=f"A2pv{wi}{j}")

                        def ev_o(e, wi=wi, j=j, qoff=qoff):
                            last = None
                            for oc in range(8):
                                bV = oc // 4
                                last = e.tensor_tensor(
                                    o_fmA[:, 512 * oc + qoff:
                                          512 * oc + qoff + 128],
                                    pb[bV][:, 128 * (oc % 4): 128 * (oc % 4 + 1)],
                                    rb32[:, 0:128], ALU.mult)
                            return last
                        stoe = P.step("dve", ev_o, deps=[stpv, strbe],
                                      tag=f"A2oe{wi}{j}")
                        P.set_bank(0, stoe)
                        P.set_bank(1, stoe)
                        pta_reader = stpv
                        rb_reader = stoe
                    P.flush()

                if tb == 0:
                    st_gx = P.step("pool", lambda g, sl=sl: g.memset(
                        bass.AP(xroll, xrs(sl, 0),
                                [[2 * 8 * 516, 128], [516, 8], [1, 1]]), 0.0),
                        tag="xroll_guard")
                    st_x32 = P.dma("sq", [(
                        bass.AP(xroll, xrs(sl, 0) + 1,
                                [[2 * 8 * 516, 128], [516, 8], [1, 512]]),
                        bass.AP(E["xT"], 0, [[T, 128], [128 * T, 8], [1, 512]]))],
                        deps=[st_gx], tag=f"x32_{tb}")
                else:
                    st_x32 = P.dma("sq", [(
                        bass.AP(xroll, xrs(sl, 0),
                                [[2 * 8 * 516, 128], [516, 8], [1, 513]]),
                        bass.AP(E["xT"], TB * tb - 1,
                                [[T, 128], [128 * T, 8], [1, 513]]))],
                        tag=f"x32_{tb}")
                zst = []
                for oc in range(8):
                    b = 4 + oc % 2

                    def mm_wo(t, oc=oc, b=b):
                        for ic in range(8):
                            t.matmul(
                                pb[b][:, 0:512],
                                wA["o"][:, 1024 * ic + 128 * oc:
                                        1024 * ic + 128 * (oc + 1)],
                                o_fmA[:, 512 * ic: 512 * (ic + 1)],
                                start=(ic == 0), stop=False)
                        return t.matmul(
                            pb[b][:, 0:512],
                            borow[0:1, 128 * oc: 128 * (oc + 1)],
                            ones_row[0:1, 0:512], start=False, stop=True)
                    stm = P.step("pe", mm_wo,
                                 deps=[st_brows, wsteps["o"], P.use_bank(b)],
                                 tag=f"A3wo{oc}")

                    def ev(e, oc=oc, b=b, sl=sl):
                        return e.scalar_tensor_tensor(
                            zbuf[:, zc(oc): zc(oc) + 512],
                            xroll[:, xrs(sl, oc) + 1: xrs(sl, oc) + 513],
                            cst[:, C_TWO:C_TWO + 1],
                            pb[b][:, 0:512], ALU.mult, ALU.add)
                    ste = P.step("dve", ev,
                                 deps=[stm, st_x32, prev_spill] + P.ln_zr,
                                 tag=f"A3z{oc}")
                    P.set_bank(b, ste)
                    zst.append(ste)

                s7, s1s = ln_stats(zst, lambda oc: zbuf[:, zc(oc): zc(oc) + 512])
                appst = []
                for oc in range(8):
                    src = zbuf[:, zc(oc): zc(oc) + 512]

                    def app(g, src=src):
                        g.tensor_sub(src, src, Bb32[:, :])
                        return g.tensor_mul(src, src, Ab32[:, :])
                    appst.append(P.step("pool", app, deps=[s7], tag="lnappA"))
                P.ln_ab = list(appst)
                P.ln_zr = s1s + appst
                tr = [(dram3(xa, TB * tb),
                       bass.AP(zbuf, 1, [[8 * 513, 128], [513, 8], [1, 512]]))]
                if upto == "A":
                    tr.append((dram3(out, TB * tb),
                               bass.AP(zbuf, 1, [[8 * 513, 128], [513, 8], [1, 512]])))
                prev_spill = P.dma("sq", tr, deps=appst, tag=f"y0spill{tb}")

            P.wait_only("pool", [prev_spill], tag="drainA")
            P.flush()

        if upto == "A":
            with nc.Block() as block:
                @block.sync
                def _(s):
                    s.wait_ge(P.sem["sq"], P.cnt["sq"])
                    s.wait_ge(P.sem["gq"], P.cnt["gq"])
            return nc

        def emit_layer(l, sin, sout, want_out):
            pqn, pqsh = params[(l, "qs")], params[(l, "qssh")]
            pkn, pksh = params[(l, "ks")], params[(l, "kssh")]
            L1 = ExitStack()
            x1_spills = [None] * NTB
            y_spills = [None] * NTB
            with L1:
                lsb = lambda name, shape, dt: L1.enter_context(
                    nc.sbuf_tensor(name, shape, dt))
                tabs = {n: lsb(f"tab{n}{l}", [128, T], BF16)
                        for n in ["cq", "sq", "ck", "sk"]}
                wqkvB = lsb(f"wqkv{l}", [128, 8 * 1536], BF16)
                wshB = lsb(f"wsh{l}", [128, 8 * 1024], BF16)
                woB = lsb(f"wo{l}", [128, 4 * 1024], BF16)
                xbfB = lsb(f"xbf{l}", [128, 2 * 8 * 516], BF16)
                xbs2 = lambda s, ic: 516 * (8 * s + ic)
                q_rot = lsb(f"qrot{l}", [128, 4 * 512], BF16)
                k_rot = lsb(f"krot{l}", [128, 2 * 4 * 512], BF16)
                v_aug = lsb(f"vaug{l}", [128, 2 * 4 * 520], BF16)
                o_fm = lsb(f"ofm{l}", [128, 4 * 512], BF16)
                PT = lsb(f"PT{l}", [128, 2 * 640], BF16)
                ybuf = lsb(f"ybuf{l}", [128, 8 * 512], BF16)
                ybs = lambda s, ic: 512 * ic
                qsqb = lsb(f"qsq{l}", [128, 2 * 512], BF16)
                etmp = lsb(f"etmp{l}", [128, 512], F32)
                e1b = lsb(f"e1{l}", [128, 2 * 512], F32)
                e2b = lsb(f"e2{l}", [128, 2 * 512], F32)

                st_tab = P.dma("gq", [(tabs[n][:, :], E["T" + n].ap())
                                      for n in ["cq", "sq", "ck", "sk"]],
                               tag=f"tabs{l}")
                st_wqkv = P.dma("gq", [(
                    wqkvB.ap().rearrange("p (ic n) -> p ic n", ic=8),
                    bass.AP(E[f"B{l}_wqkv"], 0,
                            [[1536, 128], [128 * 1536, 8], [1, 1536]]))],
                    tag=f"wqkv{l}")
                st_wsh = P.dma("gq", [(
                    wshB.ap().rearrange("p (ic n) -> p ic n", ic=8),
                    bass.AP(E[f"B{l}_wsh"], 0,
                            [[1024, 128], [128 * 1024, 8], [1, 1024]]))],
                    tag=f"wsh{l}")
                st_wo = P.dma("gq", [(
                    woB.ap().rearrange("p (ic n) -> p ic n", ic=4),
                    bass.AP(E[f"B{l}_wo"], 0,
                            [[1024, 128], [128 * 1024, 4], [1, 1024]]))],
                    tag=f"woL{l}")
                st_vone = P.step("pool", lambda g: g.memset(
                    bass.AP(v_aug, 64, [[2 * 4 * 520, 128], [65, 64], [1, 1]]),
                    1.0), tag=f"vone{l}")
                P.flush()

                guard_prev = None
                ptw = {0: None, 1: None}
                rbw = None
                rrow_w = None
                prev_fb = None
                qk_last = None

                for tb in range(NTB):
                    sl = tb % 2
                    w = tb
                    if tb == 0:
                        stg = P.step("pool", lambda g: g.memset(
                            bass.AP(xbfB, xbs2(0, 0),
                                    [[2 * 8 * 516, 128], [516, 8], [1, 1]]), 0.0),
                            tag=f"xbfg{l}")
                        st_xbf = P.dma("gq", [(
                            bass.AP(xbfB, xbs2(sl, 0) + 1,
                                    [[2 * 8 * 516, 128], [516, 8], [1, 512]]),
                            dram3(sin, 0))], deps=[stg], tag=f"xbfB{l}_0")
                    else:
                        st_xbf = P.dma("gq", [(
                            bass.AP(xbfB, xbs2(sl, 0),
                                    [[2 * 8 * 516, 128], [516, 8], [1, 513]]),
                            dram3(sin, TB * tb - 1, 513))], tag=f"xbfB{l}_{tb}")

                    def hsrc(ic, t0, width, sl=sl):
                        off = 1 if ic < 4 else 0
                        return xbfB[:, xbs2(sl, ic) + off + t0:
                                    xbs2(sl, ic) + off + t0 + width]

                    for cx in range(8):
                        isq = cx < 4
                        c = cx % 4
                        colbase = (0 if isq else 512) + 128 * c
                        bq_, bsh_, bss_, bfb_ = [(0, 1, 2, 3), (4, 5, 2, 3)][cx % 2]
                        Tc = tabs["cq"] if isq else tabs["ck"]
                        Ts = tabs["sq"] if isq else tabs["sk"]
                        ppn = pqn if isq else pkn
                        ppsh = pqsh if isq else pksh
                        dstt = q_rot if isq else k_rot
                        dstoff = 512 * c if isq else 512 * (4 * sl + c)

                        def mm_q(t, colbase=colbase, bq_=bq_):
                            last = None
                            for ic in range(8):
                                last = t.matmul(
                                    pb[bq_][:, 0:512],
                                    wqkvB[:, 1536 * ic + colbase:
                                          1536 * ic + colbase + 128],
                                    hsrc(ic, 0, 512),
                                    start=(ic == 0), stop=(ic == 7))
                            return last
                        smq = P.step("pe", mm_q,
                                     deps=[st_xbf, st_wqkv, P.use_bank(bq_)],
                                     tag=f"B1q{l}{tb}{cx}")

                        def mm_sh(t, colbase=colbase, bsh_=bsh_):
                            last = None
                            for ic in range(8):
                                last = t.matmul(
                                    pb[bsh_][:, 0:512],
                                    wshB[:, 1024 * ic + colbase:
                                         1024 * ic + colbase + 128],
                                    hsrc(ic, 0, 512),
                                    start=(ic == 0), stop=(ic == 7))
                            return last
                        smsh = P.step("pe", mm_sh,
                                      deps=[st_xbf, st_wsh, P.use_bank(bsh_)],
                                      tag=f"B1s{l}{tb}{cx}")
                        ssq = P.step("act", lambda s, bq_=bq_, cx=cx: s.activation(
                            qsqb[:, 512 * (cx % 2): 512 * (cx % 2 + 1)],
                            pb[bq_][:, 0:512], AF.Square),
                            deps=[smq], tag=f"B1sq{l}{tb}{cx}")
                        sss = P.step("pe", lambda t, bss_=bss_, cx=cx: t.matmul(
                            pb[bss_][0:2, 0:512], ind2[:, :],
                            qsqb[:, 512 * (cx % 2): 512 * (cx % 2 + 1)],
                            start=True, stop=True),
                            deps=[ssq, st_ms, P.use_bank(bss_)],
                            tag=f"B1ss{l}{tb}{cx}")

                        def fca(s, bss_=bss_, isq=isq):
                            s.activation(sspair[:, :], pb[bss_][0:2, 0:512],
                                         AF.Ln, bias=cst[0:2, C_TINY:C_TINY + 1])
                            return s.activation(
                                fpair[:, :], sspair[:, :], AF.Exp, scale=-0.5,
                                bias=(cst[0:2, C_LN8:C_LN8 + 1] if isq else 0.0))
                        sfc = P.step("act", fca, deps=[sss, prev_fb],
                                     tag=f"B1f{l}{tb}{cx}")
                        P.set_bank(bss_, sfc)
                        sfb = P.step("pe", lambda t, bfb_=bfb_: t.matmul(
                            pb[bfb_][:, 0:512], ind2t[:, :], fpair[:, :],
                            start=True, stop=True),
                            deps=[sfc, P.use_bank(bfb_)], tag=f"B1fb{l}{tb}{cx}")
                        prev_fb = sfb
                        se1 = P.step("dve", lambda e, bq_=bq_, cx=cx, Tc=Tc,
                                     ppn=ppn, tb=tb: e.scalar_tensor_tensor(
                                         e1b[:, 512 * (cx % 2): 512 * (cx % 2 + 1)],
                                         pb[bq_][:, 0:512], ppn[:, :],
                                         Tc[:, TB * tb: TB * (tb + 1)],
                                         ALU.mult, ALU.mult),
                                     deps=[smq, ssq, st_tab, st_cf],
                                     tag=f"B1e1{l}{tb}{cx}")
                        P.set_bank(bq_, [ssq, se1])
                        se2 = P.step("dve", lambda e, bsh_=bsh_, cx=cx, Ts=Ts,
                                     ppsh=ppsh, tb=tb: e.scalar_tensor_tensor(
                                         e2b[:, 512 * (cx % 2): 512 * (cx % 2 + 1)],
                                         pb[bsh_][:, 0:512], ppsh[:, :],
                                         Ts[:, TB * tb: TB * (tb + 1)],
                                         ALU.mult, ALU.mult),
                                     deps=[smsh, st_tab, st_cf],
                                     tag=f"B1e2{l}{tb}{cx}")
                        P.set_bank(bsh_, se2)
                        sadd = P.step("pool", lambda g, cx=cx: g.tensor_add(
                            e1b[:, 512 * (cx % 2): 512 * (cx % 2 + 1)],
                            e1b[:, 512 * (cx % 2): 512 * (cx % 2 + 1)],
                            e2b[:, 512 * (cx % 2): 512 * (cx % 2 + 1)]),
                            deps=[se1, se2], tag=f"B1a{l}{tb}{cx}")
                        sfm = P.step("dve", lambda e, cx=cx, dstt=dstt,
                                     dstoff=dstoff, bfb_=bfb_: e.tensor_tensor(
                                         dstt[:, dstoff: dstoff + 512],
                                         e1b[:, 512 * (cx % 2): 512 * (cx % 2 + 1)],
                                         pb[bfb_][:, 0:512], ALU.mult),
                                     deps=[sadd, sfb], tag=f"B1fm{l}{tb}{cx}")
                        P.set_bank(bfb_, sfm)
                        qk_last = sfm
                    last_vev = None
                    for tc in range(4):
                        b = (0, 1, 4, 5)[tc]

                        def mm_v(t, tc=tc, b=b):
                            last = None
                            for ic in range(8):
                                last = t.matmul(
                                    pb[b][:, 0:512],
                                    hsrc(ic, 128 * tc, 128),
                                    wqkvB[:, 1536 * ic + 1024: 1536 * ic + 1536],
                                    start=(ic == 0), stop=(ic == 7))
                            return last
                        smv = P.step("pe", mm_v,
                                     deps=[st_xbf, st_wqkv, P.use_bank(b)],
                                     tag=f"B1v{l}{tb}{tc}")
                        vdst = bass.AP(v_aug, 520 * (4 * sl + tc),
                                       [[2 * 4 * 520, 128], [65, 8], [1, 64]])
                        psrc = bass.AP(pb[b], 0, [[512, 128], [64, 8], [1, 64]])
                        sev = P.step("act", lambda s, vdst=vdst, psrc=psrc: s.copy(
                            vdst, psrc), deps=[smv, st_vone],
                            tag=f"B1ve{l}{tb}{tc}")
                        P.set_bank(b, sev)
                        last_vev = sev

                    for qt in range(4):
                        redst = []
                        for h in range(8):
                            hc = h // 2
                            hr = 64 * (h % 2)
                            bl = h % 2
                            bc_ = 2 + h % 2
                            lbspan = 512 - 128 * qt
                            curspan = 128 * (qt + 1)
                            qsl = q_rot[hr:hr + 64, 512 * hc + 128 * qt:
                                        512 * hc + 128 * (qt + 1)]

                            def mm_stat(t, hc=hc, hr=hr, bl=bl, bc_=bc_, qt=qt,
                                        qsl=qsl, lbspan=lbspan, curspan=curspan,
                                        sl=sl, w=w):
                                if w > 0:
                                    kof = 512 * (4 * ((w - 1) % 2) + hc) + 128 * qt
                                    t.matmul(pb[bl][:, 0:lbspan], qsl,
                                             k_rot[hr:hr + 64, kof: kof + lbspan],
                                             start=True, stop=False)
                                    t.matmul(pb[bl][:, 0:128], ident_bf[:, :],
                                             maskGE[:, :], start=False, stop=True)
                                kof = 512 * (4 * sl + hc)
                                t.matmul(pb[bc_][:, 0:curspan], qsl,
                                         k_rot[hr:hr + 64, kof: kof + curspan],
                                         start=True, stop=False)
                                return t.matmul(
                                    pb[bc_][:, 128 * qt: 128 * (qt + 1)],
                                    ident_bf[:, :], maskLE[:, :],
                                    start=False, stop=True)
                            sstat = P.step("pe", mm_stat,
                                           deps=[last_vev, qk_last, st_cst,
                                                 P.use_bank(bl), P.use_bank(bc_)],
                                           tag=f"B2st{l}{tb}{qt}{h}")

                            def red(e, h=h, bl=bl, bc_=bc_, w=w,
                                    lbspan=lbspan, curspan=curspan):
                                if w > 0:
                                    e.reduce_max(negtmp[:, 0:1],
                                                 pb[bl][:, 0:lbspan],
                                                 axis=AX.X, negate=True)
                                    e.reduce_max(negtmp[:, 1:2],
                                                 pb[bc_][:, 0:curspan],
                                                 axis=AX.X, negate=True)
                                    return e.tensor_tensor(
                                        negmax8[:, h:h + 1], negtmp[:, 0:1],
                                        negtmp[:, 1:2], ALU.min)
                                return e.reduce_max(negmax8[:, h:h + 1],
                                                    pb[bc_][:, 0:curspan],
                                                    axis=AX.X, negate=True)
                            sred = P.step("dve", red, deps=[sstat],
                                          tag=f"B2r{l}{tb}{qt}{h}")
                            P.set_bank(bl, sred)
                            P.set_bank(bc_, sred)
                            redst.append(sred)
                        sttp = P.step("pe", lambda t: t.transpose(
                            bass.AP(pb[7], 0, [[512, 8], [1, 128]]),
                            negmax8[:, :], ident[:, :]),
                            deps=redst + [P.use_bank(7)], tag=f"B2tp{l}{tb}{qt}")
                        stnm8 = P.step("act", lambda s: s.copy(
                            nm8[:, :], bass.AP(pb[7], 0, [[512, 8], [1, 128]])),
                            deps=[sttp], tag=f"B2nm{l}{tb}{qt}")
                        P.set_bank(7, stnm8)

                        for hp in range(4):
                            bV = 4 + hp % 2
                            pvst = []
                            for hh in range(2):
                                h = 2 * hp + hh
                                hc = h // 2
                                hr = 64 * hh
                                bT1 = hh
                                bT2 = 2 + hh
                                qsl = q_rot[hr:hr + 64, 512 * hc + 128 * qt:
                                            512 * hc + 128 * (qt + 1)]
                                chunks = []
                                if w > 0:
                                    for kc in range(qt, 4):
                                        chunks.append(((w - 1) % 2, kc,
                                                       maskLE if kc == qt else None))
                                for kc in range(0, qt + 1):
                                    chunks.append((sl, kc,
                                                   maskGE if kc == qt else None))
                                nch = len(chunks)

                                def mm_stq(t, hc=hc, hr=hr, qsl=qsl, h=h,
                                           chunks=chunks, bT1=bT1, bT2=bT2):
                                    last = None
                                    nch = len(chunks)
                                    n1 = min(nch, 4)
                                    for pos, (par, kc, msk) in enumerate(chunks):
                                        bT = bT1 if pos < 4 else bT2
                                        fo = 128 * (pos % 4)
                                        kof = 512 * (4 * par + hc) + 128 * kc
                                        t.matmul(pb[bT][:, fo: fo + 128],
                                                 k_rot[hr:hr + 64, kof: kof + 128],
                                                 qsl, start=(pos in (0, 4)),
                                                 stop=False)
                                        if msk is not None:
                                            t.matmul(pb[bT][:, fo: fo + 128],
                                                     ident_bf[:, :], msk[:, :],
                                                     start=False, stop=False)
                                    nrep1 = bass.AP(nm8, 0,
                                                    [[128, 8], [0, n1], [1, 128]])
                                    last = t.matmul(
                                        pb[bT1][:, 0:128 * n1],
                                        sel8[:, 128 * h: 128 * (h + 1)],
                                        nrep1, start=False, stop=True)
                                    if nch > 4:
                                        nrep2 = bass.AP(nm8, 0,
                                                        [[128, 8], [0, nch - 4],
                                                         [1, 128]])
                                        last = t.matmul(
                                            pb[bT2][:, 0:128 * (nch - 4)],
                                            sel8[:, 128 * h: 128 * (h + 1)],
                                            nrep2, start=False, stop=True)
                                    return last
                                smT = P.step("pe", mm_stq,
                                             deps=[stnm8, P.use_bank(bT1),
                                                   P.use_bank(bT2)],
                                             tag=f"B2t{l}{tb}{qt}{h}")

                                def expf(s, h=h, hh=hh, nch=nch, bT1=bT1, bT2=bT2):
                                    po = 640 * hh
                                    n1 = min(nch, 4) * 128
                                    r = s.activation(PT[:, po: po + n1],
                                                     pb[bT1][:, 0:n1], AF.Exp)
                                    if nch > 4:
                                        r = s.activation(
                                            PT[:, po + 512:
                                               po + 512 + 128 * (nch - 4)],
                                            pb[bT2][:, 0:128 * (nch - 4)], AF.Exp)
                                    return r
                                sexp = P.step("act", expf, deps=[smT, ptw[hh]],
                                              tag=f"B2e{l}{tb}{qt}{h}")
                                P.set_bank(bT1, sexp)
                                P.set_bank(bT2, sexp)

                                def mm_pv(t, h=h, hh=hh, chunks=chunks, bV=bV):
                                    last = None
                                    po = 640 * hh
                                    for pos, (par, kc, msk) in enumerate(chunks):
                                        vof = 520 * (4 * par + kc) + 65 * h
                                        fo = po + (128 * pos if pos < 4
                                                   else 512 + 128 * (pos - 4))
                                        last = t.matmul(
                                            pb[bV][0:65, 128 * hh: 128 * hh + 128],
                                            v_aug[:, vof: vof + 65],
                                            PT[:, fo: fo + 128],
                                            start=(pos == 0),
                                            stop=(pos == len(chunks) - 1))
                                    return last
                                dpv = [sexp] + (P.use_bank(bV) if hh == 0 else [])
                                spv = P.step("pe", mm_pv, deps=dpv,
                                             tag=f"B2pv{l}{tb}{qt}{h}")
                                ptw[hh] = spv
                                pvst.append(spv)
                            def rcp(s, bV=bV):
                                s.activation(rrow[0:1, 0:256],
                                             pb[bV][64:65, 0:256], AF.Ln)
                                return s.activation(rrow[0:1, 0:256],
                                                    rrow[0:1, 0:256], AF.Exp,
                                                    scale=-1.0)
                            src_ = P.step("act", rcp, deps=pvst + [rrow_w],
                                          tag=f"B2rc{l}{tb}{qt}{hp}")
                            rrow_w = src_
                            bRB = 6
                            srb = P.step("pe", lambda t, bRB=bRB: t.matmul(
                                pb[bRB][0:64, 0:256], ones_row32[0:1, 0:64],
                                rrow[0:1, 0:256], start=True, stop=True),
                                deps=[src_, P.use_bank(bRB)],
                                tag=f"B2rb{l}{tb}{qt}{hp}")
                            rrow_w = srb
                            srbe = P.step("act", lambda s, bRB=bRB: s.copy(
                                rb32[0:64, 0:256], pb[bRB][0:64, 0:256]),
                                deps=[srb, rbw], tag=f"B2rbe{l}{tb}{qt}{hp}")
                            P.set_bank(bRB, srbe)
                            soes = []
                            for hh in range(2):
                                h = 2 * hp + hh
                                hc = h // 2
                                hr = 64 * hh
                                soe = P.step("dve", lambda e, hc=hc, hr=hr, qt=qt,
                                             hh=hh, bV=bV: e.tensor_tensor(
                                                 o_fm[hr:hr + 64,
                                                      512 * hc + 128 * qt:
                                                      512 * hc + 128 * (qt + 1)],
                                                 pb[bV][0:64,
                                                        128 * hh: 128 * hh + 128],
                                                 rb32[0:64,
                                                      128 * hh: 128 * hh + 128],
                                                 ALU.mult),
                                             deps=[pvst[hh], srbe],
                                             tag=f"B2oe{l}{tb}{qt}{h}")
                                soes.append(soe)
                            P.set_bank(bV, soes)
                            rbw = soes
                    P.flush()

                    if tb == 0:
                        stgx = P.step("pool", lambda g: g.memset(
                            bass.AP(xroll, xrs(0, 0),
                                    [[2 * 8 * 516, 128], [516, 8], [1, 1]]), 0.0),
                            tag=f"xg{l}")
                        st_x32 = P.dma("sq", [(
                            bass.AP(xroll, xrs(sl, 0) + 1,
                                    [[2 * 8 * 516, 128], [516, 8], [1, 512]]),
                            dram3(sin, 0))], deps=[stgx], tag=f"x32B{l}_{tb}")
                        stguard = P.step("pool", lambda g: g.memset(
                            bass.AP(zbuf, 0, [[8 * 513, 128], [513, 8], [1, 1]]),
                            0.0), tag=f"zg{l}")
                    else:
                        st_x32 = P.dma("sq", [(
                            bass.AP(xroll, xrs(sl, 0),
                                    [[2 * 8 * 516, 128], [516, 8], [1, 513]]),
                            dram3(sin, TB * tb - 1, 513))], tag=f"x32B{l}_{tb}")
                        stguard = P.step("pool", lambda g: g.tensor_copy(
                            bass.AP(zbuf, 0, [[8 * 513, 128], [513, 8], [1, 1]]),
                            bass.AP(zbuf, 512, [[8 * 513, 128], [513, 8], [1, 1]])),
                            deps=[guard_prev] + P.ln_zr, tag=f"zg{l}{tb}")
                    zst = []
                    for oc in range(8):
                        b = 4 + oc % 2

                        def mm_wo2(t, oc=oc, b=b):
                            last = None
                            for icc in range(4):
                                last = t.matmul(
                                    pb[b][:, 0:512],
                                    woB[:, 1024 * icc + 128 * oc:
                                        1024 * icc + 128 * (oc + 1)],
                                    o_fm[:, 512 * icc: 512 * (icc + 1)],
                                    start=(icc == 0), stop=(icc == 3))
                            return last
                        stm = P.step("pe", mm_wo2,
                                     deps=[st_wo, rbw, P.use_bank(b)],
                                     tag=f"B3wo{l}{tb}{oc}")

                        def ev(e, oc=oc, b=b, sl=sl):
                            return e.tensor_tensor(
                                zbuf[:, zc(oc): zc(oc) + 512],
                                pb[b][:, 0:512],
                                xroll[:, xrs(sl, oc) + 1: xrs(sl, oc) + 513],
                                ALU.add)
                        ste = P.step("dve", ev,
                                     deps=[stm, st_x32, stguard,
                                           x1_spills[tb - 1] if tb >= 1 else None]
                                     + P.ln_zr,
                                     tag=f"B3z{l}{tb}{oc}")
                        P.set_bank(b, ste)
                        zst.append(ste)
                    guard_prev = zst[-1]
                    x1_spills[tb] = P.dma("sq", [(
                        dram3(sout, TB * tb),
                        bass.AP(zbuf, 1, [[8 * 513, 128], [513, 8], [1, 512]]))],
                        deps=zst, tag=f"x1sp{l}{tb}")

                    def zsrc(oc):
                        off = zc(oc) if oc < 4 else zc(oc) - 1
                        return zbuf[:, off: off + 512]
                    s7, s1s = ln_stats(zst, zsrc)
                    P.ln_ab = []
                    P.ln_zr = list(s1s)
                    yapp = []
                    prev_yapp3 = None
                    for oc in range(8):
                        sa1 = P.step("pool", lambda g, oc=oc: g.tensor_sub(
                            etmp[:, :], zsrc(oc), Bb32[:, :]),
                            deps=[s7, prev_yapp3], tag="yapp1")
                        sa2 = P.step("pool", lambda g: g.tensor_mul(
                            etmp[:, :], etmp[:, :], Ab32[:, :]), deps=[sa1],
                            tag="yapp2")
                        sa3 = P.step("dve", lambda e, oc=oc, sl=sl: e.tensor_scalar(
                            ybuf[:, ybs(sl, oc): ybs(sl, oc) + 512],
                            etmp[:, :], params[(l, "lng")][:, oc:oc + 1],
                            params[(l, "lnb")][:, oc:oc + 1], ALU.mult, ALU.add),
                            deps=[sa2, y_spills[tb - 1] if tb >= 1 else None],
                            tag="yapp3")
                        prev_yapp3 = sa3
                        yapp.append(sa3)
                        P.ln_ab.append(sa1); P.ln_ab.append(sa2)
                        P.ln_zr.append(sa1)
                    y_spills[tb] = P.dma("sq", [(
                        bass.AP(yb, TB * tb, [[T, 128], [128 * T, 8], [1, 512]]),
                        bass.AP(ybuf, 0,
                                [[8 * 512, 128], [512, 8], [1, 512]]))],
                        deps=yapp, tag=f"ysp{l}{tb}")

                P.wait_only("pool", [x1_spills[NTB - 1], y_spills[NTB - 1]],
                            tag=f"drainB13{l}")
                P.flush()

            G2 = ExitStack()
            x2_spills = []
            with G2:
                gsb = lambda name, shape, dt: G2.enter_context(
                    nc.sbuf_tensor(name, shape, dt))
                w1B = gsb(f"w1B{l}", [128, 8 * 5460], BF16)
                w2g = gsb(f"w2g{l}", [128, 2 * 4 * 1024], BF16)
                actB = gsb(f"actB{l}", [128, NJ * 512], BF16)
                ybloc = gsb(f"ybloc{l}", [128, 8 * 512], BF16)
                gel32 = gsb(f"gel{l}", [128, 512], F32)

                SPLIT = 128 * 11
                w1parts = []
                for (c0, wdt) in [(0, SPLIT), (HID, SPLIT),
                                  (SPLIT, HID - SPLIT), (HID + SPLIT, HID - SPLIT)]:
                    w1parts.append(P.dma("gq", [(
                        bass.AP(w1B, c0, [[8 * 5460, 128], [5460, 8], [1, wdt]]),
                        bass.AP(E[f"B{l}_w1"], c0,
                                [[5460, 128], [128 * 5460, 8], [1, wdt]]))],
                        tag=f"w1_{l}_{c0}"))
                P.flush()

                for tb in range(NTB):
                    sl = tb % 2
                    stx1 = P.dma("sq", [(
                        bass.AP(xroll, xrs(sl, 0) + 1,
                                [[2 * 8 * 516, 128], [516, 8], [1, 512]]),
                        dram3(sout, TB * tb))],
                        deps=[x1_spills[tb]], tag=f"x1ld{l}{tb}")
                    sty = P.dma("sq", [(
                        bass.AP(ybloc, 0, [[8 * 512, 128], [512, 8], [1, 512]]),
                        bass.AP(yb, TB * tb, [[T, 128], [128 * T, 8], [1, 512]]))],
                        deps=[y_spills[tb]], tag=f"yld{l}{tb}")

                    def w2load(g, deps, tb=tb):
                        j0 = 4 * g
                        njg = 4 if g < 5 else 2
                        trs = [(
                            bass.AP(w2g, 1024 * 4 * (g % 2),
                                    [[2 * 4 * 1024, 128], [1024, njg], [1, 1024]]),
                            bass.AP(E[f"B{l}_w2"], 1024 * 128 * j0,
                                    [[1024, 128], [128 * 1024, njg], [1, 1024]]))]
                        return P.dma("gq", trs, deps=deps, tag=f"w2g{l}{tb}{g}")

                    w2ld = [None] * 6
                    w2ld[0] = w2load(0, [])
                    w2ld[1] = w2load(1, [])

                    prev_act = [None, None]
                    actst = [None] * NJ
                    for j in range(NJ):
                        wj = 128 if j < NJ - 1 else HID - 128 * (NJ - 1)
                        ba_, bg_ = (0, 1) if j % 2 == 0 else (2, 3)
                        acol = 128 * j
                        gcol = HID + 128 * j

                        def mma(t, wj=wj, ba_=ba_, acol=acol):
                            last = None
                            for ic in range(8):
                                last = t.matmul(
                                    pb[ba_][0:wj, 0:512],
                                    w1B[:, 5460 * ic + acol: 5460 * ic + acol + wj],
                                    ybloc[:, 512 * ic: 512 * (ic + 1)],
                                    start=(ic == 0), stop=(ic == 7))
                            return last
                        w1dep = w1parts[0] if j <= 10 else w1parts[2]
                        sa_ = P.step("pe", mma,
                                     deps=[sty, w1dep, P.use_bank(ba_)],
                                     tag=f"B4a{l}{tb}{j}")

                        def mmg(t, wj=wj, bg_=bg_, gcol=gcol):
                            last = None
                            for ic in range(8):
                                last = t.matmul(
                                    pb[bg_][0:wj, 0:512],
                                    w1B[:, 5460 * ic + gcol: 5460 * ic + gcol + wj],
                                    ybloc[:, 512 * ic: 512 * (ic + 1)],
                                    start=(ic == 0), stop=(ic == 7))
                            return last
                        w1depg = w1parts[1] if j <= 10 else w1parts[3]
                        sg_ = P.step("pe", mmg,
                                     deps=[sty, w1depg, P.use_bank(bg_)],
                                     tag=f"B4g{l}{tb}{j}")
                        sgel = P.step("act", lambda s, j=j, wj=wj, bg_=bg_:
                                      s.activation(
                                          gel32[0:wj, 0:512],
                                          pb[bg_][0:wj, 0:512], AF.Gelu,
                                          bias=params[(l, "b1g")][0:wj, j:j + 1]),
                                      deps=[sg_, actst[j - 1] if j >= 1 else None],
                                      tag=f"B4gel{l}{tb}{j}")
                        P.set_bank(bg_, sgel)
                        sact = P.step("dve", lambda e, j=j, wj=wj, ba_=ba_:
                                      e.scalar_tensor_tensor(
                                          actB[0:wj, 512 * j: 512 * j + 512],
                                          pb[ba_][0:wj, 0:512],
                                          params[(l, "b1a")][0:wj, j:j + 1],
                                          gel32[0:wj, 0:512],
                                          ALU.add, ALU.mult),
                                      deps=[sa_, sgel], tag=f"B4act{l}{tb}{j}")
                        P.set_bank(ba_, sact)
                        prev_act[j % 2] = sact
                        actst[j] = sact

                    mmsteps = []
                    for g in range(6):
                        j0 = 4 * g
                        njg = 4 if g < 5 else 2

                        def mmw2(t, g=g, j0=j0, njg=njg):
                            last = None
                            for jj in range(njg):
                                j = j0 + jj
                                wj = 128 if j < NJ - 1 else HID - 128 * (NJ - 1)
                                for oc in range(8):
                                    last = t.matmul(
                                        pb[oc][:, 0:512],
                                        w2g[0:wj,
                                            1024 * (4 * (g % 2) + jj) + 128 * oc:
                                            1024 * (4 * (g % 2) + jj) +
                                            128 * (oc + 1)],
                                        actB[0:wj, 512 * j: 512 * j + 512],
                                        start=(j == 0), stop=(j == NJ - 1))
                            return last
                        depsl = [w2ld[g], actst[min(j0 + njg, NJ) - 1]]
                        if g == 0:
                            depsl += [P.use_bank(bb) for bb in range(8)]
                        sm = P.step("pe", mmw2, deps=depsl, tag=f"B4w2{l}{tb}{g}")
                        mmsteps.append(sm)
                        if g + 2 < 6:
                            w2ld[g + 2] = w2load(g + 2, [sm])
                    zst2 = []
                    for oc in range(8):
                        def ev2(e, oc=oc, sl=sl):
                            return e.scalar_tensor_tensor(
                                zbuf[:, zc(oc): zc(oc) + 512],
                                pb[oc][:, 0:512],
                                params[(l, "b2")][:, oc:oc + 1],
                                xroll[:, xrs(sl, oc) + 1: xrs(sl, oc) + 513],
                                ALU.add, ALU.add)
                        s_ = P.step("dve", ev2,
                                    deps=[mmsteps[-1], stx1,
                                          x2_spills[-1] if x2_spills else None],
                                    tag=f"B4z{l}{tb}{oc}")
                        P.set_bank(oc, s_)
                        zst2.append(s_)
                    trs = [(dram3(sout, TB * tb),
                            bass.AP(zbuf, 1, [[8 * 513, 128], [513, 8], [1, 512]]))]
                    if want_out:
                        trs.append((dram3(out, TB * tb),
                                    bass.AP(zbuf, 1,
                                            [[8 * 513, 128], [513, 8], [1, 512]])))
                    x2_spills.append(P.dma("sq", trs, deps=zst2,
                                           tag=f"x2sp{l}{tb}"))
                    P.flush()
                P.wait_only("pool", [x2_spills[-1]], tag=f"drainB4{l}")
                P.flush()
            return x2_spills[-1]

        last0 = emit_layer(0, xa, xb, want_out=(upto == "L0"))
        if upto == "L0":
            with nc.Block() as block:
                @block.sync
                def _(s):
                    s.wait_ge(P.sem["sq"], P.cnt["sq"])
                    s.wait_ge(P.sem["gq"], P.cnt["gq"])
            return nc
        last1 = emit_layer(1, xb, xc, want_out=False)

        outsp = []
        for tb in range(NTB):
            st0 = P.dma("sq", [(
                bass.AP(xroll, xrs(0, 0) + 1,
                        [[2 * 8 * 516, 128], [516, 8], [1, 512]]),
                dram3(xa, TB * tb))], deps=[last1], tag=f"fl0{tb}")
            st1 = P.dma("sq", [(
                bass.AP(xroll, xrs(1, 0) + 1,
                        [[2 * 8 * 516, 128], [516, 8], [1, 512]]),
                dram3(xc, TB * tb))], deps=[last1], tag=f"fl1{tb}")
            zst = []
            for oc in range(8):
                s_ = P.step("dve", lambda e, oc=oc: e.tensor_tensor(
                    zbuf[:, zc(oc): zc(oc) + 512],
                    xroll[:, xrs(0, oc) + 1: xrs(0, oc) + 513],
                    xroll[:, xrs(1, oc) + 1: xrs(1, oc) + 513], ALU.add),
                    deps=[st0, st1, outsp[-1] if outsp else None] + P.ln_zr,
                    tag=f"fz{tb}{oc}")
                zst.append(s_)
            s7, s1s = ln_stats(zst, lambda oc: zbuf[:, zc(oc): zc(oc) + 512])
            appst = []
            for oc in range(8):
                src = zbuf[:, zc(oc): zc(oc) + 512]

                def app(g, src=src):
                    g.tensor_sub(src, src, Bb32[:, :])
                    return g.tensor_mul(src, src, Ab32[:, :])
                appst.append(P.step("pool", app, deps=[s7], tag="lnappF"))
            P.ln_ab = list(appst)
            P.ln_zr = s1s + appst
            outsp.append(P.dma("sq", [(
                dram3(out, TB * tb),
                bass.AP(zbuf, 1, [[8 * 513, 128], [513, 8], [1, 512]]))],
                deps=appst, tag=f"outF{tb}"))
            P.flush()

        with nc.Block() as block:
            @block.sync
            def _(s):
                s.wait_ge(P.sem["sq"], P.cnt["sq"])
                s.wait_ge(P.sem["gq"], P.cnt["gq"])
    return nc


def kernel(**inputs):
    consts = host_consts()
    shared = host_layout(inputs)
    shared.update(consts)
    nc = build_nc("full")
    x = np.asarray(inputs["L0_attn_x"])
    in_maps = []
    for b in range(8):
        m = dict(shared)
        m["xT"] = np.ascontiguousarray(x[b].T)
        in_maps.append(m)
    res = run_bass_kernel_spmd(nc, in_maps, core_ids=list(range(8)))
    outs = [np.ascontiguousarray(r["yT"].T) for r in res.results]
    return np.stack(outs, axis=0)


# version 06
# speedup vs baseline: 1.1122x; 1.1122x over previous
import numpy as np
import ml_dtypes
from contextlib import ExitStack
from types import SimpleNamespace

import concourse.bass as bass
import concourse.mybir as mybir
from concourse.bass_utils import run_bass_kernel_spmd

F32 = mybir.dt.float32
BF16 = mybir.dt.bfloat16
AF = mybir.ActivationFunctionType
ALU = mybir.AluOpType
AX = mybir.AxisListType
BF = ml_dtypes.bfloat16

T = 2048
D = 1024
NTB = 4
TB = 512
HID = 2730
NJ = 22
NEG = -1.0e30
C_EIGHTH, C_EPS, C_TINY, C_INV1024, C_TWO, C_LN8 = 0, 1, 2, 3, 4, 5


def host_consts():
    c = {}
    c["identin"] = np.eye(128, dtype=np.float32)
    sel8 = np.zeros((8, 1024), np.float32)
    for h in range(8):
        sel8[h, 128 * h:128 * (h + 1)] = 1.0
    c["sel8in"] = sel8
    i2t = np.zeros((2, 128), np.float32)
    i2t[0, 0:64] = 1.0
    i2t[1, 64:128] = 1.0
    c["ind2tin"] = i2t
    r = np.arange(128)
    c["maskLEin"] = np.where(r[None, :] <= r[:, None], 0.0, NEG).astype(np.float32)
    c["maskGEin"] = np.where(r[None, :] >= r[:, None], 0.0, NEG).astype(np.float32)
    half = np.arange(0, 64, 2, dtype=np.float64)
    inv_freq = 1.0 / 10000.0 ** (half / 64)
    t = np.arange(T, dtype=np.float64)
    freqs = t[:, None] * inv_freq[None, :]
    base = (half + 0.4 * 64) / (1.4 * 64)
    power = (t - T // 2) / 256.0
    sc = base[None, :] ** power[:, None]
    cos, sin = np.cos(freqs), np.sin(freqs)
    sgn = np.concatenate([-np.ones(32), np.ones(32)])[:, None]

    def fm(tab):
        d64 = np.concatenate([tab.T, tab.T], axis=0)
        return np.concatenate([d64, d64], axis=0).astype(np.float32)

    def fmsgn(tab):
        d64 = np.concatenate([tab.T, tab.T], axis=0) * sgn
        return np.concatenate([d64, d64], axis=0).astype(np.float32)

    c["Tcq"] = fm(cos * sc)
    c["Tsq"] = fmsgn(sin * sc)
    c["Tck"] = fm(cos / sc)
    c["Tsk"] = fmsgn(sin / sc)
    return c


def host_layout(ins):
    d = {}
    d["A_wqT"] = np.ascontiguousarray(np.asarray(ins["L0_attn_Wq"]).T)
    d["A_wkT"] = np.ascontiguousarray(np.asarray(ins["L0_attn_Wk"]).T)
    d["A_wvT"] = np.ascontiguousarray(np.asarray(ins["L0_attn_Wv"]).T)
    d["A_woT"] = np.ascontiguousarray(np.asarray(ins["L0_attn_Wo"]).T)
    d["A_bq"] = np.ascontiguousarray(np.asarray(ins["L0_attn_bq"]).reshape(8, 128).T)
    d["A_bk"] = np.ascontiguousarray(np.asarray(ins["L0_attn_bk"]).reshape(8, 128).T)
    d["A_bvrow"] = np.asarray(ins["L0_attn_bv"]).reshape(1, 1024).copy()
    d["A_borow"] = np.asarray(ins["L0_attn_bo"]).reshape(1, 1024).copy()
    perm = np.concatenate([np.arange(32, 64), np.arange(32)])
    permfull = np.concatenate([64 * h + perm for h in range(8)])
    for l in range(2):
        w = np.asarray(ins["L0_ffn_wqkv"][l])
        d[f"B{l}_wqkv"] = np.ascontiguousarray(w)
        d[f"B{l}_wsh"] = np.ascontiguousarray(
            np.concatenate([w[:, permfull], w[:, 512 + permfull]], axis=1))
        d[f"B{l}_wo"] = np.ascontiguousarray(np.asarray(ins["L0_ffn_wo"][l]))
        d[f"B{l}_w1"] = np.ascontiguousarray(np.asarray(ins["L0_ffn_w1"][l]))
        w2p = np.zeros((NJ * 128, 1024), np.float32)
        w2p[:HID] = np.asarray(ins["L0_ffn_w2"][l])
        d[f"B{l}_w2"] = w2p
        qs = np.asarray(ins["L0_ffn_q_scale"][l]).astype(np.float32)
        ks = np.asarray(ins["L0_ffn_k_scale"][l]).astype(np.float32)
        d[f"B{l}_qs"] = np.tile(qs, 2).reshape(128, 1).copy()
        d[f"B{l}_qssh"] = np.tile(qs[perm], 2).reshape(128, 1).copy()
        d[f"B{l}_ks"] = np.tile(ks, 2).reshape(128, 1).copy()
        d[f"B{l}_kssh"] = np.tile(ks[perm], 2).reshape(128, 1).copy()
        d[f"B{l}_lng"] = np.ascontiguousarray(
            np.asarray(ins["L0_ffn_ln_g"][l]).reshape(8, 128).T)
        d[f"B{l}_lnb"] = np.ascontiguousarray(
            np.asarray(ins["L0_ffn_ln_b"][l]).reshape(8, 128).T)
        b1 = np.asarray(ins["L0_ffn_b1"][l])
        b1a = np.zeros(NJ * 128, np.float32); b1a[:HID] = b1[:HID]
        b1g = np.zeros(NJ * 128, np.float32); b1g[:HID] = b1[HID:]
        d[f"B{l}_b1a"] = np.ascontiguousarray(b1a.reshape(NJ, 128).T)
        d[f"B{l}_b1g"] = np.ascontiguousarray(b1g.reshape(NJ, 128).T)
        d[f"B{l}_b2"] = np.ascontiguousarray(
            np.asarray(ins["L0_ffn_b2"][l]).reshape(8, 128).T)
    return d


SEC = {"pe": "tensor", "act": "scalar", "dve": "vector",
       "pool": "gpsimd", "gq": "gpsimd", "sq": "sync"}


class Prog:
    def __init__(self, nc, ctx):
        self.nc = nc
        self.sem = {k: ctx.enter_context(nc.semaphore(f"s_{k}")) for k in SEC}
        self.cnt = {k: 0 for k in SEC}
        self.steps = []
        self.wm = {s: {} for s in set(SEC.values())}
        self.bank_state = [[] for _ in range(8)]

    def step(self, key, emit, deps=(), n=1, tag=""):
        flat = []
        for d in deps:
            if d is None:
                continue
            if isinstance(d, list):
                flat.extend(x for x in d if x is not None)
            else:
                flat.append(d)
        s = SimpleNamespace(key=key, emit=emit, tag=tag, n=n, deps=flat,
                            section=SEC[key])
        self.cnt[key] += n
        s.val = self.cnt[key]
        self.steps.append(s)
        return s

    def dma(self, key, transfers, deps=(), tag=""):
        assert key in ("sq", "gq")
        sem = self.sem[key]
        tr = list(transfers)

        def emit(eng):
            last = None
            for (o, i) in tr:
                last = eng.dma_start(o, i).then_inc(sem, 16)
            return last
        return self.step(key, emit, deps, n=16 * len(tr), tag=tag)

    def wait_only2(self, key, deps, tag=""):
        return self.wait_only(key, deps, tag)

    def wait_only(self, key, deps, tag=""):
        return self.step(key, (lambda eng: None), deps, n=0, tag=tag)

    def flush(self):
        steps, self.steps = self.steps, []
        if not steps:
            return
        secs = {}
        for s in steps:
            secs.setdefault(s.section, []).append(s)
        with self.nc.Block() as block:
            for secname, slist in secs.items():
                def make(slist=slist, secname=secname):
                    def body(eng):
                        for s in slist:
                            for d in s.deps:
                                if d.key == s.key and d.key not in ("sq", "gq"):
                                    continue
                                wm = self.wm[secname]
                                if wm.get(d.key, 0) >= d.val:
                                    continue
                                eng.wait_ge(self.sem[d.key], d.val)
                                wm[d.key] = d.val
                            last = s.emit(eng)
                            if s.key not in ("sq", "gq") and last is not None:
                                last.then_inc(self.sem[s.key], 1)
                    return body
                getattr(block, secname)(make())

    def use_bank(self, b):
        return list(self.bank_state[b] or [])

    def set_bank(self, b, step):
        self.bank_state[b] = step if isinstance(step, list) else [step]

    def add_bank_reader(self, b, step):
        if self.bank_state[b] is None:
            self.bank_state[b] = []
        self.bank_state[b].append(step)


def build_nc(upto="full"):
    nc = bass.Bass("TRN2", target_bir_lowering=False)
    E = {}

    def din(name, shape, dt=F32):
        E[name] = nc.dram_tensor(name, shape, dt, kind="ExternalInput")
        return E[name]

    din("xT", [D, T])
    din("identin", [128, 128]); din("sel8in", [8, 1024]); din("ind2tin", [2, 128])
    din("maskLEin", [128, 128]); din("maskGEin", [128, 128])
    for n in ["Tcq", "Tsq", "Tck", "Tsk"]:
        din(n, [128, T])
    din("A_wqT", [D, D]); din("A_wkT", [D, D]); din("A_wvT", [D, D]); din("A_woT", [D, D])
    din("A_bq", [128, 8]); din("A_bk", [128, 8])
    din("A_bvrow", [1, D]); din("A_borow", [1, D])
    for l in range(2):
        din(f"B{l}_wqkv", [D, 1536]); din(f"B{l}_wsh", [D, D])
        din(f"B{l}_wo", [512, D]); din(f"B{l}_w1", [D, 5460])
        din(f"B{l}_w2", [NJ * 128, D])
        for n in ["qs", "qssh", "ks", "kssh"]:
            din(f"B{l}_{n}", [128, 1])
        din(f"B{l}_lng", [128, 8]); din(f"B{l}_lnb", [128, 8])
        din(f"B{l}_b1a", [128, NJ]); din(f"B{l}_b1g", [128, NJ]); din(f"B{l}_b2", [128, 8])
    out = nc.dram_tensor("yT", [D, T], F32, kind="ExternalOutput")
    xa = nc.dram_tensor("xa_scr", [D, T], F32)
    xb = nc.dram_tensor("xb_scr", [D, T], F32)
    xc = nc.dram_tensor("xc_scr", [D, T], F32)
    yb = nc.dram_tensor("yb_scr", [D, T], BF16)

    def dram3(dten, col0, width=512):
        return bass.AP(dten, col0, [[T, 128], [128 * T, 8], [1, width]])

    ctx = ExitStack()
    with ctx:
        P = Prog(nc, ctx)
        sb = lambda name, shape, dt: ctx.enter_context(nc.sbuf_tensor(name, shape, dt))
        pb = [ctx.enter_context(nc.psum_tensor(f"pb{i}", [128, 512], F32))
              for i in range(8)]

        cst = sb("cst", [128, 8], F32)
        ident = sb("ident", [128, 128], F32)
        ident_bf = sb("ident_bf", [128, 128], BF16)
        sel8 = sb("sel8", [8, 1024], BF16)
        maskLE = sb("maskLE", [128, 128], BF16)
        maskGE = sb("maskGE", [128, 128], BF16)
        ones_row = sb("ones_row", [1, 512], BF16)
        ones_col = sb("ones_col", [128, 1], BF16)
        ones_row32 = sb("ones_row32", [1, 128], F32)
        ind2 = sb("ind2", [128, 2], BF16)
        ind2t = sb("ind2t", [2, 128], BF16)
        xroll = sb("xroll", [128, 2 * 8 * 516], F32)
        xrs = lambda s, ic: 516 * (8 * s + ic)
        negmax8 = sb("negmax8", [128, 8], F32)
        negtmp = sb("negtmp", [128, 2], F32)
        nm8 = sb("nm8", [8, 128], BF16)
        mrow = sb("mrow", [1, 512], F32)
        vrow = sb("vrow", [1, 512], F32)
        rstdr = sb("rstdr", [1, 512], F32)
        rrow = sb("rrow", [1, 256], F32)
        sspair = sb("sspair", [2, 512], F32)
        fpair = sb("fpair", [2, 512], BF16)
        Ab32 = sb("Ab32", [128, 512], F32)
        Bb32 = sb("Bb32", [128, 512], F32)
        rb32 = sb("rb32", [128, 256], F32)
        bqs = sb("bqs", [128, 8], F32)
        bkA = sb("bkA", [128, 8], F32)
        zbuf = sb("zbuf", [128, 8 * 513], F32)
        zc = lambda oc: 513 * oc + 1
        zsqs = sb("zsqs", [128, 2 * 512], BF16)
        zbfs = sb("zbfs", [128, 2 * 512], BF16)
        params = {}
        for l in range(2):
            for n in ["qs", "qssh", "ks", "kssh"]:
                params[(l, n)] = sb(f"p{l}{n}", [128, 1], F32)
            for n in ["lng", "lnb", "b2"]:
                params[(l, n)] = sb(f"p{l}{n}", [128, 8], F32)
            for n in ["b1a", "b1g"]:
                params[(l, n)] = sb(f"p{l}{n}", [128, NJ], F32)

        def init_pool(g):
            g.memset(ones_row[:, :], 1.0)
            g.memset(ones_col[:, :], 1.0)
            g.memset(ones_row32[:, :], 1.0)
            g.memset(ind2[:, :], 0.0)
            g.memset(ind2[0:64, 0:1], 1.0)
            g.memset(ind2[64:128, 1:2], 1.0)
            g.memset(cst[:, C_EIGHTH:C_EIGHTH + 1], 0.125)
            g.memset(cst[:, C_EPS:C_EPS + 1], 1e-5)
            g.memset(cst[:, C_TINY:C_TINY + 1], 1e-24)
            g.memset(cst[:, C_INV1024:C_INV1024 + 1], 1.0 / 1024.0)
            g.memset(cst[:, C_TWO:C_TWO + 1], 2.0)
            return g.memset(cst[:, C_LN8:C_LN8 + 1], 2.0794415416798357)
        st_ms = P.step("pool", init_pool, tag="init_memset")
        st_cst = P.dma("gq", [
            (ident_bf[:, :], E["identin"].ap()),
            (sel8[:, :], E["sel8in"].ap()),
            (ind2t[:, :], E["ind2tin"].ap()),
            (maskLE[:, :], E["maskLEin"].ap()),
            (maskGE[:, :], E["maskGEin"].ap()),
        ], tag="cast_consts")
        st_cf = P.dma("sq", [
            (ident[:, :], E["identin"].ap()),
            (bkA[:, :], E["A_bk"].ap()),
        ] + [(params[(l, n)][:, :], E[f"B{l}_{n}"].ap())
             for l in range(2)
             for n in ["qs", "qssh", "ks", "kssh", "lng", "lnb", "b2", "b1a", "b1g"]],
            tag="f32_consts")
        st_bq = P.dma("sq", [(bqs[:, :], E["A_bq"].ap())], tag="bq_load")
        st_bqs = P.step("dve", lambda e: e.tensor_scalar(
            bqs[:, :], bqs[:, :], cst[:, C_EIGHTH:C_EIGHTH + 1], None, ALU.mult),
            deps=[st_bq, st_ms], tag="bq_scale")
        P.wait_only("pool", [st_cst, st_cf, st_bq], tag="init_bar")
        P.flush()

        P.ln_ab = []
        P.ln_zr = []

        def ln_stats(dep8, zsrc):
            s1s = []
            csteps = []
            for oc in range(8):
                src = zsrc(oc)
                war = [csteps[oc - 2]] if oc >= 2 else [P.use_bank(6), P.use_bank(7)]
                s1 = P.step("act", lambda s, src=src, oc=oc: s.activation(
                    zsqs[:, 512 * (oc % 2): 512 * (oc % 2 + 1)], src, AF.Square),
                    deps=[dep8[oc]] + war, tag="lnsq")
                s1s.append(s1)
                s2 = P.step("dve", lambda e, src=src, oc=oc: e.tensor_copy(
                    zbfs[:, 512 * (oc % 2): 512 * (oc % 2 + 1)], src),
                    deps=[dep8[oc]] + war, tag="lnbf")

                def mmst(t, oc=oc):
                    t.matmul(pb[6][0:1, 0:512], ones_col[:, :],
                             zbfs[:, 512 * (oc % 2): 512 * (oc % 2 + 1)],
                             start=(oc == 0), stop=(oc == 7))
                    return t.matmul(pb[7][0:1, 0:512], ones_col[:, :],
                                    zsqs[:, 512 * (oc % 2): 512 * (oc % 2 + 1)],
                                    start=(oc == 0), stop=(oc == 7))
                csteps.append(P.step("pe", mmst, deps=[s1, s2], tag="lnmm"))

            def rows(e):
                e.tensor_scalar(mrow[:, :], pb[6][0:1, 0:512],
                                cst[0:1, C_INV1024:C_INV1024 + 1], None, ALU.mult)
                e.tensor_scalar(vrow[:, :], pb[7][0:1, 0:512],
                                cst[0:1, C_INV1024:C_INV1024 + 1], None, ALU.mult)
                e.tensor_mul(rstdr[:, :], mrow[:, :], mrow[:, :])
                return e.tensor_sub(vrow[:, :], vrow[:, :], rstdr[:, :])
            s4 = P.step("dve", rows, deps=[csteps[-1]], tag="lnrows")
            P.set_bank(6, s4)
            P.set_bank(7, s4)

            def rstd(s):
                s.activation(rstdr[:, :], vrow[:, :], AF.Ln,
                             bias=cst[0:1, C_EPS:C_EPS + 1])
                return s.activation(rstdr[:, :], rstdr[:, :], AF.Exp, scale=-0.5)
            s5 = P.step("act", rstd, deps=[s4], tag="lnrstd")

            def bc(t):
                t.matmul(pb[6][:, 0:512], ones_row32[:, :], rstdr[:, :],
                         start=True, stop=True)
                return t.matmul(pb[7][:, 0:512], ones_row32[:, :], mrow[:, :],
                                start=True, stop=True)
            s6 = P.step("pe", bc, deps=[s5, P.use_bank(6), P.use_bank(7)], tag="lnbc")

            def bce(s):
                s.copy(Ab32[:, :], pb[6][:, 0:512])
                return s.copy(Bb32[:, :], pb[7][:, 0:512])
            s7 = P.step("act", bce, deps=[s6] + P.ln_ab, tag="lnbce")
            P.set_bank(6, s7)
            P.set_bank(7, s7)
            return s7, s1s

        GA = ExitStack()
        with GA:
            asb = lambda name, shape, dt: GA.enter_context(
                nc.sbuf_tensor(name, shape, dt))
            wA = {n: asb(f"wA_{n}", [128, 8 * 1024], BF16) for n in "qkvo"}
            xbfA = asb("xbfA", [128, 2 * 8 * 516], BF16)
            xbs = lambda s, ic: 516 * (8 * s + ic)
            qA = asb("qA", [128, 8 * 512], BF16)
            kA = asb("kA", [128, 8 * 512], BF16)
            vA = asb("vA", [128, 4 * 1024], BF16)
            o_fmA = asb("o_fmA", [128, 8 * 512], BF16)
            PTA = asb("PTA", [128, 512], BF16)
            nm2 = asb("nm2", [2, 128], BF16)
            bvrow = asb("bvrow", [1, D], BF16)
            borow = asb("borow", [1, D], BF16)

            wsteps = {}
            for n, src in [("q", "A_wqT"), ("k", "A_wkT"), ("v", "A_wvT"),
                           ("o", "A_woT")]:
                sap = bass.AP(E[src], 0, [[1024, 128], [128 * 1024, 8], [1, 1024]])
                dap = wA[n].ap().rearrange("p (ic n) -> p ic n", ic=8)
                wsteps[n] = P.dma("gq", [(dap, sap)], tag=f"wA_{n}")
            st_brows = P.dma("gq", [(bvrow[:, :], E["A_bvrow"].ap()),
                                    (borow[:, :], E["A_borow"].ap())], tag="brows")
            P.wait_only("pool", [st_brows], tag="wA_bar")
            P.flush()

            prev_spill = None
            a3_last = [None] * NTB
            pre_xbf = {}
            pre_x32 = {}

            def mk_xbfA(tb, deps=()):
                sl = tb % 2
                if tb == 0:
                    st_g = P.step("pool", lambda g: g.memset(
                        bass.AP(xbfA, xbs(0, 0),
                                [[2 * 8 * 516, 128], [516, 8], [1, 1]]), 0.0),
                        tag="xbf_guard")
                    return P.dma("gq", [(
                        bass.AP(xbfA, xbs(sl, 0) + 1,
                                [[2 * 8 * 516, 128], [516, 8], [1, 512]]),
                        bass.AP(E["xT"], 0, [[T, 128], [128 * T, 8], [1, 512]]))],
                        deps=[st_g], tag="xbfA0")
                return P.dma("gq", [(
                    bass.AP(xbfA, xbs(sl, 0),
                            [[2 * 8 * 516, 128], [516, 8], [1, 513]]),
                    bass.AP(E["xT"], TB * tb - 1,
                            [[T, 128], [128 * T, 8], [1, 513]]))],
                    deps=deps, tag=f"xbfA{tb}")

            def mk_x32A(tb, deps=()):
                sl = tb % 2
                if tb == 0:
                    st_gx = P.step("pool", lambda g: g.memset(
                        bass.AP(xroll, xrs(0, 0),
                                [[2 * 8 * 516, 128], [516, 8], [1, 1]]), 0.0),
                        tag="xroll_guard")
                    return P.dma("sq", [(
                        bass.AP(xroll, xrs(sl, 0) + 1,
                                [[2 * 8 * 516, 128], [516, 8], [1, 512]]),
                        bass.AP(E["xT"], 0, [[T, 128], [128 * T, 8], [1, 512]]))],
                        deps=[st_gx] + list(deps), tag="x32_0")
                return P.dma("sq", [(
                    bass.AP(xroll, xrs(sl, 0),
                            [[2 * 8 * 516, 128], [516, 8], [1, 513]]),
                    bass.AP(E["xT"], TB * tb - 1,
                            [[T, 128], [128 * T, 8], [1, 513]]))],
                    deps=deps, tag=f"x32_{tb}")

            for tb in range(NTB):
                sl = tb % 2
                st_xbf = pre_xbf.pop(tb) if tb in pre_xbf else mk_xbfA(tb)
                st_x32 = pre_x32.pop(tb) if tb in pre_x32 else mk_x32A(tb)
                if tb + 1 < NTB:
                    pre_xbf[tb + 1] = mk_xbfA(tb + 1)
                    pre_x32[tb + 1] = mk_x32A(
                        tb + 1, deps=[a3_last[tb - 1]] if tb >= 1 else [])

                def xb_data(ic, sl=sl):
                    return xbfA[:, xbs(sl, ic) + 1: xbs(sl, ic) + 513]

                last_ev = None
                for which, wt, dst, brow, scale, wst in [
                        ("q", wA["q"], qA, bqs, 0.125, wsteps["q"]),
                        ("k", wA["k"], kA, bkA, 1.0, wsteps["k"])]:
                    for oc in range(8):
                        b = oc % 4

                        def mm_qk(t, wt=wt, oc=oc, b=b):
                            last = None
                            for ic in range(8):
                                last = t.matmul(
                                    pb[b][:, 0:512],
                                    wt[:, 1024 * ic + 128 * oc:
                                       1024 * ic + 128 * (oc + 1)],
                                    xb_data(ic), start=(ic == 0), stop=(ic == 7))
                            return last
                        stm = P.step("pe", mm_qk,
                                     deps=[st_xbf, wst, P.use_bank(b)],
                                     tag=f"A1{which}{oc}")

                        def ev(s, dst=dst, oc=oc, b=b, brow=brow, scale=scale):
                            return s.activation(
                                dst[:, 512 * oc: 512 * (oc + 1)],
                                pb[b][:, 0:512], AF.Identity,
                                bias=brow[:, oc:oc + 1], scale=scale)
                        ste = P.step("act", ev, deps=[stm, st_cf, st_bqs],
                                     tag=f"A1{which}e{oc}")
                        P.set_bank(b, ste)
                        last_ev = ste
                for tc in range(4):
                    for half in range(2):
                        b = 4 + (2 * tc + half) % 4

                        def mm_v(t, tc=tc, half=half, b=b, sl=sl):
                            for ic in range(8):
                                t.matmul(
                                    pb[b][:, 0:512],
                                    xbfA[:, xbs(sl, ic) + 1 + 128 * tc:
                                         xbs(sl, ic) + 1 + 128 * (tc + 1)],
                                    wA["v"][:, 1024 * ic + 512 * half:
                                            1024 * ic + 512 * (half + 1)],
                                    start=(ic == 0), stop=False)
                            return t.matmul(
                                pb[b][:, 0:512], ones_row[0:1, 0:128],
                                bvrow[0:1, 512 * half: 512 * (half + 1)],
                                start=False, stop=True)
                        stm = P.step("pe", mm_v,
                                     deps=[st_xbf, wsteps["v"], st_brows, st_ms,
                                           P.use_bank(b)], tag=f"A1v{tc}{half}")

                        def ev(s, tc=tc, half=half, b=b):
                            return s.copy(vA[:, 1024 * tc + 512 * half:
                                             1024 * tc + 512 * (half + 1)],
                                          pb[b][:, 0:512])
                        ste = P.step("act", ev, deps=[stm], tag=f"A1ve{tc}{half}")
                        P.set_bank(b, ste)
                        last_ev = ste

                pta_reader = None
                rb_reader = None
                for wi in range(2):
                    nmsteps = []
                    for j in range(2):
                        qoff = 256 * wi + 128 * j
                        bS = 4

                        def mm_s(t, wi=wi, j=j, qoff=qoff, bS=bS):
                            span = 128 * (j + 1)
                            for ic in range(8):
                                t.matmul(
                                    pb[bS][:, 0:span],
                                    qA[:, 512 * ic + qoff: 512 * ic + qoff + 128],
                                    kA[:, 512 * ic + 256 * wi:
                                       512 * ic + 256 * wi + span],
                                    start=(ic == 0), stop=False)
                            return t.matmul(
                                pb[bS][:, 128 * j: 128 * (j + 1)],
                                ident_bf[:, :], maskLE[:, :],
                                start=False, stop=True)
                        stm = P.step("pe", mm_s,
                                     deps=[last_ev, st_cst, P.use_bank(bS)],
                                     tag=f"A2s{wi}{j}")

                        def red(e, j=j, wi=wi, bS=bS):
                            return e.reduce_max(
                                negmax8[:, 2 * wi + j: 2 * wi + j + 1],
                                pb[bS][:, 0:128 * (j + 1)], axis=AX.X, negate=True)
                        str_ = P.step("dve", red, deps=[stm], tag=f"A2r{wi}{j}")
                        P.set_bank(bS, str_)
                        nmsteps.append(str_)

                    def tp(t, wi=wi):
                        return t.transpose(
                            bass.AP(pb[5], 0, [[512, 2], [1, 128]]),
                            negmax8[:, 2 * wi: 2 * wi + 2], ident[:, :])
                    sttp = P.step("pe", tp, deps=nmsteps + [st_cf, P.use_bank(5)],
                                  tag=f"A2tp{wi}")
                    stnm = P.step("act", lambda s: s.copy(
                        nm2[:, :], bass.AP(pb[5], 0, [[512, 2], [1, 128]])),
                        deps=[sttp], tag=f"A2nm{wi}")
                    P.set_bank(5, stnm)

                    for j in range(2):
                        qoff = 256 * wi + 128 * j
                        bT = 6

                        def mm_st(t, wi=wi, j=j, qoff=qoff, bT=bT):
                            for kc in range(j + 1):
                                for ic in range(8):
                                    t.matmul(
                                        pb[bT][:, 128 * kc: 128 * (kc + 1)],
                                        kA[:, 512 * ic + 256 * wi + 128 * kc:
                                           512 * ic + 256 * wi + 128 * (kc + 1)],
                                        qA[:, 512 * ic + qoff: 512 * ic + qoff + 128],
                                        start=(ic == 0 and kc == 0), stop=False)
                            t.matmul(
                                pb[bT][:, 128 * j: 128 * (j + 1)],
                                ident_bf[:, :], maskGE[:, :],
                                start=False, stop=False)
                            nrep = bass.AP(nm2, 0, [[128, 2], [0, j + 1], [1, 128]])
                            return t.matmul(
                                pb[bT][:, 0:128 * (j + 1)],
                                sel8[0:2, 128 * j: 128 * (j + 1)],
                                nrep, start=False, stop=True)
                        stm = P.step("pe", mm_st, deps=[stnm, P.use_bank(bT)],
                                     tag=f"A2t{wi}{j}")
                        stexp = P.step("act", lambda s, j=j, bT=bT: s.activation(
                            PTA[:, 0:128 * (j + 1)], pb[bT][:, 0:128 * (j + 1)],
                            AF.Exp), deps=[stm, pta_reader], tag=f"A2e{wi}{j}")
                        P.set_bank(bT, stexp)
                        bU = 7

                        def mm_u(t, j=j, bU=bU):
                            last = None
                            for kc in range(j + 1):
                                last = t.matmul(
                                    pb[bU][0:1, 0:128], ones_col[:, :],
                                    PTA[:, 128 * kc: 128 * (kc + 1)],
                                    start=(kc == 0), stop=(kc == j))
                            return last
                        stu = P.step("pe", mm_u, deps=[stexp, P.use_bank(bU)],
                                     tag=f"A2u{wi}{j}")
                        strc = P.step("dve", lambda e, bU=bU: e.reciprocal(
                            rrow[0:1, 0:128], pb[bU][0:1, 0:128]), deps=[stu],
                            tag=f"A2rc{wi}{j}")
                        P.set_bank(bU, strc)
                        bR = 5
                        strb = P.step("pe", lambda t, bR=bR: t.matmul(
                            pb[bR][:, 0:128], ones_row32[:, :], rrow[0:1, 0:128],
                            start=True, stop=True),
                            deps=[strc, P.use_bank(bR)], tag=f"A2rb{wi}{j}")
                        strbe = P.step("act", lambda s, bR=bR: s.copy(
                            rb32[:, 0:128], pb[bR][:, 0:128]), deps=[strb, rb_reader],
                            tag=f"A2rbe{wi}{j}")
                        P.set_bank(bR, strbe)

                        def mm_pv(t, wi=wi, j=j):
                            last = None
                            for oc in range(8):
                                bV = oc // 4
                                for kc in range(j + 1):
                                    tc = 2 * wi + kc
                                    last = t.matmul(
                                        pb[bV][:, 128 * (oc % 4):
                                               128 * (oc % 4 + 1)],
                                        vA[:, 1024 * tc + 128 * oc:
                                           1024 * tc + 128 * (oc + 1)],
                                        PTA[:, 128 * kc: 128 * (kc + 1)],
                                        start=(kc == 0), stop=(kc == j))
                            return last
                        stpv = P.step("pe", mm_pv,
                                      deps=[stexp, P.use_bank(0), P.use_bank(1)],
                                      tag=f"A2pv{wi}{j}")

                        def ev_o(e, wi=wi, j=j, qoff=qoff):
                            last = None
                            for oc in range(8):
                                bV = oc // 4
                                last = e.tensor_tensor(
                                    o_fmA[:, 512 * oc + qoff:
                                          512 * oc + qoff + 128],
                                    pb[bV][:, 128 * (oc % 4): 128 * (oc % 4 + 1)],
                                    rb32[:, 0:128], ALU.mult)
                            return last
                        stoe = P.step("dve", ev_o, deps=[stpv, strbe],
                                      tag=f"A2oe{wi}{j}")
                        P.set_bank(0, stoe)
                        P.set_bank(1, stoe)
                        pta_reader = stpv
                        rb_reader = stoe
                    P.flush()

                zst = []
                for oc in range(8):
                    b = 4 + oc % 2

                    def mm_wo(t, oc=oc, b=b):
                        for ic in range(8):
                            t.matmul(
                                pb[b][:, 0:512],
                                wA["o"][:, 1024 * ic + 128 * oc:
                                        1024 * ic + 128 * (oc + 1)],
                                o_fmA[:, 512 * ic: 512 * (ic + 1)],
                                start=(ic == 0), stop=False)
                        return t.matmul(
                            pb[b][:, 0:512],
                            borow[0:1, 128 * oc: 128 * (oc + 1)],
                            ones_row[0:1, 0:512], start=False, stop=True)
                    stm = P.step("pe", mm_wo,
                                 deps=[st_brows, wsteps["o"], P.use_bank(b)],
                                 tag=f"A3wo{oc}")

                    def ev(e, oc=oc, b=b, sl=sl):
                        return e.scalar_tensor_tensor(
                            zbuf[:, zc(oc): zc(oc) + 512],
                            xroll[:, xrs(sl, oc) + 1: xrs(sl, oc) + 513],
                            cst[:, C_TWO:C_TWO + 1],
                            pb[b][:, 0:512], ALU.mult, ALU.add)
                    ste = P.step("dve", ev,
                                 deps=[stm, st_x32, prev_spill] + P.ln_zr,
                                 tag=f"A3z{oc}")
                    P.set_bank(b, ste)
                    zst.append(ste)
                a3_last[tb] = zst[-1]

                s7, s1s = ln_stats(zst, lambda oc: zbuf[:, zc(oc): zc(oc) + 512])
                appst = []
                for oc in range(8):
                    src = zbuf[:, zc(oc): zc(oc) + 512]

                    def app(g, src=src):
                        g.tensor_sub(src, src, Bb32[:, :])
                        return g.tensor_mul(src, src, Ab32[:, :])
                    appst.append(P.step("pool", app, deps=[s7], tag="lnappA"))
                P.ln_ab = list(appst)
                P.ln_zr = s1s + appst
                tr = [(dram3(xa, TB * tb),
                       bass.AP(zbuf, 1, [[8 * 513, 128], [513, 8], [1, 512]]))]
                if upto == "A":
                    tr.append((dram3(out, TB * tb),
                               bass.AP(zbuf, 1, [[8 * 513, 128], [513, 8], [1, 512]])))
                prev_spill = P.dma("sq", tr, deps=appst, tag=f"y0spill{tb}")

            P.wait_only("pool", [prev_spill], tag="drainA")
            P.flush()

        if upto == "A":
            with nc.Block() as block:
                @block.sync
                def _(s):
                    s.wait_ge(P.sem["sq"], P.cnt["sq"])
                    s.wait_ge(P.sem["gq"], P.cnt["gq"])
            return nc

        def emit_layer(l, sin, sout, want_out):
            pqn, pqsh = params[(l, "qs")], params[(l, "qssh")]
            pkn, pksh = params[(l, "ks")], params[(l, "kssh")]
            L1 = ExitStack()
            x1_spills = [None] * NTB
            y_spills = [None] * NTB
            with L1:
                lsb = lambda name, shape, dt: L1.enter_context(
                    nc.sbuf_tensor(name, shape, dt))
                tabs = {n: lsb(f"tab{n}{l}", [128, T], BF16)
                        for n in ["cq", "sq", "ck", "sk"]}
                wqkvB = lsb(f"wqkv{l}", [128, 8 * 1536], BF16)
                wshB = lsb(f"wsh{l}", [128, 8 * 1024], BF16)
                woB = lsb(f"wo{l}", [128, 4 * 1024], BF16)
                xbfB = lsb(f"xbf{l}", [128, 2 * 8 * 516], BF16)
                xbs2 = lambda s, ic: 516 * (8 * s + ic)
                q_rot = lsb(f"qrot{l}", [128, 4 * 512], BF16)
                k_rot = lsb(f"krot{l}", [128, 2 * 4 * 512], BF16)
                v_aug = lsb(f"vaug{l}", [128, 2 * 4 * 520], BF16)
                o_fm = lsb(f"ofm{l}", [128, 4 * 512], BF16)
                PT = lsb(f"PT{l}", [128, 2 * 640], BF16)
                ybuf = lsb(f"ybuf{l}", [128, 8 * 512], BF16)
                ybs = lambda s, ic: 512 * ic
                qsqb = lsb(f"qsq{l}", [128, 2 * 512], BF16)
                etmp = lsb(f"etmp{l}", [128, 512], F32)
                e1b = lsb(f"e1{l}", [128, 2 * 512], F32)
                e2b = lsb(f"e2{l}", [128, 2 * 512], F32)

                st_tab = P.dma("gq", [(tabs[n][:, :], E["T" + n].ap())
                                      for n in ["cq", "sq", "ck", "sk"]],
                               tag=f"tabs{l}")
                st_wqkv = P.dma("gq", [(
                    wqkvB.ap().rearrange("p (ic n) -> p ic n", ic=8),
                    bass.AP(E[f"B{l}_wqkv"], 0,
                            [[1536, 128], [128 * 1536, 8], [1, 1536]]))],
                    tag=f"wqkv{l}")
                st_wsh = P.dma("gq", [(
                    wshB.ap().rearrange("p (ic n) -> p ic n", ic=8),
                    bass.AP(E[f"B{l}_wsh"], 0,
                            [[1024, 128], [128 * 1024, 8], [1, 1024]]))],
                    tag=f"wsh{l}")
                st_wo = P.dma("gq", [(
                    woB.ap().rearrange("p (ic n) -> p ic n", ic=4),
                    bass.AP(E[f"B{l}_wo"], 0,
                            [[1024, 128], [128 * 1024, 4], [1, 1024]]))],
                    tag=f"woL{l}")
                st_vone = P.step("pool", lambda g: g.memset(
                    bass.AP(v_aug, 64, [[2 * 4 * 520, 128], [65, 64], [1, 1]]),
                    1.0), tag=f"vone{l}")
                P.flush()

                guard_prev = None
                ptw = {0: None, 1: None}
                rbw = None
                rrow_w = None
                prev_fb = None
                qk_last = None
                b3_last = [None] * NTB
                pre_xbf = {}
                pre_x32 = {}

                def mk_xbfB(tb, deps=()):
                    sl = tb % 2
                    if tb == 0:
                        stg = P.step("pool", lambda g: g.memset(
                            bass.AP(xbfB, xbs2(0, 0),
                                    [[2 * 8 * 516, 128], [516, 8], [1, 1]]), 0.0),
                            tag=f"xbfg{l}")
                        return P.dma("gq", [(
                            bass.AP(xbfB, xbs2(sl, 0) + 1,
                                    [[2 * 8 * 516, 128], [516, 8], [1, 512]]),
                            dram3(sin, 0))], deps=[stg], tag=f"xbfB{l}_0")
                    return P.dma("gq", [(
                        bass.AP(xbfB, xbs2(sl, 0),
                                [[2 * 8 * 516, 128], [516, 8], [1, 513]]),
                        dram3(sin, TB * tb - 1, 513))], deps=deps,
                        tag=f"xbfB{l}_{tb}")

                def mk_x32B(tb, deps=()):
                    sl = tb % 2
                    if tb == 0:
                        stgx = P.step("pool", lambda g: g.memset(
                            bass.AP(xroll, xrs(0, 0),
                                    [[2 * 8 * 516, 128], [516, 8], [1, 1]]), 0.0),
                            tag=f"xg{l}")
                        return P.dma("sq", [(
                            bass.AP(xroll, xrs(sl, 0) + 1,
                                    [[2 * 8 * 516, 128], [516, 8], [1, 512]]),
                            dram3(sin, 0))], deps=[stgx] + list(deps),
                            tag=f"x32B{l}_{tb}")
                    return P.dma("sq", [(
                        bass.AP(xroll, xrs(sl, 0),
                                [[2 * 8 * 516, 128], [516, 8], [1, 513]]),
                        dram3(sin, TB * tb - 1, 513))], deps=deps,
                        tag=f"x32B{l}_{tb}")

                for tb in range(NTB):
                    sl = tb % 2
                    w = tb
                    st_xbf = pre_xbf.pop(tb) if tb in pre_xbf else mk_xbfB(tb)
                    st_x32 = pre_x32.pop(tb) if tb in pre_x32 else mk_x32B(tb)
                    if tb + 1 < NTB:
                        pre_xbf[tb + 1] = mk_xbfB(tb + 1)
                        pre_x32[tb + 1] = mk_x32B(
                            tb + 1, deps=[b3_last[tb - 1]] if tb >= 1 else [])

                    def hsrc(ic, t0, width, sl=sl):
                        off = 1 if ic < 4 else 0
                        return xbfB[:, xbs2(sl, ic) + off + t0:
                                    xbs2(sl, ic) + off + t0 + width]

                    for cx in range(8):
                        isq = cx < 4
                        c = cx % 4
                        colbase = (0 if isq else 512) + 128 * c
                        bq_, bsh_, bss_, bfb_ = [(0, 1, 2, 3), (4, 5, 2, 3)][cx % 2]
                        Tc = tabs["cq"] if isq else tabs["ck"]
                        Ts = tabs["sq"] if isq else tabs["sk"]
                        ppn = pqn if isq else pkn
                        ppsh = pqsh if isq else pksh
                        dstt = q_rot if isq else k_rot
                        dstoff = 512 * c if isq else 512 * (4 * sl + c)

                        def mm_q(t, colbase=colbase, bq_=bq_):
                            last = None
                            for ic in range(8):
                                last = t.matmul(
                                    pb[bq_][:, 0:512],
                                    wqkvB[:, 1536 * ic + colbase:
                                          1536 * ic + colbase + 128],
                                    hsrc(ic, 0, 512),
                                    start=(ic == 0), stop=(ic == 7))
                            return last
                        smq = P.step("pe", mm_q,
                                     deps=[st_xbf, st_wqkv, P.use_bank(bq_)],
                                     tag=f"B1q{l}{tb}{cx}")

                        def mm_sh(t, colbase=colbase, bsh_=bsh_):
                            last = None
                            for ic in range(8):
                                last = t.matmul(
                                    pb[bsh_][:, 0:512],
                                    wshB[:, 1024 * ic + colbase:
                                         1024 * ic + colbase + 128],
                                    hsrc(ic, 0, 512),
                                    start=(ic == 0), stop=(ic == 7))
                            return last
                        smsh = P.step("pe", mm_sh,
                                      deps=[st_xbf, st_wsh, P.use_bank(bsh_)],
                                      tag=f"B1s{l}{tb}{cx}")
                        ssq = P.step("act", lambda s, bq_=bq_, cx=cx: s.activation(
                            qsqb[:, 512 * (cx % 2): 512 * (cx % 2 + 1)],
                            pb[bq_][:, 0:512], AF.Square),
                            deps=[smq], tag=f"B1sq{l}{tb}{cx}")
                        sss = P.step("pe", lambda t, bss_=bss_, cx=cx: t.matmul(
                            pb[bss_][0:2, 0:512], ind2[:, :],
                            qsqb[:, 512 * (cx % 2): 512 * (cx % 2 + 1)],
                            start=True, stop=True),
                            deps=[ssq, st_ms, P.use_bank(bss_)],
                            tag=f"B1ss{l}{tb}{cx}")

                        def fca(s, bss_=bss_, isq=isq):
                            s.activation(sspair[:, :], pb[bss_][0:2, 0:512],
                                         AF.Ln, bias=cst[0:2, C_TINY:C_TINY + 1])
                            return s.activation(
                                fpair[:, :], sspair[:, :], AF.Exp, scale=-0.5,
                                bias=(cst[0:2, C_LN8:C_LN8 + 1] if isq else 0.0))
                        sfc = P.step("act", fca, deps=[sss, prev_fb],
                                     tag=f"B1f{l}{tb}{cx}")
                        P.set_bank(bss_, sfc)
                        sfb = P.step("pe", lambda t, bfb_=bfb_: t.matmul(
                            pb[bfb_][:, 0:512], ind2t[:, :], fpair[:, :],
                            start=True, stop=True),
                            deps=[sfc, P.use_bank(bfb_)], tag=f"B1fb{l}{tb}{cx}")
                        prev_fb = sfb
                        se1 = P.step("dve", lambda e, bq_=bq_, cx=cx, Tc=Tc,
                                     ppn=ppn, tb=tb: e.scalar_tensor_tensor(
                                         e1b[:, 512 * (cx % 2): 512 * (cx % 2 + 1)],
                                         pb[bq_][:, 0:512], ppn[:, :],
                                         Tc[:, TB * tb: TB * (tb + 1)],
                                         ALU.mult, ALU.mult),
                                     deps=[smq, ssq, st_tab, st_cf],
                                     tag=f"B1e1{l}{tb}{cx}")
                        P.set_bank(bq_, [ssq, se1])
                        se2 = P.step("dve", lambda e, bsh_=bsh_, cx=cx, Ts=Ts,
                                     ppsh=ppsh, tb=tb: e.scalar_tensor_tensor(
                                         e2b[:, 512 * (cx % 2): 512 * (cx % 2 + 1)],
                                         pb[bsh_][:, 0:512], ppsh[:, :],
                                         Ts[:, TB * tb: TB * (tb + 1)],
                                         ALU.mult, ALU.mult),
                                     deps=[smsh, st_tab, st_cf],
                                     tag=f"B1e2{l}{tb}{cx}")
                        P.set_bank(bsh_, se2)
                        sadd = P.step("pool", lambda g, cx=cx: g.tensor_add(
                            e1b[:, 512 * (cx % 2): 512 * (cx % 2 + 1)],
                            e1b[:, 512 * (cx % 2): 512 * (cx % 2 + 1)],
                            e2b[:, 512 * (cx % 2): 512 * (cx % 2 + 1)]),
                            deps=[se1, se2], tag=f"B1a{l}{tb}{cx}")
                        sfm = P.step("dve", lambda e, cx=cx, dstt=dstt,
                                     dstoff=dstoff, bfb_=bfb_: e.tensor_tensor(
                                         dstt[:, dstoff: dstoff + 512],
                                         e1b[:, 512 * (cx % 2): 512 * (cx % 2 + 1)],
                                         pb[bfb_][:, 0:512], ALU.mult),
                                     deps=[sadd, sfb], tag=f"B1fm{l}{tb}{cx}")
                        P.set_bank(bfb_, sfm)
                        qk_last = sfm
                    last_vev = None
                    for tc in range(4):
                        b = (0, 1, 4, 5)[tc]

                        def mm_v(t, tc=tc, b=b):
                            last = None
                            for ic in range(8):
                                last = t.matmul(
                                    pb[b][:, 0:512],
                                    hsrc(ic, 128 * tc, 128),
                                    wqkvB[:, 1536 * ic + 1024: 1536 * ic + 1536],
                                    start=(ic == 0), stop=(ic == 7))
                            return last
                        smv = P.step("pe", mm_v,
                                     deps=[st_xbf, st_wqkv, P.use_bank(b)],
                                     tag=f"B1v{l}{tb}{tc}")
                        vdst = bass.AP(v_aug, 520 * (4 * sl + tc),
                                       [[2 * 4 * 520, 128], [65, 8], [1, 64]])
                        psrc = bass.AP(pb[b], 0, [[512, 128], [64, 8], [1, 64]])
                        sev = P.step("act", lambda s, vdst=vdst, psrc=psrc: s.copy(
                            vdst, psrc), deps=[smv, st_vone],
                            tag=f"B1ve{l}{tb}{tc}")
                        P.set_bank(b, sev)
                        last_vev = sev

                    for qt in range(4):
                        redst = []
                        for h in range(8):
                            hc = h // 2
                            hr = 64 * (h % 2)
                            bl = h % 2
                            bc_ = 2 + h % 2
                            lbspan = 512 - 128 * qt
                            curspan = 128 * (qt + 1)
                            qsl = q_rot[hr:hr + 64, 512 * hc + 128 * qt:
                                        512 * hc + 128 * (qt + 1)]

                            def mm_stat(t, hc=hc, hr=hr, bl=bl, bc_=bc_, qt=qt,
                                        qsl=qsl, lbspan=lbspan, curspan=curspan,
                                        sl=sl, w=w):
                                if w > 0:
                                    kof = 512 * (4 * ((w - 1) % 2) + hc) + 128 * qt
                                    t.matmul(pb[bl][:, 0:lbspan], qsl,
                                             k_rot[hr:hr + 64, kof: kof + lbspan],
                                             start=True, stop=False)
                                    t.matmul(pb[bl][:, 0:128], ident_bf[:, :],
                                             maskGE[:, :], start=False, stop=True)
                                kof = 512 * (4 * sl + hc)
                                t.matmul(pb[bc_][:, 0:curspan], qsl,
                                         k_rot[hr:hr + 64, kof: kof + curspan],
                                         start=True, stop=False)
                                return t.matmul(
                                    pb[bc_][:, 128 * qt: 128 * (qt + 1)],
                                    ident_bf[:, :], maskLE[:, :],
                                    start=False, stop=True)
                            sstat = P.step("pe", mm_stat,
                                           deps=[last_vev, qk_last, st_cst,
                                                 P.use_bank(bl), P.use_bank(bc_)],
                                           tag=f"B2st{l}{tb}{qt}{h}")

                            def red(e, h=h, bl=bl, bc_=bc_, w=w,
                                    lbspan=lbspan, curspan=curspan):
                                if w > 0:
                                    e.reduce_max(negtmp[:, 0:1],
                                                 pb[bl][:, 0:lbspan],
                                                 axis=AX.X, negate=True)
                                    e.reduce_max(negtmp[:, 1:2],
                                                 pb[bc_][:, 0:curspan],
                                                 axis=AX.X, negate=True)
                                    return e.tensor_tensor(
                                        negmax8[:, h:h + 1], negtmp[:, 0:1],
                                        negtmp[:, 1:2], ALU.min)
                                return e.reduce_max(negmax8[:, h:h + 1],
                                                    pb[bc_][:, 0:curspan],
                                                    axis=AX.X, negate=True)
                            sred = P.step("dve", red, deps=[sstat],
                                          tag=f"B2r{l}{tb}{qt}{h}")
                            P.set_bank(bl, sred)
                            P.set_bank(bc_, sred)
                            redst.append(sred)
                        sttp = P.step("pe", lambda t: t.transpose(
                            bass.AP(pb[7], 0, [[512, 8], [1, 128]]),
                            negmax8[:, :], ident[:, :]),
                            deps=redst + [P.use_bank(7)], tag=f"B2tp{l}{tb}{qt}")
                        stnm8 = P.step("act", lambda s: s.copy(
                            nm8[:, :], bass.AP(pb[7], 0, [[512, 8], [1, 128]])),
                            deps=[sttp], tag=f"B2nm{l}{tb}{qt}")
                        P.set_bank(7, stnm8)

                        for hp in range(4):
                            bV = 4 + hp % 2
                            pvst = []
                            for hh in range(2):
                                h = 2 * hp + hh
                                hc = h // 2
                                hr = 64 * hh
                                bT1 = hh
                                bT2 = 2 + hh
                                qsl = q_rot[hr:hr + 64, 512 * hc + 128 * qt:
                                            512 * hc + 128 * (qt + 1)]
                                chunks = []
                                if w > 0:
                                    for kc in range(qt, 4):
                                        chunks.append(((w - 1) % 2, kc,
                                                       maskLE if kc == qt else None))
                                for kc in range(0, qt + 1):
                                    chunks.append((sl, kc,
                                                   maskGE if kc == qt else None))
                                nch = len(chunks)

                                def mm_stq(t, hc=hc, hr=hr, qsl=qsl, h=h,
                                           chunks=chunks, bT1=bT1, bT2=bT2):
                                    last = None
                                    nch = len(chunks)
                                    n1 = min(nch, 4)
                                    for pos, (par, kc, msk) in enumerate(chunks):
                                        bT = bT1 if pos < 4 else bT2
                                        fo = 128 * (pos % 4)
                                        kof = 512 * (4 * par + hc) + 128 * kc
                                        t.matmul(pb[bT][:, fo: fo + 128],
                                                 k_rot[hr:hr + 64, kof: kof + 128],
                                                 qsl, start=(pos in (0, 4)),
                                                 stop=False)
                                        if msk is not None:
                                            t.matmul(pb[bT][:, fo: fo + 128],
                                                     ident_bf[:, :], msk[:, :],
                                                     start=False, stop=False)
                                    nrep1 = bass.AP(nm8, 0,
                                                    [[128, 8], [0, n1], [1, 128]])
                                    last = t.matmul(
                                        pb[bT1][:, 0:128 * n1],
                                        sel8[:, 128 * h: 128 * (h + 1)],
                                        nrep1, start=False, stop=True)
                                    if nch > 4:
                                        nrep2 = bass.AP(nm8, 0,
                                                        [[128, 8], [0, nch - 4],
                                                         [1, 128]])
                                        last = t.matmul(
                                            pb[bT2][:, 0:128 * (nch - 4)],
                                            sel8[:, 128 * h: 128 * (h + 1)],
                                            nrep2, start=False, stop=True)
                                    return last
                                smT = P.step("pe", mm_stq,
                                             deps=[stnm8, P.use_bank(bT1),
                                                   P.use_bank(bT2)],
                                             tag=f"B2t{l}{tb}{qt}{h}")

                                def expf(s, h=h, hh=hh, nch=nch, bT1=bT1, bT2=bT2):
                                    po = 640 * hh
                                    n1 = min(nch, 4) * 128
                                    r = s.activation(PT[:, po: po + n1],
                                                     pb[bT1][:, 0:n1], AF.Exp)
                                    if nch > 4:
                                        r = s.activation(
                                            PT[:, po + 512:
                                               po + 512 + 128 * (nch - 4)],
                                            pb[bT2][:, 0:128 * (nch - 4)], AF.Exp)
                                    return r
                                sexp = P.step("act", expf, deps=[smT, ptw[hh]],
                                              tag=f"B2e{l}{tb}{qt}{h}")
                                P.set_bank(bT1, sexp)
                                P.set_bank(bT2, sexp)

                                def mm_pv(t, h=h, hh=hh, chunks=chunks, bV=bV):
                                    last = None
                                    po = 640 * hh
                                    for pos, (par, kc, msk) in enumerate(chunks):
                                        vof = 520 * (4 * par + kc) + 65 * h
                                        fo = po + (128 * pos if pos < 4
                                                   else 512 + 128 * (pos - 4))
                                        last = t.matmul(
                                            pb[bV][0:65, 128 * hh: 128 * hh + 128],
                                            v_aug[:, vof: vof + 65],
                                            PT[:, fo: fo + 128],
                                            start=(pos == 0),
                                            stop=(pos == len(chunks) - 1))
                                    return last
                                dpv = [sexp] + (P.use_bank(bV) if hh == 0 else [])
                                spv = P.step("pe", mm_pv, deps=dpv,
                                             tag=f"B2pv{l}{tb}{qt}{h}")
                                ptw[hh] = spv
                                pvst.append(spv)
                            def rcp(s, bV=bV):
                                s.activation(rrow[0:1, 0:256],
                                             pb[bV][64:65, 0:256], AF.Ln)
                                return s.activation(rrow[0:1, 0:256],
                                                    rrow[0:1, 0:256], AF.Exp,
                                                    scale=-1.0)
                            src_ = P.step("act", rcp, deps=pvst + [rrow_w],
                                          tag=f"B2rc{l}{tb}{qt}{hp}")
                            rrow_w = src_
                            bRB = 6
                            srb = P.step("pe", lambda t, bRB=bRB: t.matmul(
                                pb[bRB][0:64, 0:256], ones_row32[0:1, 0:64],
                                rrow[0:1, 0:256], start=True, stop=True),
                                deps=[src_, P.use_bank(bRB)],
                                tag=f"B2rb{l}{tb}{qt}{hp}")
                            rrow_w = srb
                            srbe = P.step("act", lambda s, bRB=bRB: s.copy(
                                rb32[0:64, 0:256], pb[bRB][0:64, 0:256]),
                                deps=[srb, rbw], tag=f"B2rbe{l}{tb}{qt}{hp}")
                            P.set_bank(bRB, srbe)
                            soes = []
                            for hh in range(2):
                                h = 2 * hp + hh
                                hc = h // 2
                                hr = 64 * hh
                                soe = P.step("dve", lambda e, hc=hc, hr=hr, qt=qt,
                                             hh=hh, bV=bV: e.tensor_tensor(
                                                 o_fm[hr:hr + 64,
                                                      512 * hc + 128 * qt:
                                                      512 * hc + 128 * (qt + 1)],
                                                 pb[bV][0:64,
                                                        128 * hh: 128 * hh + 128],
                                                 rb32[0:64,
                                                      128 * hh: 128 * hh + 128],
                                                 ALU.mult),
                                             deps=[pvst[hh], srbe],
                                             tag=f"B2oe{l}{tb}{qt}{h}")
                                soes.append(soe)
                            P.set_bank(bV, soes)
                            rbw = soes
                    P.flush()

                    if tb == 0:
                        stguard = P.step("pool", lambda g: g.memset(
                            bass.AP(zbuf, 0, [[8 * 513, 128], [513, 8], [1, 1]]),
                            0.0), tag=f"zg{l}")
                    else:
                        stguard = P.step("pool", lambda g: g.tensor_copy(
                            bass.AP(zbuf, 0, [[8 * 513, 128], [513, 8], [1, 1]]),
                            bass.AP(zbuf, 512, [[8 * 513, 128], [513, 8], [1, 1]])),
                            deps=[guard_prev] + P.ln_zr, tag=f"zg{l}{tb}")
                    zst = []
                    for oc in range(8):
                        b = 4 + oc % 2

                        def mm_wo2(t, oc=oc, b=b):
                            last = None
                            for icc in range(4):
                                last = t.matmul(
                                    pb[b][:, 0:512],
                                    woB[:, 1024 * icc + 128 * oc:
                                        1024 * icc + 128 * (oc + 1)],
                                    o_fm[:, 512 * icc: 512 * (icc + 1)],
                                    start=(icc == 0), stop=(icc == 3))
                            return last
                        stm = P.step("pe", mm_wo2,
                                     deps=[st_wo, rbw, P.use_bank(b)],
                                     tag=f"B3wo{l}{tb}{oc}")

                        def ev(e, oc=oc, b=b, sl=sl):
                            return e.tensor_tensor(
                                zbuf[:, zc(oc): zc(oc) + 512],
                                pb[b][:, 0:512],
                                xroll[:, xrs(sl, oc) + 1: xrs(sl, oc) + 513],
                                ALU.add)
                        ste = P.step("dve", ev,
                                     deps=[stm, st_x32, stguard,
                                           x1_spills[tb - 1] if tb >= 1 else None]
                                     + P.ln_zr,
                                     tag=f"B3z{l}{tb}{oc}")
                        P.set_bank(b, ste)
                        zst.append(ste)
                    guard_prev = zst[-1]
                    b3_last[tb] = zst[-1]
                    x1_spills[tb] = P.dma("sq", [(
                        dram3(sout, TB * tb),
                        bass.AP(zbuf, 1, [[8 * 513, 128], [513, 8], [1, 512]]))],
                        deps=zst, tag=f"x1sp{l}{tb}")

                    def zsrc(oc):
                        off = zc(oc) if oc < 4 else zc(oc) - 1
                        return zbuf[:, off: off + 512]
                    s7, s1s = ln_stats(zst, zsrc)
                    P.ln_ab = []
                    P.ln_zr = list(s1s)
                    yapp = []
                    prev_yapp3 = None
                    for oc in range(8):
                        sa1 = P.step("pool", lambda g, oc=oc: g.tensor_sub(
                            etmp[:, :], zsrc(oc), Bb32[:, :]),
                            deps=[s7, prev_yapp3], tag="yapp1")
                        sa2 = P.step("pool", lambda g: g.tensor_mul(
                            etmp[:, :], etmp[:, :], Ab32[:, :]), deps=[sa1],
                            tag="yapp2")
                        sa3 = P.step("dve", lambda e, oc=oc, sl=sl: e.tensor_scalar(
                            ybuf[:, ybs(sl, oc): ybs(sl, oc) + 512],
                            etmp[:, :], params[(l, "lng")][:, oc:oc + 1],
                            params[(l, "lnb")][:, oc:oc + 1], ALU.mult, ALU.add),
                            deps=[sa2, y_spills[tb - 1] if tb >= 1 else None],
                            tag="yapp3")
                        prev_yapp3 = sa3
                        yapp.append(sa3)
                        P.ln_ab.append(sa1); P.ln_ab.append(sa2)
                        P.ln_zr.append(sa1)
                    y_spills[tb] = P.dma("sq", [(
                        bass.AP(yb, TB * tb, [[T, 128], [128 * T, 8], [1, 512]]),
                        bass.AP(ybuf, 0,
                                [[8 * 512, 128], [512, 8], [1, 512]]))],
                        deps=yapp, tag=f"ysp{l}{tb}")

                P.wait_only("pool", [x1_spills[NTB - 1], y_spills[NTB - 1]],
                            tag=f"drainB13{l}")
                P.flush()

            G2 = ExitStack()
            x2_spills = []
            with G2:
                gsb = lambda name, shape, dt: G2.enter_context(
                    nc.sbuf_tensor(name, shape, dt))
                w1B = gsb(f"w1B{l}", [128, 8 * 5460], BF16)
                w2g = gsb(f"w2g{l}", [128, 2 * 4 * 1024], BF16)
                actB = gsb(f"actB{l}", [128, NJ * 512], BF16)
                ybloc = gsb(f"ybloc{l}", [128, 8 * 512], BF16)
                gel32 = gsb(f"gel{l}", [128, 512], F32)

                SPLIT = 128 * 11
                w1parts = []
                for (c0, wdt) in [(0, SPLIT), (HID, SPLIT),
                                  (SPLIT, HID - SPLIT), (HID + SPLIT, HID - SPLIT)]:
                    w1parts.append(P.dma("gq", [(
                        bass.AP(w1B, c0, [[8 * 5460, 128], [5460, 8], [1, wdt]]),
                        bass.AP(E[f"B{l}_w1"], c0,
                                [[5460, 128], [128 * 5460, 8], [1, wdt]]))],
                        tag=f"w1_{l}_{c0}"))
                P.flush()

                pre4 = {}
                ev2_last = None
                act_carry = None
                sg_carry = None

                def mk_x1ld(tb, deps=()):
                    return P.dma("sq", [(
                        bass.AP(xroll, xrs(tb % 2, 0) + 1,
                                [[2 * 8 * 516, 128], [516, 8], [1, 512]]),
                        dram3(sout, TB * tb))],
                        deps=[x1_spills[tb]] + list(deps), tag=f"x1ld{l}{tb}")

                def mk_yld(tb, deps=()):
                    return P.dma("sq", [(
                        bass.AP(ybloc, 0, [[8 * 512, 128], [512, 8], [1, 512]]),
                        bass.AP(yb, TB * tb, [[T, 128], [128 * T, 8], [1, 512]]))],
                        deps=[y_spills[tb]] + list(deps), tag=f"yld{l}{tb}")

                for tb in range(NTB):
                    sl = tb % 2
                    if tb in pre4:
                        pp = pre4.pop(tb)
                        stx1 = pp[0]
                        sty = pp[1] if len(pp) > 1 else mk_yld(
                            tb, deps=[sg_carry] if sg_carry else [])
                    else:
                        stx1 = mk_x1ld(tb)
                        sty = mk_yld(tb)
                    if tb + 1 < NTB:
                        pre4[tb + 1] = [
                            mk_x1ld(tb + 1, deps=[ev2_last] if ev2_last else [])]

                    def w2load(g, deps, tb=tb):
                        j0 = 4 * g
                        njg = 4 if g < 5 else 2
                        trs = [(
                            bass.AP(w2g, 1024 * 4 * (g % 2),
                                    [[2 * 4 * 1024, 128], [1024, njg], [1, 1024]]),
                            bass.AP(E[f"B{l}_w2"], 1024 * 128 * j0,
                                    [[1024, 128], [128 * 1024, njg], [1, 1024]]))]
                        return P.dma("gq", trs, deps=deps, tag=f"w2g{l}{tb}{g}")

                    if tb == 0:
                        w2ld = [None] * 6
                        w2ld[0] = w2load(0, [])
                        w2ld[1] = w2load(1, [])
                    else:
                        w2ld = [None] * 6
                        w2ld[0] = nw2[0]
                        w2ld[1] = nw2[1]
                    nw2 = [None, None]

                    prev_act = [None, None]
                    actst = [None] * NJ
                    for j in range(NJ):
                        wj = 128 if j < NJ - 1 else HID - 128 * (NJ - 1)
                        ba_, bg_ = (0, 1) if j % 2 == 0 else (2, 3)
                        acol = 128 * j
                        gcol = HID + 128 * j

                        def mma(t, wj=wj, ba_=ba_, acol=acol):
                            last = None
                            for ic in range(8):
                                last = t.matmul(
                                    pb[ba_][0:wj, 0:512],
                                    w1B[:, 5460 * ic + acol: 5460 * ic + acol + wj],
                                    ybloc[:, 512 * ic: 512 * (ic + 1)],
                                    start=(ic == 0), stop=(ic == 7))
                            return last
                        w1dep = w1parts[0] if j <= 10 else w1parts[2]
                        sa_ = P.step("pe", mma,
                                     deps=[sty, w1dep, P.use_bank(ba_)],
                                     tag=f"B4a{l}{tb}{j}")

                        def mmg(t, wj=wj, bg_=bg_, gcol=gcol):
                            last = None
                            for ic in range(8):
                                last = t.matmul(
                                    pb[bg_][0:wj, 0:512],
                                    w1B[:, 5460 * ic + gcol: 5460 * ic + gcol + wj],
                                    ybloc[:, 512 * ic: 512 * (ic + 1)],
                                    start=(ic == 0), stop=(ic == 7))
                            return last
                        w1depg = w1parts[1] if j <= 10 else w1parts[3]
                        sg_ = P.step("pe", mmg,
                                     deps=[sty, w1depg, P.use_bank(bg_)],
                                     tag=f"B4g{l}{tb}{j}")
                        sgel = P.step("act", lambda s, j=j, wj=wj, bg_=bg_:
                                      s.activation(
                                          gel32[0:wj, 0:512],
                                          pb[bg_][0:wj, 0:512], AF.Gelu,
                                          bias=params[(l, "b1g")][0:wj, j:j + 1]),
                                      deps=[sg_, actst[j - 1] if j >= 1
                                            else act_carry],
                                      tag=f"B4gel{l}{tb}{j}")
                        P.set_bank(bg_, sgel)
                        sact = P.step("dve", lambda e, j=j, wj=wj, ba_=ba_:
                                      e.scalar_tensor_tensor(
                                          actB[0:wj, 512 * j: 512 * j + 512],
                                          pb[ba_][0:wj, 0:512],
                                          params[(l, "b1a")][0:wj, j:j + 1],
                                          gel32[0:wj, 0:512],
                                          ALU.add, ALU.mult),
                                      deps=[sa_, sgel], tag=f"B4act{l}{tb}{j}")
                        P.set_bank(ba_, sact)
                        prev_act[j % 2] = sact
                        actst[j] = sact
                    act_carry = actst[NJ - 1]
                    sg_carry = sg_
                    if tb + 1 < NTB and (tb + 1) in pre4:
                        pre4[tb + 1].append(mk_yld(tb + 1, deps=[sg_carry]))

                    mmsteps = []
                    for g in range(6):
                        j0 = 4 * g
                        njg = 4 if g < 5 else 2

                        def mmw2(t, g=g, j0=j0, njg=njg):
                            last = None
                            for jj in range(njg):
                                j = j0 + jj
                                wj = 128 if j < NJ - 1 else HID - 128 * (NJ - 1)
                                for oc in range(8):
                                    last = t.matmul(
                                        pb[oc][:, 0:512],
                                        w2g[0:wj,
                                            1024 * (4 * (g % 2) + jj) + 128 * oc:
                                            1024 * (4 * (g % 2) + jj) +
                                            128 * (oc + 1)],
                                        actB[0:wj, 512 * j: 512 * j + 512],
                                        start=(j == 0), stop=(j == NJ - 1))
                            return last
                        depsl = [w2ld[g], actst[min(j0 + njg, NJ) - 1]]
                        if g == 0:
                            depsl += [P.use_bank(bb) for bb in range(8)]
                        sm = P.step("pe", mmw2, deps=depsl, tag=f"B4w2{l}{tb}{g}")
                        mmsteps.append(sm)
                        if g + 2 < 6:
                            w2ld[g + 2] = w2load(g + 2, [sm])
                        elif tb + 1 < NTB:
                            nw2[g - 4] = w2load(g - 4, [sm], tb=tb + 1)
                    zst2 = []
                    for oc in range(8):
                        def ev2(e, oc=oc, sl=sl):
                            return e.scalar_tensor_tensor(
                                zbuf[:, zc(oc): zc(oc) + 512],
                                pb[oc][:, 0:512],
                                params[(l, "b2")][:, oc:oc + 1],
                                xroll[:, xrs(sl, oc) + 1: xrs(sl, oc) + 513],
                                ALU.add, ALU.add)
                        s_ = P.step("dve", ev2,
                                    deps=[mmsteps[-1], stx1,
                                          x2_spills[-1] if x2_spills else None],
                                    tag=f"B4z{l}{tb}{oc}")
                        P.set_bank(oc, s_)
                        zst2.append(s_)
                    ev2_last = zst2[-1]
                    trs = [(dram3(sout, TB * tb),
                            bass.AP(zbuf, 1, [[8 * 513, 128], [513, 8], [1, 512]]))]
                    if want_out:
                        trs.append((dram3(out, TB * tb),
                                    bass.AP(zbuf, 1,
                                            [[8 * 513, 128], [513, 8], [1, 512]])))
                    x2_spills.append(P.dma("sq", trs, deps=zst2,
                                           tag=f"x2sp{l}{tb}"))
                P.wait_only("pool", [x2_spills[-1]], tag=f"drainB4{l}")
                P.flush()
            return x2_spills[-1]

        last0 = emit_layer(0, xa, xb, want_out=(upto == "L0"))
        if upto == "L0":
            with nc.Block() as block:
                @block.sync
                def _(s):
                    s.wait_ge(P.sem["sq"], P.cnt["sq"])
                    s.wait_ge(P.sem["gq"], P.cnt["gq"])
            return nc
        last1 = emit_layer(1, xb, xc, want_out=False)

        outsp = []
        for tb in range(NTB):
            st0 = P.dma("sq", [(
                bass.AP(xroll, xrs(0, 0) + 1,
                        [[2 * 8 * 516, 128], [516, 8], [1, 512]]),
                dram3(xa, TB * tb))], deps=[last1], tag=f"fl0{tb}")
            st1 = P.dma("sq", [(
                bass.AP(xroll, xrs(1, 0) + 1,
                        [[2 * 8 * 516, 128], [516, 8], [1, 512]]),
                dram3(xc, TB * tb))], deps=[last1], tag=f"fl1{tb}")
            zst = []
            for oc in range(8):
                s_ = P.step("dve", lambda e, oc=oc: e.tensor_tensor(
                    zbuf[:, zc(oc): zc(oc) + 512],
                    xroll[:, xrs(0, oc) + 1: xrs(0, oc) + 513],
                    xroll[:, xrs(1, oc) + 1: xrs(1, oc) + 513], ALU.add),
                    deps=[st0, st1, outsp[-1] if outsp else None] + P.ln_zr,
                    tag=f"fz{tb}{oc}")
                zst.append(s_)
            s7, s1s = ln_stats(zst, lambda oc: zbuf[:, zc(oc): zc(oc) + 512])
            appst = []
            for oc in range(8):
                src = zbuf[:, zc(oc): zc(oc) + 512]

                def app(g, src=src):
                    g.tensor_sub(src, src, Bb32[:, :])
                    return g.tensor_mul(src, src, Ab32[:, :])
                appst.append(P.step("pool", app, deps=[s7], tag="lnappF"))
            P.ln_ab = list(appst)
            P.ln_zr = s1s + appst
            outsp.append(P.dma("sq", [(
                dram3(out, TB * tb),
                bass.AP(zbuf, 1, [[8 * 513, 128], [513, 8], [1, 512]]))],
                deps=appst, tag=f"outF{tb}"))
            P.flush()

        with nc.Block() as block:
            @block.sync
            def _(s):
                s.wait_ge(P.sem["sq"], P.cnt["sq"])
                s.wait_ge(P.sem["gq"], P.cnt["gq"])
    return nc


def kernel(**inputs):
    consts = host_consts()
    shared = host_layout(inputs)
    shared.update(consts)
    nc = build_nc("full")
    x = np.asarray(inputs["L0_attn_x"])
    in_maps = []
    for b in range(8):
        m = dict(shared)
        m["xT"] = np.ascontiguousarray(x[b].T)
        in_maps.append(m)
    res = run_bass_kernel_spmd(nc, in_maps, core_ids=list(range(8)))
    outs = [np.ascontiguousarray(r["yT"].T) for r in res.results]
    return np.stack(outs, axis=0)


# version 15
# speedup vs baseline: 1.1915x; 1.0713x over previous
import numpy as np
import ml_dtypes
from contextlib import ExitStack
from types import SimpleNamespace

import concourse.bass as bass
import concourse.mybir as mybir
from concourse.bass_utils import run_bass_kernel_spmd

F32 = mybir.dt.float32
BF16 = mybir.dt.bfloat16
AF = mybir.ActivationFunctionType
ALU = mybir.AluOpType
AX = mybir.AxisListType
BF = ml_dtypes.bfloat16

T = 2048
D = 1024
NTB = 4
TB = 512
HID = 2730
NJ = 22
NEG = -1.0e30
C_EIGHTH, C_EPS, C_TINY, C_INV1024, C_TWO, C_LN8 = 0, 1, 2, 3, 4, 5


def host_consts():
    c = {}
    c["identin"] = np.eye(128, dtype=np.float32)
    sel8 = np.zeros((8, 1024), np.float32)
    for h in range(8):
        sel8[h, 128 * h:128 * (h + 1)] = 1.0
    c["sel8in"] = sel8
    i2t = np.zeros((2, 128), np.float32)
    i2t[0, 0:64] = 1.0
    i2t[1, 64:128] = 1.0
    c["ind2tin"] = i2t
    r = np.arange(128)
    c["maskLEin"] = np.where(r[None, :] <= r[:, None], 0.0, NEG).astype(np.float32)
    c["maskGEin"] = np.where(r[None, :] >= r[:, None], 0.0, NEG).astype(np.float32)
    half = np.arange(0, 64, 2, dtype=np.float64)
    inv_freq = 1.0 / 10000.0 ** (half / 64)
    t = np.arange(T, dtype=np.float64)
    freqs = t[:, None] * inv_freq[None, :]
    base = (half + 0.4 * 64) / (1.4 * 64)
    power = (t - T // 2) / 256.0
    sc = base[None, :] ** power[:, None]
    cos, sin = np.cos(freqs), np.sin(freqs)
    sgn = np.concatenate([-np.ones(32), np.ones(32)])[:, None]

    def fm(tab):
        d64 = np.concatenate([tab.T, tab.T], axis=0)
        return np.concatenate([d64, d64], axis=0).astype(np.float32)

    def fmsgn(tab):
        d64 = np.concatenate([tab.T, tab.T], axis=0) * sgn
        return np.concatenate([d64, d64], axis=0).astype(np.float32)

    c["Tcq"] = fm(cos * sc)
    c["Tsq"] = fmsgn(sin * sc)
    c["Tck"] = fm(cos / sc)
    c["Tsk"] = fmsgn(sin / sc)
    return c


def host_layout(ins):
    d = {}
    d["A_wqT"] = np.ascontiguousarray(np.asarray(ins["L0_attn_Wq"]).T)
    d["A_wkT"] = np.ascontiguousarray(np.asarray(ins["L0_attn_Wk"]).T)
    d["A_wvT"] = np.ascontiguousarray(np.asarray(ins["L0_attn_Wv"]).T)
    d["A_woT"] = np.ascontiguousarray(np.asarray(ins["L0_attn_Wo"]).T)
    d["A_bq"] = np.ascontiguousarray(np.asarray(ins["L0_attn_bq"]).reshape(8, 128).T)
    d["A_bk"] = np.ascontiguousarray(np.asarray(ins["L0_attn_bk"]).reshape(8, 128).T)
    d["A_bvrow"] = np.asarray(ins["L0_attn_bv"]).reshape(1, 1024).copy()
    d["A_borow"] = np.asarray(ins["L0_attn_bo"]).reshape(1, 1024).copy()
    perm = np.concatenate([np.arange(32, 64), np.arange(32)])
    permfull = np.concatenate([64 * h + perm for h in range(8)])
    for l in range(2):
        w = np.asarray(ins["L0_ffn_wqkv"][l])
        d[f"B{l}_wqkv"] = np.ascontiguousarray(w)
        d[f"B{l}_wsh"] = np.ascontiguousarray(
            np.concatenate([w[:, permfull], w[:, 512 + permfull]], axis=1))
        d[f"B{l}_wo"] = np.ascontiguousarray(np.asarray(ins["L0_ffn_wo"][l]))
        d[f"B{l}_w1"] = np.ascontiguousarray(np.asarray(ins["L0_ffn_w1"][l]))
        w2p = np.zeros((NJ * 128, 1024), np.float32)
        w2p[:HID] = np.asarray(ins["L0_ffn_w2"][l])
        d[f"B{l}_w2"] = w2p
        qs = np.asarray(ins["L0_ffn_q_scale"][l]).astype(np.float32)
        ks = np.asarray(ins["L0_ffn_k_scale"][l]).astype(np.float32)
        d[f"B{l}_qs"] = np.tile(qs, 2).reshape(128, 1).copy()
        d[f"B{l}_qssh"] = np.tile(qs[perm], 2).reshape(128, 1).copy()
        d[f"B{l}_ks"] = np.tile(ks, 2).reshape(128, 1).copy()
        d[f"B{l}_kssh"] = np.tile(ks[perm], 2).reshape(128, 1).copy()
        d[f"B{l}_lng"] = np.ascontiguousarray(
            np.asarray(ins["L0_ffn_ln_g"][l]).reshape(8, 128).T)
        d[f"B{l}_lnb"] = np.ascontiguousarray(
            np.asarray(ins["L0_ffn_ln_b"][l]).reshape(8, 128).T)
        b1 = np.asarray(ins["L0_ffn_b1"][l])
        b1a = np.zeros(NJ * 128, np.float32); b1a[:HID] = b1[:HID]
        b1g = np.zeros(NJ * 128, np.float32); b1g[:HID] = b1[HID:]
        d[f"B{l}_b1a"] = np.ascontiguousarray(b1a.reshape(NJ, 128).T)
        d[f"B{l}_b1g"] = np.ascontiguousarray(b1g.reshape(NJ, 128).T)
        d[f"B{l}_b2"] = np.ascontiguousarray(
            np.asarray(ins["L0_ffn_b2"][l]).reshape(8, 128).T)
    return d


SEC = {"pe": "tensor", "act": "scalar", "dve": "vector",
       "pool": "gpsimd", "gq": "gpsimd", "sq": "sync"}


class Prog:
    def __init__(self, nc, ctx):
        self.nc = nc
        self.sem = {k: ctx.enter_context(nc.semaphore(f"s_{k}")) for k in SEC}
        self.cnt = {k: 0 for k in SEC}
        self.steps = []
        self.wm = {s: {} for s in set(SEC.values())}
        self.bank_state = [[] for _ in range(8)]

    def step(self, key, emit, deps=(), n=1, tag=""):
        flat = []
        for d in deps:
            if d is None:
                continue
            if isinstance(d, list):
                flat.extend(x for x in d if x is not None)
            else:
                flat.append(d)
        s = SimpleNamespace(key=key, emit=emit, tag=tag, n=n, deps=flat,
                            section=SEC[key])
        self.cnt[key] += n
        s.val = self.cnt[key]
        self.steps.append(s)
        return s

    def dma(self, key, transfers, deps=(), tag=""):
        assert key in ("sq", "gq")
        sem = self.sem[key]
        tr = list(transfers)

        def emit(eng):
            last = None
            for (o, i) in tr:
                last = eng.dma_start(o, i).then_inc(sem, 16)
            return last
        return self.step(key, emit, deps, n=16 * len(tr), tag=tag)

    def wait_only2(self, key, deps, tag=""):
        return self.wait_only(key, deps, tag)

    def wait_only(self, key, deps, tag=""):
        return self.step(key, (lambda eng: None), deps, n=0, tag=tag)

    def flush(self):
        steps, self.steps = self.steps, []
        if not steps:
            return
        secs = {}
        for s in steps:
            secs.setdefault(s.section, []).append(s)
        with self.nc.Block() as block:
            for secname, slist in secs.items():
                def make(slist=slist, secname=secname):
                    def body(eng):
                        for s in slist:
                            for d in s.deps:
                                if d.key == s.key and d.key not in ("sq", "gq"):
                                    continue
                                wm = self.wm[secname]
                                if wm.get(d.key, 0) >= d.val:
                                    continue
                                eng.wait_ge(self.sem[d.key], d.val)
                                wm[d.key] = d.val
                            last = s.emit(eng)
                            if s.key not in ("sq", "gq") and last is not None:
                                last.then_inc(self.sem[s.key], 1)
                    return body
                getattr(block, secname)(make())

    def use_bank(self, b):
        return list(self.bank_state[b] or [])

    def set_bank(self, b, step):
        self.bank_state[b] = step if isinstance(step, list) else [step]

    def add_bank_reader(self, b, step):
        if self.bank_state[b] is None:
            self.bank_state[b] = []
        self.bank_state[b].append(step)


def build_nc(upto="full"):
    nc = bass.Bass("TRN2", target_bir_lowering=False)
    E = {}

    def din(name, shape, dt=F32):
        E[name] = nc.dram_tensor(name, shape, dt, kind="ExternalInput")
        return E[name]

    din("xT", [D, T])
    din("identin", [128, 128]); din("sel8in", [8, 1024]); din("ind2tin", [2, 128])
    din("maskLEin", [128, 128]); din("maskGEin", [128, 128])
    for n in ["Tcq", "Tsq", "Tck", "Tsk"]:
        din(n, [128, T])
    din("A_wqT", [D, D]); din("A_wkT", [D, D]); din("A_wvT", [D, D]); din("A_woT", [D, D])
    din("A_bq", [128, 8]); din("A_bk", [128, 8])
    din("A_bvrow", [1, D]); din("A_borow", [1, D])
    for l in range(2):
        din(f"B{l}_wqkv", [D, 1536]); din(f"B{l}_wsh", [D, D])
        din(f"B{l}_wo", [512, D]); din(f"B{l}_w1", [D, 5460])
        din(f"B{l}_w2", [NJ * 128, D])
        for n in ["qs", "qssh", "ks", "kssh"]:
            din(f"B{l}_{n}", [128, 1])
        din(f"B{l}_lng", [128, 8]); din(f"B{l}_lnb", [128, 8])
        din(f"B{l}_b1a", [128, NJ]); din(f"B{l}_b1g", [128, NJ]); din(f"B{l}_b2", [128, 8])
    out = nc.dram_tensor("yT", [D, T], F32, kind="ExternalOutput")
    xa = nc.dram_tensor("xa_scr", [D, T], F32)
    xb = nc.dram_tensor("xb_scr", [D, T], F32)
    xc = nc.dram_tensor("xc_scr", [D, T], F32)
    yb = nc.dram_tensor("yb_scr", [D, T], BF16)

    def dram3(dten, col0, width=512):
        return bass.AP(dten, col0, [[T, 128], [128 * T, 8], [1, width]])

    ctx = ExitStack()
    with ctx:
        P = Prog(nc, ctx)
        sb = lambda name, shape, dt: ctx.enter_context(nc.sbuf_tensor(name, shape, dt))
        pbig = ctx.enter_context(nc.psum_tensor("pbig", [128, 4096], F32))
        pb = [pbig.ap()[:, 512 * i: 512 * (i + 1)] for i in range(8)]

        cst = sb("cst", [128, 8], F32)
        ident = sb("ident", [128, 128], F32)
        ident_bf = sb("ident_bf", [128, 128], BF16)
        sel8 = sb("sel8", [8, 1024], BF16)
        maskLE = sb("maskLE", [128, 128], BF16)
        maskGE = sb("maskGE", [128, 128], BF16)
        ones_row = sb("ones_row", [1, 512], BF16)
        ones_col = sb("ones_col", [128, 1], BF16)
        ones_row32 = sb("ones_row32", [1, 128], F32)
        ind2 = sb("ind2", [128, 2], BF16)
        ind2t = sb("ind2t", [2, 128], BF16)
        xroll = sb("xroll", [128, 2 * 8 * 516], F32)
        xrs = lambda s, ic: 516 * (8 * s + ic)
        negmax8 = sb("negmax8", [128, 8], F32)
        negtmp = sb("negtmp", [128, 2], F32)
        nm8 = sb("nm8", [8, 128], BF16)
        mrow = sb("mrow", [1, 512], F32)
        vrow = sb("vrow", [1, 512], F32)
        rstdr = sb("rstdr", [1, 512], F32)
        rrow = sb("rrow", [1, 512], F32)
        sspair = sb("sspair", [2, 512], F32)
        fpair = sb("fpair", [2, 512], BF16)
        Ab32 = sb("Ab32", [128, 512], F32)
        Bb32 = sb("Bb32", [128, 512], F32)
        rb32 = sb("rb32", [128, 512], F32)
        bqs = sb("bqs", [128, 8], F32)
        bkA = sb("bkA", [128, 8], F32)
        zbuf = sb("zbuf", [128, 8 * 513], F32)
        zc = lambda oc: 513 * oc + 1
        zsqs = sb("zsqs", [128, 2 * 512], BF16)
        zbfs = sb("zbfs", [128, 2 * 512], BF16)
        params = {}
        for l in range(2):
            for n in ["qs", "qssh", "ks", "kssh"]:
                params[(l, n)] = sb(f"p{l}{n}", [128, 1], F32)
            for n in ["lng", "lnb", "b2"]:
                params[(l, n)] = sb(f"p{l}{n}", [128, 8], F32)
            for n in ["b1a", "b1g"]:
                params[(l, n)] = sb(f"p{l}{n}", [128, NJ], F32)

        def init_pool(g):
            g.memset(ones_row[:, :], 1.0)
            g.memset(ones_col[:, :], 1.0)
            g.memset(ones_row32[:, :], 1.0)
            g.memset(ind2[:, :], 0.0)
            g.memset(ind2[0:64, 0:1], 1.0)
            g.memset(ind2[64:128, 1:2], 1.0)
            g.memset(cst[:, C_EIGHTH:C_EIGHTH + 1], 0.125)
            g.memset(cst[:, C_EPS:C_EPS + 1], 1e-5)
            g.memset(cst[:, C_TINY:C_TINY + 1], 1e-24)
            g.memset(cst[:, C_INV1024:C_INV1024 + 1], 1.0 / 1024.0)
            g.memset(cst[:, C_TWO:C_TWO + 1], 2.0)
            return g.memset(cst[:, C_LN8:C_LN8 + 1], 2.0794415416798357)
        st_ms = P.step("pool", init_pool, tag="init_memset")
        st_cst = P.dma("gq", [
            (ident_bf[:, :], E["identin"].ap()),
            (sel8[:, :], E["sel8in"].ap()),
            (ind2t[:, :], E["ind2tin"].ap()),
            (maskLE[:, :], E["maskLEin"].ap()),
            (maskGE[:, :], E["maskGEin"].ap()),
        ], tag="cast_consts")
        st_cf = P.dma("sq", [
            (ident[:, :], E["identin"].ap()),
            (bkA[:, :], E["A_bk"].ap()),
        ] + [(params[(l, n)][:, :], E[f"B{l}_{n}"].ap())
             for l in range(2)
             for n in ["qs", "qssh", "ks", "kssh", "lng", "lnb", "b2", "b1a", "b1g"]],
            tag="f32_consts")
        st_bq = P.dma("sq", [(bqs[:, :], E["A_bq"].ap())], tag="bq_load")
        st_bqs = P.step("dve", lambda e: e.tensor_scalar(
            bqs[:, :], bqs[:, :], cst[:, C_EIGHTH:C_EIGHTH + 1], None, ALU.mult),
            deps=[st_bq, st_ms], tag="bq_scale")
        P.wait_only("pool", [st_cst, st_cf, st_bq], tag="init_bar")
        P.flush()

        P.ln_ab = []
        P.ln_zr = []

        def ln_stats(dep8, zsrc):
            s1s = []
            csteps = []
            for oc in range(8):
                src = zsrc(oc)
                war = [csteps[oc - 2]] if oc >= 2 else [P.use_bank(6), P.use_bank(7)]
                s1 = P.step("act", lambda s, src=src, oc=oc: s.activation(
                    zsqs[:, 512 * (oc % 2): 512 * (oc % 2 + 1)], src, AF.Square),
                    deps=[dep8[oc]] + war, tag="lnsq")
                s1s.append(s1)
                s2 = P.step("dve", lambda e, src=src, oc=oc: e.tensor_copy(
                    zbfs[:, 512 * (oc % 2): 512 * (oc % 2 + 1)], src),
                    deps=[dep8[oc]] + war, tag="lnbf")

                def mmst(t, oc=oc):
                    t.matmul(pb[6][0:1, 0:512], ones_col[:, :],
                             zbfs[:, 512 * (oc % 2): 512 * (oc % 2 + 1)],
                             start=(oc == 0), stop=(oc == 7))
                    return t.matmul(pb[7][0:1, 0:512], ones_col[:, :],
                                    zsqs[:, 512 * (oc % 2): 512 * (oc % 2 + 1)],
                                    start=(oc == 0), stop=(oc == 7))
                csteps.append(P.step("pe", mmst, deps=[s1, s2], tag="lnmm"))

            def rows(e):
                e.tensor_scalar(mrow[:, :], pb[6][0:1, 0:512],
                                cst[0:1, C_INV1024:C_INV1024 + 1], None, ALU.mult)
                e.tensor_scalar(vrow[:, :], pb[7][0:1, 0:512],
                                cst[0:1, C_INV1024:C_INV1024 + 1], None, ALU.mult)
                e.tensor_mul(rstdr[:, :], mrow[:, :], mrow[:, :])
                return e.tensor_sub(vrow[:, :], vrow[:, :], rstdr[:, :])
            s4 = P.step("dve", rows, deps=[csteps[-1]], tag="lnrows")
            P.set_bank(6, s4)
            P.set_bank(7, s4)

            def rstd(s):
                s.activation(rstdr[:, :], vrow[:, :], AF.Ln,
                             bias=cst[0:1, C_EPS:C_EPS + 1])
                return s.activation(rstdr[:, :], rstdr[:, :], AF.Exp, scale=-0.5)
            s5 = P.step("act", rstd, deps=[s4], tag="lnrstd")

            def bc(t):
                t.matmul(pb[6][:, 0:512], ones_row32[:, :], rstdr[:, :],
                         start=True, stop=True)
                return t.matmul(pb[7][:, 0:512], ones_row32[:, :], mrow[:, :],
                                start=True, stop=True)
            s6 = P.step("pe", bc, deps=[s5, P.use_bank(6), P.use_bank(7)], tag="lnbc")

            def bce(s):
                s.copy(Ab32[:, :], pb[6][:, 0:512])
                return s.copy(Bb32[:, :], pb[7][:, 0:512])
            s7 = P.step("act", bce, deps=[s6] + P.ln_ab, tag="lnbce")
            P.set_bank(6, s7)
            P.set_bank(7, s7)
            return s7, s1s

        GA = ExitStack()
        with GA:
            asb = lambda name, shape, dt: GA.enter_context(
                nc.sbuf_tensor(name, shape, dt))
            wA = {n: asb(f"wA_{n}", [128, 8 * 1024], BF16) for n in "qkvo"}
            xbfA = asb("xbfA", [128, 2 * 8 * 516], BF16)
            xbs = lambda s, ic: 516 * (8 * s + ic)
            qA = asb("qA", [128, 8 * 512], BF16)
            kA = asb("kA", [128, 8 * 512], BF16)
            vA = asb("vA", [128, 4 * 1024], BF16)
            o_fmA = asb("o_fmA", [128, 8 * 512], BF16)
            PTA = asb("PTA", [128, 512], BF16)
            nm2 = asb("nm2", [2, 128], BF16)
            bvrow = asb("bvrow", [1, D], BF16)
            borow = asb("borow", [1, D], BF16)

            st_brows = P.dma("gq", [(bvrow[:, :], E["A_bvrow"].ap()),
                                    (borow[:, :], E["A_borow"].ap())], tag="brows")
            P.wait_only("pool", [st_brows], tag="wA_bar")
            P.flush()
            wsteps = {}
            for n, src in [("q", "A_wqT"), ("k", "A_wkT"), ("v", "A_wvT"),
                           ("o", "A_woT")]:
                sap = bass.AP(E[src], 0, [[1024, 128], [128 * 1024, 8], [1, 1024]])
                dap = wA[n].ap().rearrange("p (ic n) -> p ic n", ic=8)
                wsteps[n] = P.dma("gq", [(dap, sap)], tag=f"wA_{n}")

            prev_spill = None
            a3_last = [None] * NTB
            pre_xbf = {}
            pre_x32 = {}
            a1pe_last = [None] * NTB
            a2pe_last = [None] * NTB
            pta_reader = None
            rb_reader = None

            def mk_xbfA(tb, deps=()):
                sl = tb % 2
                if tb == 0:
                    st_g = P.step("pool", lambda g: g.memset(
                        bass.AP(xbfA, xbs(0, 0),
                                [[2 * 8 * 516, 128], [516, 8], [1, 1]]), 0.0),
                        tag="xbf_guard")
                    return P.dma("gq", [(
                        bass.AP(xbfA, xbs(sl, 0) + 1,
                                [[2 * 8 * 516, 128], [516, 8], [1, 512]]),
                        bass.AP(E["xT"], 0, [[T, 128], [128 * T, 8], [1, 512]]))],
                        deps=[st_g], tag="xbfA0")
                return P.dma("gq", [(
                    bass.AP(xbfA, xbs(sl, 0),
                            [[2 * 8 * 516, 128], [516, 8], [1, 513]]),
                    bass.AP(E["xT"], TB * tb - 1,
                            [[T, 128], [128 * T, 8], [1, 513]]))],
                    deps=list(deps), tag=f"xbfA{tb}")

            def mk_x32A(tb, deps=()):
                sl = tb % 2
                if tb == 0:
                    st_gx = P.step("pool", lambda g: g.memset(
                        bass.AP(xroll, xrs(0, 0),
                                [[2 * 8 * 516, 128], [516, 8], [1, 1]]), 0.0),
                        tag="xroll_guard")
                    return P.dma("sq", [(
                        bass.AP(xroll, xrs(sl, 0) + 1,
                                [[2 * 8 * 516, 128], [516, 8], [1, 512]]),
                        bass.AP(E["xT"], 0, [[T, 128], [128 * T, 8], [1, 512]]))],
                        deps=[st_gx] + list(deps), tag="x32_0")
                return P.dma("sq", [(
                    bass.AP(xroll, xrs(sl, 0),
                            [[2 * 8 * 516, 128], [516, 8], [1, 513]]),
                    bass.AP(E["xT"], TB * tb - 1,
                            [[T, 128], [128 * T, 8], [1, 513]]))],
                    deps=deps, tag=f"x32_{tb}")

            for tb in range(NTB):
                sl = tb % 2
                st_xbf = pre_xbf.pop(tb) if tb in pre_xbf else mk_xbfA(tb)
                st_x32 = pre_x32.pop(tb) if tb in pre_x32 else mk_x32A(tb)
                if tb + 1 < NTB:
                    pre_xbf[tb + 1] = mk_xbfA(
                        tb + 1, deps=[a1pe_last[tb - 1]] if tb >= 1 else [])
                    pre_x32[tb + 1] = mk_x32A(
                        tb + 1, deps=[a3_last[tb - 1]] if tb >= 1 else [])

                def xb_data(ic, sl=sl):
                    return xbfA[:, xbs(sl, ic) + 1: xbs(sl, ic) + 513]

                last_ev = None
                for which, wt, dst, brow, scale, wst in [
                        ("q", wA["q"], qA, bqs, 0.125, wsteps["q"]),
                        ("k", wA["k"], kA, bkA, 1.0, wsteps["k"])]:
                    for oc in range(8):
                        b = oc % 4

                        def mm_qk(t, wt=wt, oc=oc, b=b):
                            last = None
                            for ic in range(8):
                                last = t.matmul(
                                    pb[b][:, 0:512],
                                    wt[:, 1024 * ic + 128 * oc:
                                       1024 * ic + 128 * (oc + 1)],
                                    xb_data(ic), start=(ic == 0), stop=(ic == 7))
                            return last
                        stm = P.step("pe", mm_qk,
                                     deps=[st_xbf, wst, P.use_bank(b)],
                                     tag=f"A1{which}{oc}")

                        def ev(s, dst=dst, oc=oc, b=b, brow=brow, scale=scale):
                            return s.activation(
                                dst[:, 512 * oc: 512 * (oc + 1)],
                                pb[b][:, 0:512], AF.Identity,
                                bias=brow[:, oc:oc + 1], scale=scale)
                        ste = P.step("act", ev,
                                     deps=[stm, st_cf, st_bqs,
                                           a2pe_last[tb - 1] if tb >= 1 else None],
                                     tag=f"A1{which}e{oc}")
                        P.set_bank(b, ste)
                        last_ev = ste
                for tc in range(4):
                    for half in range(2):
                        b = 4 + (2 * tc + half) % 4

                        def mm_v(t, tc=tc, half=half, b=b, sl=sl):
                            for ic in range(8):
                                t.matmul(
                                    pb[b][:, 0:512],
                                    xbfA[:, xbs(sl, ic) + 1 + 128 * tc:
                                         xbs(sl, ic) + 1 + 128 * (tc + 1)],
                                    wA["v"][:, 1024 * ic + 512 * half:
                                            1024 * ic + 512 * (half + 1)],
                                    start=(ic == 0), stop=False)
                            return t.matmul(
                                pb[b][:, 0:512], ones_row[0:1, 0:128],
                                bvrow[0:1, 512 * half: 512 * (half + 1)],
                                start=False, stop=True)
                        stm = P.step("pe", mm_v,
                                     deps=[st_xbf, wsteps["v"], st_brows, st_ms,
                                           P.use_bank(b)], tag=f"A1v{tc}{half}")

                        def ev(s, tc=tc, half=half, b=b):
                            return s.copy(vA[:, 1024 * tc + 512 * half:
                                             1024 * tc + 512 * (half + 1)],
                                          pb[b][:, 0:512])
                        ste = P.step("act", ev,
                                     deps=[stm,
                                           a2pe_last[tb - 1] if tb >= 1 else None],
                                     tag=f"A1ve{tc}{half}")
                        P.set_bank(b, ste)
                        last_ev = ste
                        a1pe_last[tb] = stm

                for wi in range(2):
                    nmsteps = []
                    for j in range(2):
                        qoff = 256 * wi + 128 * j
                        bS = 4

                        def mm_s(t, wi=wi, j=j, qoff=qoff, bS=bS):
                            span = 128 * (j + 1)
                            for ic in range(8):
                                t.matmul(
                                    pb[bS][:, 0:span],
                                    qA[:, 512 * ic + qoff: 512 * ic + qoff + 128],
                                    kA[:, 512 * ic + 256 * wi:
                                       512 * ic + 256 * wi + span],
                                    start=(ic == 0), stop=False)
                            return t.matmul(
                                pb[bS][:, 128 * j: 128 * (j + 1)],
                                ident_bf[:, :], maskLE[:, :],
                                start=False, stop=True)
                        stm = P.step("pe", mm_s,
                                     deps=[last_ev, st_cst, P.use_bank(bS)],
                                     tag=f"A2s{wi}{j}")

                        def red(e, j=j, wi=wi, bS=bS):
                            return e.reduce_max(
                                negmax8[:, 2 * wi + j: 2 * wi + j + 1],
                                pb[bS][:, 0:128 * (j + 1)], axis=AX.X, negate=True)
                        str_ = P.step("dve", red, deps=[stm], tag=f"A2r{wi}{j}")
                        P.set_bank(bS, str_)
                        nmsteps.append(str_)

                    def tp(t, wi=wi):
                        return t.transpose(
                            bass.AP(pbig, 512 * 5, [[4096, 2], [1, 128]]),
                            negmax8[:, 2 * wi: 2 * wi + 2], ident[:, :])
                    sttp = P.step("pe", tp, deps=nmsteps + [st_cf, P.use_bank(5)],
                                  tag=f"A2tp{wi}")
                    stnm = P.step("act", lambda s: s.copy(
                        nm2[:, :], bass.AP(pbig, 512 * 5, [[4096, 2], [1, 128]])),
                        deps=[sttp], tag=f"A2nm{wi}")
                    P.set_bank(5, stnm)

                    for j in range(2):
                        qoff = 256 * wi + 128 * j
                        bT = 6

                        def mm_st(t, wi=wi, j=j, qoff=qoff, bT=bT):
                            for kc in range(j + 1):
                                for ic in range(8):
                                    t.matmul(
                                        pb[bT][:, 128 * kc: 128 * (kc + 1)],
                                        kA[:, 512 * ic + 256 * wi + 128 * kc:
                                           512 * ic + 256 * wi + 128 * (kc + 1)],
                                        qA[:, 512 * ic + qoff: 512 * ic + qoff + 128],
                                        start=(ic == 0 and kc == 0), stop=False)
                            t.matmul(
                                pb[bT][:, 128 * j: 128 * (j + 1)],
                                ident_bf[:, :], maskGE[:, :],
                                start=False, stop=False)
                            nrep = bass.AP(nm2, 0, [[128, 2], [0, j + 1], [1, 128]])
                            return t.matmul(
                                pb[bT][:, 0:128 * (j + 1)],
                                sel8[0:2, 128 * j: 128 * (j + 1)],
                                nrep, start=False, stop=True)
                        stm = P.step("pe", mm_st, deps=[stnm, P.use_bank(bT)],
                                     tag=f"A2t{wi}{j}")
                        stexp = P.step("act", lambda s, j=j, bT=bT: s.activation(
                            PTA[:, 0:128 * (j + 1)], pb[bT][:, 0:128 * (j + 1)],
                            AF.Exp), deps=[stm, pta_reader], tag=f"A2e{wi}{j}")
                        P.set_bank(bT, stexp)
                        bU = 7

                        def mm_u(t, j=j, bU=bU):
                            last = None
                            for kc in range(j + 1):
                                last = t.matmul(
                                    pb[bU][0:1, 0:128], ones_col[:, :],
                                    PTA[:, 128 * kc: 128 * (kc + 1)],
                                    start=(kc == 0), stop=(kc == j))
                            return last
                        stu = P.step("pe", mm_u, deps=[stexp, P.use_bank(bU)],
                                     tag=f"A2u{wi}{j}")
                        strc = P.step("dve", lambda e, bU=bU: e.reciprocal(
                            rrow[0:1, 0:128], pb[bU][0:1, 0:128]), deps=[stu],
                            tag=f"A2rc{wi}{j}")
                        P.set_bank(bU, strc)
                        bR = 5
                        strb = P.step("pe", lambda t, bR=bR: t.matmul(
                            pb[bR][:, 0:128], ones_row32[:, :], rrow[0:1, 0:128],
                            start=True, stop=True),
                            deps=[strc, P.use_bank(bR)], tag=f"A2rb{wi}{j}")
                        strbe = P.step("act", lambda s, bR=bR: s.copy(
                            rb32[:, 0:128], pb[bR][:, 0:128]), deps=[strb, rb_reader],
                            tag=f"A2rbe{wi}{j}")
                        P.set_bank(bR, strbe)

                        def mm_pv(t, wi=wi, j=j):
                            last = None
                            for oc in range(8):
                                bV = oc // 4
                                for kc in range(j + 1):
                                    tc = 2 * wi + kc
                                    last = t.matmul(
                                        pb[bV][:, 128 * (oc % 4):
                                               128 * (oc % 4 + 1)],
                                        vA[:, 1024 * tc + 128 * oc:
                                           1024 * tc + 128 * (oc + 1)],
                                        PTA[:, 128 * kc: 128 * (kc + 1)],
                                        start=(kc == 0), stop=(kc == j))
                            return last
                        stpv = P.step("pe", mm_pv,
                                      deps=[stexp, P.use_bank(0), P.use_bank(1)],
                                      tag=f"A2pv{wi}{j}")

                        def ev_o(e, wi=wi, j=j, qoff=qoff):
                            last = None
                            for oc in range(8):
                                bV = oc // 4
                                last = e.tensor_tensor(
                                    o_fmA[:, 512 * oc + qoff:
                                          512 * oc + qoff + 128],
                                    pb[bV][:, 128 * (oc % 4): 128 * (oc % 4 + 1)],
                                    rb32[:, 0:128], ALU.mult)
                            return last
                        stoe = P.step("dve", ev_o, deps=[stpv, strbe],
                                      tag=f"A2oe{wi}{j}")
                        P.set_bank(0, stoe)
                        P.set_bank(1, stoe)
                        pta_reader = stpv
                        rb_reader = stoe
                        a2pe_last[tb] = stpv
                    if wi == 1:
                        P.flush()

                zst = []
                for oc in range(8):
                    b = 4 + oc % 2

                    def mm_wo(t, oc=oc, b=b):
                        for ic in range(8):
                            t.matmul(
                                pb[b][:, 0:512],
                                wA["o"][:, 1024 * ic + 128 * oc:
                                        1024 * ic + 128 * (oc + 1)],
                                o_fmA[:, 512 * ic: 512 * (ic + 1)],
                                start=(ic == 0), stop=False)
                        return t.matmul(
                            pb[b][:, 0:512],
                            borow[0:1, 128 * oc: 128 * (oc + 1)],
                            ones_row[0:1, 0:512], start=False, stop=True)
                    stm = P.step("pe", mm_wo,
                                 deps=[st_brows, wsteps["o"], rb_reader,
                                       P.use_bank(b)],
                                 tag=f"A3wo{oc}")

                    def ev(e, oc=oc, b=b, sl=sl):
                        return e.scalar_tensor_tensor(
                            zbuf[:, zc(oc): zc(oc) + 512],
                            xroll[:, xrs(sl, oc) + 1: xrs(sl, oc) + 513],
                            cst[:, C_TWO:C_TWO + 1],
                            pb[b][:, 0:512], ALU.mult, ALU.add)
                    ste = P.step("dve", ev,
                                 deps=[stm, st_x32, prev_spill] + P.ln_zr,
                                 tag=f"A3z{oc}")
                    P.set_bank(b, ste)
                    zst.append(ste)
                a3_last[tb] = zst[-1]

                s7, s1s = ln_stats(zst, lambda oc: zbuf[:, zc(oc): zc(oc) + 512])
                appst = []
                for oc in range(8):
                    src = zbuf[:, zc(oc): zc(oc) + 512]

                    def app(g, src=src):
                        g.tensor_sub(src, src, Bb32[:, :])
                        return g.tensor_mul(src, src, Ab32[:, :])
                    appst.append(P.step("pool", app, deps=[s7], tag="lnappA"))
                P.ln_ab = list(appst)
                P.ln_zr = s1s + appst
                tr = [(dram3(xa, TB * tb),
                       bass.AP(zbuf, 1, [[8 * 513, 128], [513, 8], [1, 512]]))]
                if upto == "A":
                    tr.append((dram3(out, TB * tb),
                               bass.AP(zbuf, 1, [[8 * 513, 128], [513, 8], [1, 512]])))
                prev_spill = P.dma("sq", tr, deps=appst, tag=f"y0spill{tb}")

            P.wait_only("pool", [prev_spill], tag="drainA")
            P.flush()

        if upto == "A":
            with nc.Block() as block:
                @block.sync
                def _(s):
                    s.wait_ge(P.sem["sq"], P.cnt["sq"])
                    s.wait_ge(P.sem["gq"], P.cnt["gq"])
            return nc

        def emit_layer(l, sin, sout, want_out):
            pqn, pqsh = params[(l, "qs")], params[(l, "qssh")]
            pkn, pksh = params[(l, "ks")], params[(l, "kssh")]
            L1 = ExitStack()
            x1_spills = [None] * NTB
            y_spills = [None] * NTB
            with L1:
                lsb = lambda name, shape, dt: L1.enter_context(
                    nc.sbuf_tensor(name, shape, dt))
                tabs = {n: lsb(f"tab{n}{l}", [128, T], BF16)
                        for n in ["cq", "sq", "ck", "sk"]}
                wqkvB = lsb(f"wqkv{l}", [128, 8 * 1536], BF16)
                wshB = lsb(f"wsh{l}", [128, 8 * 1024], BF16)
                woB = lsb(f"wo{l}", [128, 4 * 1024], BF16)
                xbfB = lsb(f"xbf{l}", [128, 2 * 8 * 516], BF16)
                xbs2 = lambda s, ic: 516 * (8 * s + ic)
                q_rot = lsb(f"qrot{l}", [128, 4 * 512], BF16)
                k_rot = lsb(f"krot{l}", [128, 2 * 4 * 512], BF16)
                v_aug = lsb(f"vaug{l}", [128, 2 * 4 * 520], BF16)
                o_fm = lsb(f"ofm{l}", [128, 4 * 512], BF16)
                PT = lsb(f"PT{l}", [128, 2 * 640], BF16)
                ybuf = lsb(f"ybuf{l}", [128, 8 * 512], BF16)
                ybs = lambda s, ic: 512 * ic
                qsqb = lsb(f"qsq{l}", [128, 2 * 512], BF16)
                etmp = lsb(f"etmp{l}", [128, 512], F32)
                e1b = lsb(f"e1{l}", [128, 2 * 512], F32)
                e2b = lsb(f"e2{l}", [128, 2 * 512], F32)

                st_tab = P.dma("gq", [(tabs[n][:, :], E["T" + n].ap())
                                      for n in ["cq", "sq", "ck", "sk"]],
                               tag=f"tabs{l}")
                st_wqkv = P.dma("gq", [(
                    wqkvB.ap().rearrange("p (ic n) -> p ic n", ic=8),
                    bass.AP(E[f"B{l}_wqkv"], 0,
                            [[1536, 128], [128 * 1536, 8], [1, 1536]]))],
                    tag=f"wqkv{l}")
                st_wsh = P.dma("gq", [(
                    wshB.ap().rearrange("p (ic n) -> p ic n", ic=8),
                    bass.AP(E[f"B{l}_wsh"], 0,
                            [[1024, 128], [128 * 1024, 8], [1, 1024]]))],
                    tag=f"wsh{l}")
                st_wo = P.dma("gq", [(
                    woB.ap().rearrange("p (ic n) -> p ic n", ic=4),
                    bass.AP(E[f"B{l}_wo"], 0,
                            [[1024, 128], [128 * 1024, 4], [1, 1024]]))],
                    tag=f"woL{l}")
                st_vone = P.step("pool", lambda g: g.memset(
                    bass.AP(v_aug, 64, [[2 * 4 * 520, 128], [65, 64], [1, 1]]),
                    1.0), tag=f"vone{l}")

                guard_prev = None
                ptw = {0: None, 1: None}
                rbw = None
                rrow_w = None
                prev_fb = None
                qk_last = None
                b3_last = [None] * NTB
                pre_xbf = {}
                pre_x32 = {}
                b1pe_last = [None] * NTB
                b2pe_last = [None] * NTB
                prev_yapp3 = None

                def mk_xbfB(tb, deps=()):
                    sl = tb % 2
                    if tb == 0:
                        stg = P.step("pool", lambda g: g.memset(
                            bass.AP(xbfB, xbs2(0, 0),
                                    [[2 * 8 * 516, 128], [516, 8], [1, 1]]), 0.0),
                            tag=f"xbfg{l}")
                        return P.dma("gq", [(
                            bass.AP(xbfB, xbs2(sl, 0) + 1,
                                    [[2 * 8 * 516, 128], [516, 8], [1, 512]]),
                            dram3(sin, 0))], deps=[stg], tag=f"xbfB{l}_0")
                    return P.dma("gq", [(
                        bass.AP(xbfB, xbs2(sl, 0),
                                [[2 * 8 * 516, 128], [516, 8], [1, 513]]),
                        dram3(sin, TB * tb - 1, 513))], deps=list(deps),
                        tag=f"xbfB{l}_{tb}")

                def mk_x32B(tb, deps=()):
                    sl = tb % 2
                    if tb == 0:
                        stgx = P.step("pool", lambda g: g.memset(
                            bass.AP(xroll, xrs(0, 0),
                                    [[2 * 8 * 516, 128], [516, 8], [1, 1]]), 0.0),
                            tag=f"xg{l}")
                        return P.dma("sq", [(
                            bass.AP(xroll, xrs(sl, 0) + 1,
                                    [[2 * 8 * 516, 128], [516, 8], [1, 512]]),
                            dram3(sin, 0))], deps=[stgx] + list(deps),
                            tag=f"x32B{l}_{tb}")
                    return P.dma("sq", [(
                        bass.AP(xroll, xrs(sl, 0),
                                [[2 * 8 * 516, 128], [516, 8], [1, 513]]),
                        dram3(sin, TB * tb - 1, 513))], deps=deps,
                        tag=f"x32B{l}_{tb}")

                for tb in range(NTB):
                    sl = tb % 2
                    w = tb
                    st_xbf = pre_xbf.pop(tb) if tb in pre_xbf else mk_xbfB(tb)
                    st_x32 = pre_x32.pop(tb) if tb in pre_x32 else mk_x32B(tb)
                    if tb + 1 < NTB:
                        pre_xbf[tb + 1] = mk_xbfB(
                            tb + 1, deps=[b1pe_last[tb - 1]] if tb >= 1 else [])
                        pre_x32[tb + 1] = mk_x32B(
                            tb + 1, deps=[b3_last[tb - 1]] if tb >= 1 else [])

                    def hsrc(ic, t0, width, sl=sl):
                        off = 1 if ic < 4 else 0
                        return xbfB[:, xbs2(sl, ic) + off + t0:
                                    xbs2(sl, ic) + off + t0 + width]

                    for cx in range(8):
                        isq = cx < 4
                        c = cx % 4
                        colbase = (0 if isq else 512) + 128 * c
                        bq_, bsh_, bss_, bfb_ = [(0, 1, 2, 3), (4, 5, 2, 3)][cx % 2]
                        Tc = tabs["cq"] if isq else tabs["ck"]
                        Ts = tabs["sq"] if isq else tabs["sk"]
                        ppn = pqn if isq else pkn
                        ppsh = pqsh if isq else pksh
                        dstt = q_rot if isq else k_rot
                        dstoff = 512 * c if isq else 512 * (4 * sl + c)

                        def mm_q(t, colbase=colbase, bq_=bq_):
                            last = None
                            for ic in range(8):
                                last = t.matmul(
                                    pb[bq_][:, 0:512],
                                    wqkvB[:, 1536 * ic + colbase:
                                          1536 * ic + colbase + 128],
                                    hsrc(ic, 0, 512),
                                    start=(ic == 0), stop=(ic == 7))
                            return last
                        smq = P.step("pe", mm_q,
                                     deps=[st_xbf, st_wqkv, P.use_bank(bq_)],
                                     tag=f"B1q{l}{tb}{cx}")

                        def mm_sh(t, colbase=colbase, bsh_=bsh_):
                            last = None
                            for ic in range(8):
                                last = t.matmul(
                                    pb[bsh_][:, 0:512],
                                    wshB[:, 1024 * ic + colbase:
                                         1024 * ic + colbase + 128],
                                    hsrc(ic, 0, 512),
                                    start=(ic == 0), stop=(ic == 7))
                            return last
                        smsh = P.step("pe", mm_sh,
                                      deps=[st_xbf, st_wsh, P.use_bank(bsh_)],
                                      tag=f"B1s{l}{tb}{cx}")
                        ssq = P.step("act", lambda s, bq_=bq_, cx=cx: s.activation(
                            qsqb[:, 512 * (cx % 2): 512 * (cx % 2 + 1)],
                            pb[bq_][:, 0:512], AF.Square),
                            deps=[smq], tag=f"B1sq{l}{tb}{cx}")
                        sss = P.step("pe", lambda t, bss_=bss_, cx=cx: t.matmul(
                            pb[bss_][0:2, 0:512], ind2[:, :],
                            qsqb[:, 512 * (cx % 2): 512 * (cx % 2 + 1)],
                            start=True, stop=True),
                            deps=[ssq, st_ms, P.use_bank(bss_)],
                            tag=f"B1ss{l}{tb}{cx}")

                        def fca(s, bss_=bss_, isq=isq):
                            s.activation(sspair[:, :], pb[bss_][0:2, 0:512],
                                         AF.Ln, bias=cst[0:2, C_TINY:C_TINY + 1])
                            return s.activation(
                                fpair[:, :], sspair[:, :], AF.Exp, scale=-0.5,
                                bias=(cst[0:2, C_LN8:C_LN8 + 1] if isq else 0.0))
                        sfc = P.step("act", fca, deps=[sss, prev_fb],
                                     tag=f"B1f{l}{tb}{cx}")
                        P.set_bank(bss_, sfc)
                        sfb = P.step("pe", lambda t, bfb_=bfb_: t.matmul(
                            pb[bfb_][:, 0:512], ind2t[:, :], fpair[:, :],
                            start=True, stop=True),
                            deps=[sfc, P.use_bank(bfb_)], tag=f"B1fb{l}{tb}{cx}")
                        prev_fb = sfb
                        se1 = P.step("dve", lambda e, bq_=bq_, cx=cx, Tc=Tc,
                                     ppn=ppn, tb=tb: e.scalar_tensor_tensor(
                                         e1b[:, 512 * (cx % 2): 512 * (cx % 2 + 1)],
                                         pb[bq_][:, 0:512], ppn[:, :],
                                         Tc[:, TB * tb: TB * (tb + 1)],
                                         ALU.mult, ALU.mult),
                                     deps=[smq, ssq, st_tab, st_cf],
                                     tag=f"B1e1{l}{tb}{cx}")
                        P.set_bank(bq_, [ssq, se1])
                        se2 = P.step("dve", lambda e, bsh_=bsh_, cx=cx, Ts=Ts,
                                     ppsh=ppsh, tb=tb: e.scalar_tensor_tensor(
                                         e2b[:, 512 * (cx % 2): 512 * (cx % 2 + 1)],
                                         pb[bsh_][:, 0:512], ppsh[:, :],
                                         Ts[:, TB * tb: TB * (tb + 1)],
                                         ALU.mult, ALU.mult),
                                     deps=[smsh, st_tab, st_cf],
                                     tag=f"B1e2{l}{tb}{cx}")
                        P.set_bank(bsh_, se2)
                        sadd = P.step("pool", lambda g, cx=cx: g.tensor_add(
                            e1b[:, 512 * (cx % 2): 512 * (cx % 2 + 1)],
                            e1b[:, 512 * (cx % 2): 512 * (cx % 2 + 1)],
                            e2b[:, 512 * (cx % 2): 512 * (cx % 2 + 1)]),
                            deps=[se1, se2], tag=f"B1a{l}{tb}{cx}")
                        sfm = P.step("dve", lambda e, cx=cx, dstt=dstt,
                                     dstoff=dstoff, bfb_=bfb_: e.tensor_tensor(
                                         dstt[:, dstoff: dstoff + 512],
                                         e1b[:, 512 * (cx % 2): 512 * (cx % 2 + 1)],
                                         pb[bfb_][:, 0:512], ALU.mult),
                                     deps=[sadd, sfb,
                                           b2pe_last[tb - 1] if tb >= 1 else None],
                                     tag=f"B1fm{l}{tb}{cx}")
                        P.set_bank(bfb_, sfm)
                        qk_last = sfm
                    last_vev = None
                    for tc in range(4):
                        b = (0, 1, 4, 5)[tc]

                        def mm_v(t, tc=tc, b=b):
                            last = None
                            for ic in range(8):
                                last = t.matmul(
                                    pb[b][:, 0:512],
                                    hsrc(ic, 128 * tc, 128),
                                    wqkvB[:, 1536 * ic + 1024: 1536 * ic + 1536],
                                    start=(ic == 0), stop=(ic == 7))
                            return last
                        smv = P.step("pe", mm_v,
                                     deps=[st_xbf, st_wqkv, P.use_bank(b)],
                                     tag=f"B1v{l}{tb}{tc}")
                        vdst = bass.AP(v_aug, 520 * (4 * sl + tc),
                                       [[2 * 4 * 520, 128], [65, 8], [1, 64]])
                        psrc = bass.AP(pbig, 512 * b, [[4096, 128], [64, 8], [1, 64]])
                        sev = P.step("act", lambda s, vdst=vdst, psrc=psrc: s.copy(
                            vdst, psrc),
                            deps=[smv, st_vone,
                                  b2pe_last[tb - 1] if tb >= 1 else None],
                            tag=f"B1ve{l}{tb}{tc}")
                        P.set_bank(b, sev)
                        last_vev = sev
                        b1pe_last[tb] = smv

                    for qt in range(4):
                        redst = []
                        for hp in range(4):
                            lbspan = 512 - 128 * qt
                            curspan = 128 * (qt + 1)
                            hc = hp

                            def mm_stat2(t, hc=hc, qt=qt, lbspan=lbspan,
                                         curspan=curspan, sl=sl, w=w):
                                qs2 = [q_rot[64 * hh:64 * hh + 64,
                                             512 * hc + 128 * qt:
                                             512 * hc + 128 * (qt + 1)]
                                       for hh in range(2)]
                                if w > 0:
                                    kof = 512 * (4 * ((w - 1) % 2) + hc) + 128 * qt
                                    for hh in range(2):
                                        t.matmul(pb[hh][:, 0:lbspan], qs2[hh],
                                                 k_rot[64 * hh:64 * hh + 64,
                                                       kof: kof + lbspan],
                                                 start=True, stop=False)
                                kof = 512 * (4 * sl + hc)
                                for hh in range(2):
                                    t.matmul(pb[2 + hh][:, 0:curspan], qs2[hh],
                                             k_rot[64 * hh:64 * hh + 64,
                                                   kof: kof + curspan],
                                             start=True, stop=False)
                                if w > 0:
                                    for hh in range(2):
                                        t.matmul(pb[hh][:, 0:128], ident_bf[:, :],
                                                 maskGE[:, :], start=False,
                                                 stop=True)
                                last = None
                                for hh in range(2):
                                    last = t.matmul(
                                        pb[2 + hh][:, 128 * qt: 128 * (qt + 1)],
                                        ident_bf[:, :], maskLE[:, :],
                                        start=False, stop=True)
                                return last
                            sstat = P.step("pe", mm_stat2,
                                           deps=[last_vev, qk_last, st_cst,
                                                 P.use_bank(0), P.use_bank(1),
                                                 P.use_bank(2), P.use_bank(3)],
                                           tag=f"B2st{l}{tb}{qt}{hp}")

                            def red2(e, hp=hp, w=w, lbspan=lbspan,
                                     curspan=curspan):
                                last = None
                                for hh in range(2):
                                    h = 2 * hp + hh
                                    if w > 0:
                                        e.reduce_max(negtmp[:, 0:1],
                                                     pb[hh][:, 0:lbspan],
                                                     axis=AX.X, negate=True)
                                        e.reduce_max(negtmp[:, 1:2],
                                                     pb[2 + hh][:, 0:curspan],
                                                     axis=AX.X, negate=True)
                                        last = e.tensor_tensor(
                                            negmax8[:, h:h + 1], negtmp[:, 0:1],
                                            negtmp[:, 1:2], ALU.min)
                                    else:
                                        last = e.reduce_max(
                                            negmax8[:, h:h + 1],
                                            pb[2 + hh][:, 0:curspan],
                                            axis=AX.X, negate=True)
                                return last
                            sred = P.step("dve", red2, deps=[sstat],
                                          tag=f"B2r{l}{tb}{qt}{hp}")
                            for bb in range(4):
                                P.set_bank(bb, sred)
                            redst.append(sred)
                        sttp = P.step("pe", lambda t: t.transpose(
                            bass.AP(pbig, 512 * 7, [[4096, 8], [1, 128]]),
                            negmax8[:, :], ident[:, :]),
                            deps=redst + [P.use_bank(7)], tag=f"B2tp{l}{tb}{qt}")
                        stnm8 = P.step("act", lambda s: s.copy(
                            nm8[:, :], bass.AP(pbig, 512 * 7, [[4096, 8], [1, 128]])),
                            deps=[sttp], tag=f"B2nm{l}{tb}{qt}")
                        P.set_bank(7, stnm8)

                        allpv = []
                        for hp in range(4):
                            bV = 4 + hp % 2
                            pvst = []
                            chunks = []
                            if w > 0:
                                for kc in range(qt, 4):
                                    chunks.append(((w - 1) % 2, kc,
                                                   maskLE if kc == qt else None))
                            for kc in range(0, qt + 1):
                                chunks.append((sl, kc,
                                               maskGE if kc == qt else None))
                            nch = len(chunks)
                            hcp = hp

                            def mm_stq2(t, hcp=hcp, hp=hp, chunks=chunks, qt=qt):
                                last = None
                                nch = len(chunks)
                                n1 = min(nch, 4)
                                qs2 = [q_rot[64 * hh:64 * hh + 64,
                                             512 * hcp + 128 * qt:
                                             512 * hcp + 128 * (qt + 1)]
                                       for hh in range(2)]
                                for pos, (par, kc, msk) in enumerate(chunks):
                                    fo = 128 * (pos % 4)
                                    kof0 = 512 * (4 * par + hcp) + 128 * kc
                                    for hh in range(2):
                                        bT = 2 * hh if pos < 4 else 2 * hh + 1
                                        t.matmul(pb[bT][:, fo: fo + 128],
                                                 k_rot[64 * hh:64 * hh + 64,
                                                       kof0: kof0 + 128],
                                                 qs2[hh], start=(pos in (0, 4)),
                                                 stop=False)
                                for pos, (par, kc, msk) in enumerate(chunks):
                                    if msk is None:
                                        continue
                                    fo = 128 * (pos % 4)
                                    for hh in range(2):
                                        bT = 2 * hh if pos < 4 else 2 * hh + 1
                                        t.matmul(pb[bT][:, fo: fo + 128],
                                                 ident_bf[:, :], msk[:, :],
                                                 start=False, stop=False)
                                nrep1 = bass.AP(nm8, 0,
                                                [[128, 8], [0, n1], [1, 128]])
                                for hh in range(2):
                                    h = 2 * hp + hh
                                    last = t.matmul(
                                        pb[2 * hh][:, 0:128 * n1],
                                        sel8[:, 128 * h: 128 * (h + 1)],
                                        nrep1, start=False, stop=True)
                                if nch > 4:
                                    nrep2 = bass.AP(nm8, 0,
                                                    [[128, 8], [0, nch - 4],
                                                     [1, 128]])
                                    for hh in range(2):
                                        h = 2 * hp + hh
                                        last = t.matmul(
                                            pb[2 * hh + 1][:, 0:128 * (nch - 4)],
                                            sel8[:, 128 * h: 128 * (h + 1)],
                                            nrep2, start=False, stop=True)
                                return last
                            smTp = P.step("pe", mm_stq2,
                                          deps=[stnm8, P.use_bank(0),
                                                P.use_bank(1), P.use_bank(2),
                                                P.use_bank(3)],
                                          tag=f"B2t{l}{tb}{qt}{hp}")
                            for hh in range(2):
                                h = 2 * hp + hh
                                hc = h // 2
                                hr = 64 * hh
                                bT1 = 2 * hh
                                bT2 = 2 * hh + 1
                                smT = smTp

                                def expf(s, h=h, hh=hh, nch=nch, bT1=bT1):
                                    po = 640 * hh
                                    return s.activation(
                                        PT[:, po: po + 128 * nch],
                                        pbig.ap()[:, 512 * bT1:
                                                  512 * bT1 + 128 * nch], AF.Exp)
                                sexp = P.step("act", expf, deps=[smT, ptw[hh]],
                                              tag=f"B2e{l}{tb}{qt}{h}")
                                P.set_bank(bT1, sexp)
                                P.set_bank(bT2, sexp)

                                def mm_pv(t, h=h, hh=hh, chunks=chunks, bV=bV):
                                    last = None
                                    po = 640 * hh
                                    for pos, (par, kc, msk) in enumerate(chunks):
                                        vof = 520 * (4 * par + kc) + 65 * h
                                        fo = po + 128 * pos
                                        last = t.matmul(
                                            pb[bV][0:65, 128 * hh: 128 * hh + 128],
                                            v_aug[:, vof: vof + 65],
                                            PT[:, fo: fo + 128],
                                            start=(pos == 0),
                                            stop=(pos == len(chunks) - 1))
                                    return last
                                dpv = [sexp] + (P.use_bank(bV) if hh == 0 else [])
                                spv = P.step("pe", mm_pv, deps=dpv,
                                             tag=f"B2pv{l}{tb}{qt}{h}")
                                ptw[hh] = spv
                                pvst.append(spv)
                            allpv.extend(pvst)
                            if hp % 2 == 0:
                                continue
                            def rcp(s):
                                ap = bass.AP(pbig, 64 * 4096 + 512 * 4,
                                             [[4096, 1], [512, 2], [1, 256]])
                                s.activation(rrow[0:1, 0:512], ap, AF.Ln)
                                return s.activation(rrow[0:1, 0:512],
                                                    rrow[0:1, 0:512], AF.Exp,
                                                    scale=-1.0)
                            src_ = P.step("act", rcp, deps=allpv + [rrow_w],
                                          tag=f"B2rc{l}{tb}{qt}{hp}")
                            bRB = 6
                            srb = P.step("pe", lambda t, bRB=bRB: t.matmul(
                                pb[bRB][0:64, 0:512], ones_row32[0:1, 0:64],
                                rrow[0:1, 0:512], start=True, stop=True),
                                deps=[src_, P.use_bank(bRB)],
                                tag=f"B2rb{l}{tb}{qt}{hp}")
                            rrow_w = srb
                            srbe = P.step("act", lambda s, bRB=bRB: s.copy(
                                rb32[0:64, 0:512], pb[bRB][0:64, 0:512]),
                                deps=[srb, rbw], tag=f"B2rbe{l}{tb}{qt}{hp}")
                            P.set_bank(bRB, srbe)
                            soes = []
                            for k4 in range(4):
                                h = 2 * (hp - 1) + k4
                                hc = h // 2
                                hh = k4 % 2
                                hr = 64 * hh
                                bV2 = 4 + (h // 2) % 2
                                soe = P.step("dve", lambda e, hc=hc, hr=hr, qt=qt,
                                             hh=hh, k4=k4, bV2=bV2: e.tensor_tensor(
                                                 o_fm[hr:hr + 64,
                                                      512 * hc + 128 * qt:
                                                      512 * hc + 128 * (qt + 1)],
                                                 pb[bV2][0:64,
                                                         128 * hh: 128 * hh + 128],
                                                 rb32[0:64,
                                                      128 * k4: 128 * k4 + 128],
                                                 ALU.mult),
                                             deps=[allpv[k4], srbe],
                                             tag=f"B2oe{l}{tb}{qt}{h}")
                                soes.append(soe)
                            P.set_bank(4, soes[0:2])
                            P.set_bank(5, soes[2:4])
                            rbw = soes
                            b2pe_last[tb] = srb
                            allpv = []
                        P.flush()

                    if tb == 0:
                        stguard = P.step("pool", lambda g: g.memset(
                            bass.AP(zbuf, 0, [[8 * 513, 128], [513, 8], [1, 1]]),
                            0.0), tag=f"zg{l}")
                    else:
                        stguard = P.step("pool", lambda g: g.tensor_copy(
                            bass.AP(zbuf, 0, [[8 * 513, 128], [513, 8], [1, 1]]),
                            bass.AP(zbuf, 512, [[8 * 513, 128], [513, 8], [1, 1]])),
                            deps=[guard_prev] + P.ln_zr, tag=f"zg{l}{tb}")
                    zst = []
                    for oc in range(8):
                        b = 4 + oc % 2

                        def mm_wo2(t, oc=oc, b=b):
                            last = None
                            for icc in range(4):
                                last = t.matmul(
                                    pb[b][:, 0:512],
                                    woB[:, 1024 * icc + 128 * oc:
                                        1024 * icc + 128 * (oc + 1)],
                                    o_fm[:, 512 * icc: 512 * (icc + 1)],
                                    start=(icc == 0), stop=(icc == 3))
                            return last
                        stm = P.step("pe", mm_wo2,
                                     deps=[st_wo, rbw, P.use_bank(b)],
                                     tag=f"B3wo{l}{tb}{oc}")

                        def ev(e, oc=oc, b=b, sl=sl):
                            return e.tensor_tensor(
                                zbuf[:, zc(oc): zc(oc) + 512],
                                pb[b][:, 0:512],
                                xroll[:, xrs(sl, oc) + 1: xrs(sl, oc) + 513],
                                ALU.add)
                        ste = P.step("dve", ev,
                                     deps=[stm, st_x32, stguard,
                                           x1_spills[tb - 1] if tb >= 1 else None]
                                     + P.ln_zr,
                                     tag=f"B3z{l}{tb}{oc}")
                        P.set_bank(b, ste)
                        zst.append(ste)
                    guard_prev = zst[-1]
                    b3_last[tb] = zst[-1]
                    x1_spills[tb] = P.dma("sq", [(
                        dram3(sout, TB * tb),
                        bass.AP(zbuf, 1, [[8 * 513, 128], [513, 8], [1, 512]]))],
                        deps=zst, tag=f"x1sp{l}{tb}")

                    def zsrc(oc):
                        off = zc(oc) if oc < 4 else zc(oc) - 1
                        return zbuf[:, off: off + 512]
                    s7, s1s = ln_stats(zst, zsrc)
                    P.ln_ab = []
                    P.ln_zr = list(s1s)
                    yapp = []
                    for oc in range(8):
                        sa1 = P.step("pool", lambda g, oc=oc: g.tensor_sub(
                            etmp[:, :], zsrc(oc), Bb32[:, :]),
                            deps=[s7, prev_yapp3], tag="yapp1")
                        sa2 = P.step("pool", lambda g: g.tensor_mul(
                            etmp[:, :], etmp[:, :], Ab32[:, :]), deps=[sa1],
                            tag="yapp2")
                        sa3 = P.step("dve", lambda e, oc=oc, sl=sl: e.tensor_scalar(
                            ybuf[:, ybs(sl, oc): ybs(sl, oc) + 512],
                            etmp[:, :], params[(l, "lng")][:, oc:oc + 1],
                            params[(l, "lnb")][:, oc:oc + 1], ALU.mult, ALU.add),
                            deps=[sa2, y_spills[tb - 1] if tb >= 1 else None],
                            tag="yapp3")
                        prev_yapp3 = sa3
                        yapp.append(sa3)
                        P.ln_ab.append(sa1); P.ln_ab.append(sa2)
                        P.ln_zr.append(sa1)
                    y_spills[tb] = P.dma("sq", [(
                        bass.AP(yb, TB * tb, [[T, 128], [128 * T, 8], [1, 512]]),
                        bass.AP(ybuf, 0,
                                [[8 * 512, 128], [512, 8], [1, 512]]))],
                        deps=yapp, tag=f"ysp{l}{tb}")

                P.wait_only("pool", [x1_spills[NTB - 1], y_spills[NTB - 1]],
                            tag=f"drainB13{l}")
                P.flush()

            G2 = ExitStack()
            x2_spills = []
            with G2:
                gsb = lambda name, shape, dt: G2.enter_context(
                    nc.sbuf_tensor(name, shape, dt))
                w1B = gsb(f"w1B{l}", [128, 8 * 5460], BF16)
                w2g = gsb(f"w2g{l}", [128, 2 * 3 * 1024], BF16)
                actB = gsb(f"actB{l}", [128, NJ * 512], BF16)
                ybloc = gsb(f"ybloc{l}", [128, 8 * 512], BF16)
                gel32 = gsb(f"gel{l}", [128, 512], F32)

                SPLIT = 128 * 11
                w1parts = []
                for (c0, wdt) in [(0, SPLIT), (HID, SPLIT),
                                  (SPLIT, HID - SPLIT), (HID + SPLIT, HID - SPLIT)]:
                    w1parts.append(P.dma("gq", [(
                        bass.AP(w1B, c0, [[8 * 5460, 128], [5460, 8], [1, wdt]]),
                        bass.AP(E[f"B{l}_w1"], c0,
                                [[5460, 128], [128 * 5460, 8], [1, wdt]]))],
                        tag=f"w1_{l}_{c0}"))

                pre4 = {}
                ev2_last = None
                act_carry = None
                sg_carry = None

                def mk_x1ld(tb, deps=()):
                    return P.dma("sq", [(
                        bass.AP(xroll, xrs(tb % 2, 0) + 1,
                                [[2 * 8 * 516, 128], [516, 8], [1, 512]]),
                        dram3(sout, TB * tb))],
                        deps=[x1_spills[tb]] + list(deps), tag=f"x1ld{l}{tb}")

                def mk_yld(tb, deps=()):
                    return P.dma("sq", [(
                        bass.AP(ybloc, 0, [[8 * 512, 128], [512, 8], [1, 512]]),
                        bass.AP(yb, TB * tb, [[T, 128], [128 * T, 8], [1, 512]]))],
                        deps=[y_spills[tb]] + list(deps), tag=f"yld{l}{tb}")

                for tb in range(NTB):
                    sl = tb % 2
                    if tb in pre4:
                        pp = pre4.pop(tb)
                        stx1 = pp[0]
                        sty = pp[1] if len(pp) > 1 else mk_yld(
                            tb, deps=[sg_carry] if sg_carry else [])
                    else:
                        stx1 = mk_x1ld(tb)
                        sty = mk_yld(tb)
                    if tb + 1 < NTB:
                        pre4[tb + 1] = [
                            mk_x1ld(tb + 1, deps=[ev2_last] if ev2_last else [])]

                    def w2load(g, deps, tb=tb):
                        j0 = 3 * g
                        njg = min(3, NJ - 3 * g)
                        trs = [(
                            bass.AP(w2g, 1024 * 3 * (g % 2),
                                    [[2 * 3 * 1024, 128], [1024, njg], [1, 1024]]),
                            bass.AP(E[f"B{l}_w2"], 1024 * 128 * j0,
                                    [[1024, 128], [128 * 1024, njg], [1, 1024]]))]
                        return P.dma("gq", trs, deps=deps, tag=f"w2g{l}{tb}{g}")

                    if tb == 0:
                        w2ld = [None] * 8
                        w2ld[0] = w2load(0, [])
                        w2ld[1] = w2load(1, [])
                    else:
                        w2ld = [None] * 8
                        w2ld[0] = nw2[0]
                        w2ld[1] = nw2[1]
                    nw2 = [None, None]

                    prev_act = [None, None]
                    actst = [None] * NJ
                    for j in range(NJ):
                        wj = 128 if j < NJ - 1 else HID - 128 * (NJ - 1)
                        ba_, bg_ = (0, 1) if j % 2 == 0 else (2, 3)
                        acol = 128 * j
                        gcol = HID + 128 * j

                        def mma(t, wj=wj, ba_=ba_, acol=acol):
                            last = None
                            for ic in range(8):
                                last = t.matmul(
                                    pb[ba_][0:wj, 0:512],
                                    w1B[:, 5460 * ic + acol: 5460 * ic + acol + wj],
                                    ybloc[:, 512 * ic: 512 * (ic + 1)],
                                    start=(ic == 0), stop=(ic == 7))
                            return last
                        w1dep = w1parts[0] if j <= 10 else w1parts[2]
                        sa_ = P.step("pe", mma,
                                     deps=[sty, w1dep, P.use_bank(ba_)],
                                     tag=f"B4a{l}{tb}{j}")

                        def mmg(t, wj=wj, bg_=bg_, gcol=gcol):
                            last = None
                            for ic in range(8):
                                last = t.matmul(
                                    pb[bg_][0:wj, 0:512],
                                    w1B[:, 5460 * ic + gcol: 5460 * ic + gcol + wj],
                                    ybloc[:, 512 * ic: 512 * (ic + 1)],
                                    start=(ic == 0), stop=(ic == 7))
                            return last
                        w1depg = w1parts[1] if j <= 10 else w1parts[3]
                        sg_ = P.step("pe", mmg,
                                     deps=[sty, w1depg, P.use_bank(bg_)],
                                     tag=f"B4g{l}{tb}{j}")
                        sgel = P.step("act", lambda s, j=j, wj=wj, bg_=bg_:
                                      s.activation(
                                          gel32[0:wj, 0:512],
                                          pb[bg_][0:wj, 0:512], AF.Gelu,
                                          bias=params[(l, "b1g")][0:wj, j:j + 1]),
                                      deps=[sg_, actst[j - 1] if j >= 1
                                            else act_carry],
                                      tag=f"B4gel{l}{tb}{j}")
                        P.set_bank(bg_, sgel)
                        sact = P.step("dve", lambda e, j=j, wj=wj, ba_=ba_:
                                      e.scalar_tensor_tensor(
                                          actB[0:wj, 512 * j: 512 * j + 512],
                                          pb[ba_][0:wj, 0:512],
                                          params[(l, "b1a")][0:wj, j:j + 1],
                                          gel32[0:wj, 0:512],
                                          ALU.add, ALU.mult),
                                      deps=[sa_, sgel], tag=f"B4act{l}{tb}{j}")
                        P.set_bank(ba_, sact)
                        prev_act[j % 2] = sact
                        actst[j] = sact
                    act_carry = actst[NJ - 1]
                    sg_carry = sg_
                    if tb + 1 < NTB and (tb + 1) in pre4:
                        pre4[tb + 1].append(mk_yld(tb + 1, deps=[sg_carry]))

                    mmsteps = []
                    for g in range(8):
                        j0 = 3 * g
                        njg = min(3, NJ - 3 * g)

                        def mmw2(t, g=g, j0=j0, njg=njg):
                            last = None
                            for jj in range(njg):
                                j = j0 + jj
                                wj = 128 if j < NJ - 1 else HID - 128 * (NJ - 1)
                                for oc in range(8):
                                    last = t.matmul(
                                        pb[oc][:, 0:512],
                                        w2g[0:wj,
                                            1024 * (3 * (g % 2) + jj) + 128 * oc:
                                            1024 * (3 * (g % 2) + jj) +
                                            128 * (oc + 1)],
                                        actB[0:wj, 512 * j: 512 * j + 512],
                                        start=(j == 0), stop=(j == NJ - 1))
                            return last
                        depsl = [w2ld[g], actst[min(j0 + njg, NJ) - 1]]
                        if g == 0:
                            depsl += [P.use_bank(bb) for bb in range(8)]
                        sm = P.step("pe", mmw2, deps=depsl, tag=f"B4w2{l}{tb}{g}")
                        mmsteps.append(sm)
                        if g + 2 < 8:
                            w2ld[g + 2] = w2load(g + 2, [sm])
                        elif tb + 1 < NTB:
                            nw2[g - 6] = w2load(g - 6, [sm], tb=tb + 1)
                    zst2 = []
                    for oc in range(8):
                        def ev2(e, oc=oc, sl=sl):
                            return e.scalar_tensor_tensor(
                                zbuf[:, zc(oc): zc(oc) + 512],
                                pb[oc][:, 0:512],
                                params[(l, "b2")][:, oc:oc + 1],
                                xroll[:, xrs(sl, oc) + 1: xrs(sl, oc) + 513],
                                ALU.add, ALU.add)
                        s_ = P.step("dve", ev2,
                                    deps=[mmsteps[-1], stx1,
                                          x2_spills[-1] if x2_spills else None],
                                    tag=f"B4z{l}{tb}{oc}")
                        P.set_bank(oc, s_)
                        zst2.append(s_)
                    ev2_last = zst2[-1]
                    trs = [(dram3(sout, TB * tb),
                            bass.AP(zbuf, 1, [[8 * 513, 128], [513, 8], [1, 512]]))]
                    if want_out:
                        trs.append((dram3(out, TB * tb),
                                    bass.AP(zbuf, 1,
                                            [[8 * 513, 128], [513, 8], [1, 512]])))
                    x2_spills.append(P.dma("sq", trs, deps=zst2,
                                           tag=f"x2sp{l}{tb}"))
                P.wait_only("pool", [x2_spills[-1]], tag=f"drainB4{l}")
                P.flush()
            return x2_spills[-1]

        last0 = emit_layer(0, xa, xb, want_out=(upto == "L0"))
        if upto == "L0":
            with nc.Block() as block:
                @block.sync
                def _(s):
                    s.wait_ge(P.sem["sq"], P.cnt["sq"])
                    s.wait_ge(P.sem["gq"], P.cnt["gq"])
            return nc
        last1 = emit_layer(1, xb, xc, want_out=False)

        GF = ExitStack()
        with GF:
            zbufF = GF.enter_context(nc.sbuf_tensor("zbufF", [128, 8 * 513], F32))
            xrollF = GF.enter_context(
                nc.sbuf_tensor("xrollF", [128, 2 * 8 * 516], F32))
            outsp = []
            fz_last = [None] * NTB
            prefin = {}

            def mk_fin(tb, deps=()):
                xr = xroll if tb % 2 == 0 else xrollF
                st0 = P.dma("sq", [(
                    bass.AP(xr, xrs(0, 0) + 1,
                            [[2 * 8 * 516, 128], [516, 8], [1, 512]]),
                    dram3(xa, TB * tb))], deps=[last1] + list(deps),
                    tag=f"fl0{tb}")
                st1 = P.dma("sq", [(
                    bass.AP(xr, xrs(1, 0) + 1,
                            [[2 * 8 * 516, 128], [516, 8], [1, 512]]),
                    dram3(xc, TB * tb))], deps=[last1] + list(deps),
                    tag=f"fl1{tb}")
                return (st0, st1)

            for tb in range(NTB):
                zb = zbuf if tb % 2 == 0 else zbufF
                xr = xroll if tb % 2 == 0 else xrollF
                if tb in prefin:
                    st0, st1 = prefin.pop(tb)
                else:
                    st0, st1 = mk_fin(tb)
                zst = []
                for oc in range(8):
                    s_ = P.step("dve", lambda e, oc=oc, zb=zb, xr=xr:
                                e.tensor_tensor(
                                    zb[:, zc(oc): zc(oc) + 512],
                                    xr[:, xrs(0, oc) + 1: xrs(0, oc) + 513],
                                    xr[:, xrs(1, oc) + 1: xrs(1, oc) + 513],
                                    ALU.add),
                                deps=[st0, st1,
                                      outsp[tb - 2] if tb >= 2 else None]
                                + P.ln_zr, tag=f"fz{tb}{oc}")
                    zst.append(s_)
                fz_last[tb] = zst[-1]
                if tb + 1 < NTB:
                    prefin[tb + 1] = mk_fin(
                        tb + 1, deps=[fz_last[tb - 1]] if tb >= 1 else [])
                s7, s1s = ln_stats(zst, lambda oc, zb=zb: zb[:, zc(oc): zc(oc) + 512])
                appst = []
                for oc in range(8):
                    src = zb[:, zc(oc): zc(oc) + 512]

                    def app(g, src=src):
                        g.tensor_sub(src, src, Bb32[:, :])
                        return g.tensor_mul(src, src, Ab32[:, :])
                    appst.append(P.step("pool", app, deps=[s7], tag="lnappF"))
                P.ln_ab = list(appst)
                P.ln_zr = s1s + appst
                outsp.append(P.dma("sq", [(
                    dram3(out, TB * tb),
                    bass.AP(zb, 1, [[8 * 513, 128], [513, 8], [1, 512]]))],
                    deps=appst, tag=f"outF{tb}"))
            P.flush()

        with nc.Block() as block:
            @block.sync
            def _(s):
                s.wait_ge(P.sem["sq"], P.cnt["sq"])
                s.wait_ge(P.sem["gq"], P.cnt["gq"])
    return nc


def kernel(**inputs):
    consts = host_consts()
    shared = host_layout(inputs)
    shared.update(consts)
    nc = build_nc("full")
    x = np.asarray(inputs["L0_attn_x"])
    in_maps = []
    for b in range(8):
        m = dict(shared)
        m["xT"] = np.ascontiguousarray(x[b].T)
        in_maps.append(m)
    res = run_bass_kernel_spmd(nc, in_maps, core_ids=list(range(8)))
    outs = [np.ascontiguousarray(r["yT"].T) for r in res.results]
    return np.stack(outs, axis=0)


# version 21
# speedup vs baseline: 1.2195x; 1.0235x over previous
import numpy as np
import ml_dtypes
from contextlib import ExitStack
from types import SimpleNamespace

import concourse.bass as bass
import concourse.mybir as mybir
from concourse.bass_utils import run_bass_kernel_spmd

F32 = mybir.dt.float32
BF16 = mybir.dt.bfloat16
AF = mybir.ActivationFunctionType
ALU = mybir.AluOpType
AX = mybir.AxisListType
BF = ml_dtypes.bfloat16

T = 2048
D = 1024
NTB = 4
TB = 512
HID = 2730
NJ = 22
NEG = -1.0e30
C_EIGHTH, C_EPS, C_TINY, C_INV1024, C_TWO, C_LN8 = 0, 1, 2, 3, 4, 5


def host_consts():
    c = {}
    c["identin"] = np.eye(128, dtype=np.float32)
    sel8 = np.zeros((8, 1024), np.float32)
    for h in range(8):
        sel8[h, 128 * h:128 * (h + 1)] = 1.0
    c["sel8in"] = sel8
    i2t = np.zeros((2, 128), np.float32)
    i2t[0, 0:64] = 1.0
    i2t[1, 64:128] = 1.0
    c["ind2tin"] = i2t
    r = np.arange(128)
    c["maskLEin"] = np.where(r[None, :] <= r[:, None], 0.0, NEG).astype(np.float32)
    c["maskGEin"] = np.where(r[None, :] >= r[:, None], 0.0, NEG).astype(np.float32)
    half = np.arange(0, 64, 2, dtype=np.float64)
    inv_freq = 1.0 / 10000.0 ** (half / 64)
    t = np.arange(T, dtype=np.float64)
    freqs = t[:, None] * inv_freq[None, :]
    base = (half + 0.4 * 64) / (1.4 * 64)
    power = (t - T // 2) / 256.0
    sc = base[None, :] ** power[:, None]
    cos, sin = np.cos(freqs), np.sin(freqs)
    sgn = np.concatenate([-np.ones(32), np.ones(32)])[:, None]

    def fm(tab):
        d64 = np.concatenate([tab.T, tab.T], axis=0)
        return np.concatenate([d64, d64], axis=0).astype(np.float32)

    def fmsgn(tab):
        d64 = np.concatenate([tab.T, tab.T], axis=0) * sgn
        return np.concatenate([d64, d64], axis=0).astype(np.float32)

    c["Tcq"] = fm(cos * sc)
    c["Tsq"] = fmsgn(sin * sc)
    c["Tck"] = fm(cos / sc)
    c["Tsk"] = fmsgn(sin / sc)
    return c


def host_layout(ins):
    d = {}
    d["A_wqT"] = np.ascontiguousarray(np.asarray(ins["L0_attn_Wq"]).T)
    d["A_wkT"] = np.ascontiguousarray(np.asarray(ins["L0_attn_Wk"]).T)
    d["A_wvT"] = np.ascontiguousarray(np.asarray(ins["L0_attn_Wv"]).T)
    d["A_woT"] = np.ascontiguousarray(np.asarray(ins["L0_attn_Wo"]).T)
    d["A_bq"] = np.ascontiguousarray(np.asarray(ins["L0_attn_bq"]).reshape(8, 128).T)
    d["A_bk"] = np.ascontiguousarray(np.asarray(ins["L0_attn_bk"]).reshape(8, 128).T)
    d["A_bvrow"] = np.asarray(ins["L0_attn_bv"]).reshape(1, 1024).copy()
    d["A_borow"] = np.asarray(ins["L0_attn_bo"]).reshape(1, 1024).copy()
    perm = np.concatenate([np.arange(32, 64), np.arange(32)])
    permfull = np.concatenate([64 * h + perm for h in range(8)])
    for l in range(2):
        w = np.asarray(ins["L0_ffn_wqkv"][l])
        d[f"B{l}_wqkv"] = np.ascontiguousarray(w)
        d[f"B{l}_wsh"] = np.ascontiguousarray(
            np.concatenate([w[:, permfull], w[:, 512 + permfull]], axis=1))
        d[f"B{l}_wo"] = np.ascontiguousarray(np.asarray(ins["L0_ffn_wo"][l]))
        d[f"B{l}_w1"] = np.ascontiguousarray(np.asarray(ins["L0_ffn_w1"][l]))
        w2p = np.zeros((NJ * 128, 1024), np.float32)
        w2p[:HID] = np.asarray(ins["L0_ffn_w2"][l])
        d[f"B{l}_w2"] = w2p
        qs = np.asarray(ins["L0_ffn_q_scale"][l]).astype(np.float32)
        ks = np.asarray(ins["L0_ffn_k_scale"][l]).astype(np.float32)
        d[f"B{l}_qs"] = np.tile(qs, 2).reshape(128, 1).copy()
        d[f"B{l}_qssh"] = np.tile(qs[perm], 2).reshape(128, 1).copy()
        d[f"B{l}_ks"] = np.tile(ks, 2).reshape(128, 1).copy()
        d[f"B{l}_kssh"] = np.tile(ks[perm], 2).reshape(128, 1).copy()
        d[f"B{l}_lng"] = np.ascontiguousarray(
            np.asarray(ins["L0_ffn_ln_g"][l]).reshape(8, 128).T)
        d[f"B{l}_lnb"] = np.ascontiguousarray(
            np.asarray(ins["L0_ffn_ln_b"][l]).reshape(8, 128).T)
        b1 = np.asarray(ins["L0_ffn_b1"][l])
        b1a = np.zeros(NJ * 128, np.float32); b1a[:HID] = b1[:HID]
        b1g = np.zeros(NJ * 128, np.float32); b1g[:HID] = b1[HID:]
        d[f"B{l}_b1a"] = np.ascontiguousarray(b1a.reshape(NJ, 128).T)
        d[f"B{l}_b1g"] = np.ascontiguousarray(b1g.reshape(NJ, 128).T)
        d[f"B{l}_b2"] = np.ascontiguousarray(
            np.asarray(ins["L0_ffn_b2"][l]).reshape(8, 128).T)
    return d


SEC = {"pe": "tensor", "act": "scalar", "dve": "vector",
       "pool": "gpsimd", "gq": "gpsimd", "sq": "sync"}


class Prog:
    def __init__(self, nc, ctx):
        self.nc = nc
        self.sem = {k: ctx.enter_context(nc.semaphore(f"s_{k}")) for k in SEC}
        self.cnt = {k: 0 for k in SEC}
        self.steps = []
        self.wm = {s: {} for s in set(SEC.values())}
        self.bank_state = [[] for _ in range(8)]

    def step(self, key, emit, deps=(), n=1, tag=""):
        flat = []
        for d in deps:
            if d is None:
                continue
            if isinstance(d, list):
                flat.extend(x for x in d if x is not None)
            else:
                flat.append(d)
        s = SimpleNamespace(key=key, emit=emit, tag=tag, n=n, deps=flat,
                            section=SEC[key])
        self.cnt[key] += n
        s.val = self.cnt[key]
        self.steps.append(s)
        return s

    def dma(self, key, transfers, deps=(), tag=""):
        assert key in ("sq", "gq")
        sem = self.sem[key]
        tr = list(transfers)

        def emit(eng):
            last = None
            for (o, i) in tr:
                last = eng.dma_start(o, i).then_inc(sem, 16)
            return last
        return self.step(key, emit, deps, n=16 * len(tr), tag=tag)

    def wait_only2(self, key, deps, tag=""):
        return self.wait_only(key, deps, tag)

    def wait_only(self, key, deps, tag=""):
        return self.step(key, (lambda eng: None), deps, n=0, tag=tag)

    def flush(self):
        steps, self.steps = self.steps, []
        if not steps:
            return
        secs = {}
        for s in steps:
            secs.setdefault(s.section, []).append(s)
        with self.nc.Block() as block:
            for secname, slist in secs.items():
                def make(slist=slist, secname=secname):
                    def body(eng):
                        for s in slist:
                            for d in s.deps:
                                if d.key == s.key and d.key not in ("sq", "gq"):
                                    continue
                                wm = self.wm[secname]
                                if wm.get(d.key, 0) >= d.val:
                                    continue
                                eng.wait_ge(self.sem[d.key], d.val)
                                wm[d.key] = d.val
                            last = s.emit(eng)
                            if s.key not in ("sq", "gq") and last is not None:
                                last.then_inc(self.sem[s.key], 1)
                    return body
                getattr(block, secname)(make())

    def use_bank(self, b):
        return list(self.bank_state[b] or [])

    def set_bank(self, b, step):
        self.bank_state[b] = step if isinstance(step, list) else [step]

    def add_bank_reader(self, b, step):
        if self.bank_state[b] is None:
            self.bank_state[b] = []
        self.bank_state[b].append(step)


def build_nc(upto="full"):
    nc = bass.Bass("TRN2", target_bir_lowering=False)
    E = {}

    def din(name, shape, dt=F32):
        E[name] = nc.dram_tensor(name, shape, dt, kind="ExternalInput")
        return E[name]

    din("xT", [D, T])
    din("identin", [128, 128]); din("sel8in", [8, 1024]); din("ind2tin", [2, 128])
    din("maskLEin", [128, 128]); din("maskGEin", [128, 128])
    for n in ["Tcq", "Tsq", "Tck", "Tsk"]:
        din(n, [128, T])
    din("A_wqT", [D, D]); din("A_wkT", [D, D]); din("A_wvT", [D, D]); din("A_woT", [D, D])
    din("A_bq", [128, 8]); din("A_bk", [128, 8])
    din("A_bvrow", [1, D]); din("A_borow", [1, D])
    for l in range(2):
        din(f"B{l}_wqkv", [D, 1536]); din(f"B{l}_wsh", [D, D])
        din(f"B{l}_wo", [512, D]); din(f"B{l}_w1", [D, 5460])
        din(f"B{l}_w2", [NJ * 128, D])
        for n in ["qs", "qssh", "ks", "kssh"]:
            din(f"B{l}_{n}", [128, 1])
        din(f"B{l}_lng", [128, 8]); din(f"B{l}_lnb", [128, 8])
        din(f"B{l}_b1a", [128, NJ]); din(f"B{l}_b1g", [128, NJ]); din(f"B{l}_b2", [128, 8])
    out = nc.dram_tensor("yT", [D, T], F32, kind="ExternalOutput")
    xa = nc.dram_tensor("xa_scr", [D, T], F32)
    xb = nc.dram_tensor("xb_scr", [D, T], F32)
    xc = nc.dram_tensor("xc_scr", [D, T], F32)
    yb = nc.dram_tensor("yb_scr", [D, T], BF16)

    def dram3(dten, col0, width=512):
        return bass.AP(dten, col0, [[T, 128], [128 * T, 8], [1, width]])

    ctx = ExitStack()
    with ctx:
        P = Prog(nc, ctx)
        sb = lambda name, shape, dt: ctx.enter_context(nc.sbuf_tensor(name, shape, dt))
        pbig = ctx.enter_context(nc.psum_tensor("pbig", [128, 4096], F32))
        pb = [pbig.ap()[:, 512 * i: 512 * (i + 1)] for i in range(8)]

        cst = sb("cst", [128, 8], F32)
        ident = sb("ident", [128, 128], F32)
        ident_bf = sb("ident_bf", [128, 128], BF16)
        sel8 = sb("sel8", [8, 1024], BF16)
        maskLE = sb("maskLE", [128, 128], BF16)
        maskGE = sb("maskGE", [128, 128], BF16)
        ones_row = sb("ones_row", [1, 512], BF16)
        ones_col = sb("ones_col", [128, 1], BF16)
        ones_row32 = sb("ones_row32", [1, 128], F32)
        ind2 = sb("ind2", [128, 2], BF16)
        ind2t = sb("ind2t", [2, 128], BF16)
        xroll = sb("xroll", [128, 2 * 8 * 516], F32)
        xrs = lambda s, ic: 516 * (8 * s + ic)
        negmax8 = sb("negmax8", [128, 8], F32)
        negtmp = sb("negtmp", [128, 2], F32)
        nm8 = sb("nm8", [8, 128], BF16)
        nm32 = sb("nm32", [8, 512], BF16)
        mrow = sb("mrow", [1, 512], F32)
        vrow = sb("vrow", [1, 512], F32)
        rstdr = sb("rstdr", [1, 512], F32)
        rrow = sb("rrow", [1, 512], F32)
        sspair = sb("sspair", [2, 512], F32)
        fpair = sb("fpair", [2, 512], BF16)
        Ab32 = sb("Ab32", [128, 512], F32)
        Bb32 = sb("Bb32", [128, 512], F32)
        rb32 = sb("rb32", [128, 512], F32)
        bqs = sb("bqs", [128, 8], F32)
        bkA = sb("bkA", [128, 8], F32)
        zbuf = sb("zbuf", [128, 8 * 513], F32)
        zc = lambda oc: 513 * oc + 1
        zsqs = sb("zsqs", [128, 2 * 512], BF16)
        zbfs = sb("zbfs", [128, 2 * 512], BF16)
        params = {}
        for l in range(2):
            for n in ["qs", "qssh", "ks", "kssh"]:
                params[(l, n)] = sb(f"p{l}{n}", [128, 1], F32)
            for n in ["lng", "lnb", "b2"]:
                params[(l, n)] = sb(f"p{l}{n}", [128, 8], F32)
            for n in ["b1a", "b1g"]:
                params[(l, n)] = sb(f"p{l}{n}", [128, NJ], F32)

        def init_pool(g):
            g.memset(ones_row[:, :], 1.0)
            g.memset(ones_col[:, :], 1.0)
            g.memset(ones_row32[:, :], 1.0)
            g.memset(ind2[0:64, 0:1], 1.0)
            g.memset(ind2[0:64, 1:2], 0.0)
            g.memset(ind2[64:128, 0:1], 0.0)
            g.memset(ind2[64:128, 1:2], 1.0)
            g.memset(cst[:, C_EIGHTH:C_EIGHTH + 1], 0.125)
            g.memset(cst[:, C_EPS:C_EPS + 1], 1e-5)
            g.memset(cst[:, C_TINY:C_TINY + 1], 1e-24)
            g.memset(cst[:, C_INV1024:C_INV1024 + 1], 1.0 / 1024.0)
            g.memset(cst[:, C_TWO:C_TWO + 1], 2.0)
            return g.memset(cst[:, C_LN8:C_LN8 + 1], 2.0794415416798357)
        st_ms = P.step("pool", init_pool, tag="init_memset")
        st_cst = P.dma("gq", [
            (ident_bf[:, :], E["identin"].ap()),
            (sel8[:, :], E["sel8in"].ap()),
            (ind2t[:, :], E["ind2tin"].ap()),
            (maskLE[:, :], E["maskLEin"].ap()),
            (maskGE[:, :], E["maskGEin"].ap()),
        ], tag="cast_consts")
        st_cf = P.dma("sq", [
            (ident[:, :], E["identin"].ap()),
            (bkA[:, :], E["A_bk"].ap()),
        ] + [(params[(l, n)][:, :], E[f"B{l}_{n}"].ap())
             for l in range(2)
             for n in ["qs", "qssh", "ks", "kssh", "lng", "lnb", "b2", "b1a", "b1g"]],
            tag="f32_consts")
        st_bq = P.dma("sq", [(bqs[:, :], E["A_bq"].ap())], tag="bq_load")
        st_bqs = P.step("dve", lambda e: e.tensor_scalar(
            bqs[:, :], bqs[:, :], cst[:, C_EIGHTH:C_EIGHTH + 1], None, ALU.mult),
            deps=[st_bq, st_ms], tag="bq_scale")
        P.wait_only("pool", [st_cst, st_cf, st_bq], tag="init_bar")
        P.flush()

        P.ln_ab = []
        P.ln_zr = []

        def ln_stats(dep8, zsrc):
            s1s = []
            csteps = []
            for oc in range(8):
                src = zsrc(oc)
                war = [csteps[oc - 2]] if oc >= 2 else [P.use_bank(6), P.use_bank(7)]
                s1 = P.step("act", lambda s, src=src, oc=oc: s.activation(
                    zsqs[:, 512 * (oc % 2): 512 * (oc % 2 + 1)], src, AF.Square),
                    deps=[dep8[oc]] + war, tag="lnsq")
                s1s.append(s1)
                s2 = P.step("dve", lambda e, src=src, oc=oc: e.tensor_copy(
                    zbfs[:, 512 * (oc % 2): 512 * (oc % 2 + 1)], src),
                    deps=[dep8[oc]] + war, tag="lnbf")

                def mmst(t, oc=oc):
                    t.matmul(pb[6][0:1, 0:512], ones_col[:, :],
                             zbfs[:, 512 * (oc % 2): 512 * (oc % 2 + 1)],
                             start=(oc == 0), stop=(oc == 7))
                    return t.matmul(pb[7][0:1, 0:512], ones_col[:, :],
                                    zsqs[:, 512 * (oc % 2): 512 * (oc % 2 + 1)],
                                    start=(oc == 0), stop=(oc == 7))
                csteps.append(P.step("pe", mmst, deps=[s1, s2], tag="lnmm"))

            def rows(e):
                e.tensor_scalar(mrow[:, :], pb[6][0:1, 0:512],
                                cst[0:1, C_INV1024:C_INV1024 + 1], None, ALU.mult)
                e.tensor_scalar(vrow[:, :], pb[7][0:1, 0:512],
                                cst[0:1, C_INV1024:C_INV1024 + 1], None, ALU.mult)
                e.tensor_mul(rstdr[:, :], mrow[:, :], mrow[:, :])
                return e.tensor_sub(vrow[:, :], vrow[:, :], rstdr[:, :])
            s4 = P.step("dve", rows, deps=[csteps[-1]], tag="lnrows")
            P.set_bank(6, s4)
            P.set_bank(7, s4)

            def rstd(s):
                s.activation(rstdr[:, :], vrow[:, :], AF.Ln,
                             bias=cst[0:1, C_EPS:C_EPS + 1])
                return s.activation(rstdr[:, :], rstdr[:, :], AF.Exp, scale=-0.5)
            s5 = P.step("act", rstd, deps=[s4], tag="lnrstd")

            def bc(t):
                t.matmul(pb[6][:, 0:512], ones_row32[:, :], rstdr[:, :],
                         start=True, stop=True)
                return t.matmul(pb[7][:, 0:512], ones_row32[:, :], mrow[:, :],
                                start=True, stop=True)
            s6 = P.step("pe", bc, deps=[s5, P.use_bank(6), P.use_bank(7)], tag="lnbc")

            def bce(s):
                s.copy(Ab32[:, :], pb[6][:, 0:512])
                return s.copy(Bb32[:, :], pb[7][:, 0:512])
            s7 = P.step("act", bce, deps=[s6] + P.ln_ab, tag="lnbce")
            P.set_bank(6, s7)
            P.set_bank(7, s7)
            return s7, s1s

        GA = ExitStack()
        with GA:
            asb = lambda name, shape, dt: GA.enter_context(
                nc.sbuf_tensor(name, shape, dt))
            wA = {n: asb(f"wA_{n}", [128, 8 * 1024], BF16) for n in "qkvo"}
            xbfA = asb("xbfA", [128, 2 * 8 * 516], BF16)
            xbs = lambda s, ic: 516 * (8 * s + ic)
            qA = asb("qA", [128, 8 * 512], BF16)
            kA = asb("kA", [128, 8 * 512], BF16)
            vA = asb("vA", [128, 4 * 1024], BF16)
            o_fmA = asb("o_fmA", [128, 8 * 512], BF16)
            PTA = asb("PTA", [128, 512], BF16)
            nm2 = asb("nm2", [2, 128], BF16)
            bvrow = asb("bvrow", [1, D], BF16)
            borow = asb("borow", [1, D], BF16)

            st_brows = P.dma("gq", [(bvrow[:, :], E["A_bvrow"].ap()),
                                    (borow[:, :], E["A_borow"].ap())], tag="brows")
            P.wait_only("pool", [st_brows], tag="wA_bar")
            P.flush()
            wsteps = {}
            for n, src in [("q", "A_wqT"), ("k", "A_wkT"), ("v", "A_wvT"),
                           ("o", "A_woT")]:
                sap = bass.AP(E[src], 0, [[1024, 128], [128 * 1024, 8], [1, 1024]])
                dap = wA[n].ap().rearrange("p (ic n) -> p ic n", ic=8)
                wsteps[n] = P.dma("gq", [(dap, sap)], tag=f"wA_{n}")

            prev_spill = None
            a3_last = [None] * NTB
            pre_xbf = {}
            pre_x32 = {}
            a1pe_last = [None] * NTB
            a2pe_last = [None] * NTB
            pta_reader = None
            rb_reader = None

            def mk_xbfA(tb, deps=()):
                sl = tb % 2
                if tb == 0:
                    st_g = P.step("pool", lambda g: g.memset(
                        bass.AP(xbfA, xbs(0, 0),
                                [[2 * 8 * 516, 128], [516, 8], [1, 1]]), 0.0),
                        tag="xbf_guard")
                    return P.dma("gq", [(
                        bass.AP(xbfA, xbs(sl, 0) + 1,
                                [[2 * 8 * 516, 128], [516, 8], [1, 512]]),
                        bass.AP(E["xT"], 0, [[T, 128], [128 * T, 8], [1, 512]]))],
                        deps=[st_g], tag="xbfA0")
                return P.dma("gq", [(
                    bass.AP(xbfA, xbs(sl, 0),
                            [[2 * 8 * 516, 128], [516, 8], [1, 513]]),
                    bass.AP(E["xT"], TB * tb - 1,
                            [[T, 128], [128 * T, 8], [1, 513]]))],
                    deps=list(deps), tag=f"xbfA{tb}")

            def mk_x32A(tb, deps=()):
                sl = tb % 2
                if tb == 0:
                    st_gx = P.step("pool", lambda g: g.memset(
                        bass.AP(xroll, xrs(0, 0),
                                [[2 * 8 * 516, 128], [516, 8], [1, 1]]), 0.0),
                        tag="xroll_guard")
                    return P.dma("sq", [(
                        bass.AP(xroll, xrs(sl, 0) + 1,
                                [[2 * 8 * 516, 128], [516, 8], [1, 512]]),
                        bass.AP(E["xT"], 0, [[T, 128], [128 * T, 8], [1, 512]]))],
                        deps=[st_gx] + list(deps), tag="x32_0")
                return P.dma("sq", [(
                    bass.AP(xroll, xrs(sl, 0),
                            [[2 * 8 * 516, 128], [516, 8], [1, 513]]),
                    bass.AP(E["xT"], TB * tb - 1,
                            [[T, 128], [128 * T, 8], [1, 513]]))],
                    deps=deps, tag=f"x32_{tb}")

            for tb in range(NTB):
                sl = tb % 2
                st_xbf = pre_xbf.pop(tb) if tb in pre_xbf else mk_xbfA(tb)
                st_x32 = pre_x32.pop(tb) if tb in pre_x32 else mk_x32A(tb)
                if tb + 1 < NTB:
                    pre_xbf[tb + 1] = mk_xbfA(
                        tb + 1, deps=[a1pe_last[tb - 1]] if tb >= 1 else [])
                    pre_x32[tb + 1] = mk_x32A(
                        tb + 1, deps=[a3_last[tb - 1]] if tb >= 1 else [])

                def xb_data(ic, sl=sl):
                    return xbfA[:, xbs(sl, ic) + 1: xbs(sl, ic) + 513]

                last_ev = None
                for which, wt, dst, brow, scale, wst in [
                        ("q", wA["q"], qA, bqs, 0.125, wsteps["q"]),
                        ("k", wA["k"], kA, bkA, 1.0, wsteps["k"])]:
                    for oc in range(8):
                        b = oc % 4

                        def mm_qk(t, wt=wt, oc=oc, b=b):
                            last = None
                            for ic in range(8):
                                last = t.matmul(
                                    pb[b][:, 0:512],
                                    wt[:, 1024 * ic + 128 * oc:
                                       1024 * ic + 128 * (oc + 1)],
                                    xb_data(ic), start=(ic == 0), stop=(ic == 7))
                            return last
                        stm = P.step("pe", mm_qk,
                                     deps=[st_xbf, wst, P.use_bank(b)],
                                     tag=f"A1{which}{oc}")

                        def ev(s, dst=dst, oc=oc, b=b, brow=brow, scale=scale):
                            return s.activation(
                                dst[:, 512 * oc: 512 * (oc + 1)],
                                pb[b][:, 0:512], AF.Identity,
                                bias=brow[:, oc:oc + 1], scale=scale)
                        ste = P.step("act", ev,
                                     deps=[stm, st_cf, st_bqs,
                                           a2pe_last[tb - 1] if tb >= 1 else None],
                                     tag=f"A1{which}e{oc}")
                        P.set_bank(b, ste)
                        last_ev = ste
                for tc in range(4):
                    for half in range(2):
                        b = 4 + (2 * tc + half) % 4

                        def mm_v(t, tc=tc, half=half, b=b, sl=sl):
                            for ic in range(8):
                                t.matmul(
                                    pb[b][:, 0:512],
                                    xbfA[:, xbs(sl, ic) + 1 + 128 * tc:
                                         xbs(sl, ic) + 1 + 128 * (tc + 1)],
                                    wA["v"][:, 1024 * ic + 512 * half:
                                            1024 * ic + 512 * (half + 1)],
                                    start=(ic == 0), stop=False)
                            return t.matmul(
                                pb[b][:, 0:512], ones_row[0:1, 0:128],
                                bvrow[0:1, 512 * half: 512 * (half + 1)],
                                start=False, stop=True)
                        stm = P.step("pe", mm_v,
                                     deps=[st_xbf, wsteps["v"], st_brows, st_ms,
                                           P.use_bank(b)], tag=f"A1v{tc}{half}")

                        def ev(s, tc=tc, half=half, b=b):
                            return s.copy(vA[:, 1024 * tc + 512 * half:
                                             1024 * tc + 512 * (half + 1)],
                                          pb[b][:, 0:512])
                        ste = P.step("act", ev,
                                     deps=[stm,
                                           a2pe_last[tb - 1] if tb >= 1 else None],
                                     tag=f"A1ve{tc}{half}")
                        P.set_bank(b, ste)
                        last_ev = ste
                        a1pe_last[tb] = stm

                for wi in range(2):
                    nmsteps = []
                    for j in range(2):
                        qoff = 256 * wi + 128 * j
                        bS = 4

                        def mm_s(t, wi=wi, j=j, qoff=qoff, bS=bS):
                            span = 128 * (j + 1)
                            for ic in range(8):
                                t.matmul(
                                    pb[bS][:, 0:span],
                                    qA[:, 512 * ic + qoff: 512 * ic + qoff + 128],
                                    kA[:, 512 * ic + 256 * wi:
                                       512 * ic + 256 * wi + span],
                                    start=(ic == 0), stop=False)
                            return t.matmul(
                                pb[bS][:, 128 * j: 128 * (j + 1)],
                                ident_bf[:, :], maskLE[:, :],
                                start=False, stop=True)
                        stm = P.step("pe", mm_s,
                                     deps=[last_ev, st_cst, P.use_bank(bS)],
                                     tag=f"A2s{wi}{j}")

                        def red(e, j=j, wi=wi, bS=bS):
                            return e.reduce_max(
                                negmax8[:, 2 * wi + j: 2 * wi + j + 1],
                                pb[bS][:, 0:128 * (j + 1)], axis=AX.X, negate=True)
                        str_ = P.step("dve", red, deps=[stm], tag=f"A2r{wi}{j}")
                        P.set_bank(bS, str_)
                        nmsteps.append(str_)

                    def tp(t, wi=wi):
                        return t.transpose(
                            bass.AP(pbig, 512 * 5, [[4096, 2], [1, 128]]),
                            negmax8[:, 2 * wi: 2 * wi + 2], ident[:, :])
                    sttp = P.step("pe", tp, deps=nmsteps + [st_cf, P.use_bank(5)],
                                  tag=f"A2tp{wi}")
                    stnm = P.step("act", lambda s: s.copy(
                        nm2[:, :], bass.AP(pbig, 512 * 5, [[4096, 2], [1, 128]])),
                        deps=[sttp], tag=f"A2nm{wi}")
                    P.set_bank(5, stnm)

                    for j in range(2):
                        qoff = 256 * wi + 128 * j
                        bT = 6

                        def mm_st(t, wi=wi, j=j, qoff=qoff, bT=bT):
                            for kc in range(j + 1):
                                for ic in range(8):
                                    t.matmul(
                                        pb[bT][:, 128 * kc: 128 * (kc + 1)],
                                        kA[:, 512 * ic + 256 * wi + 128 * kc:
                                           512 * ic + 256 * wi + 128 * (kc + 1)],
                                        qA[:, 512 * ic + qoff: 512 * ic + qoff + 128],
                                        start=(ic == 0 and kc == 0), stop=False)
                            t.matmul(
                                pb[bT][:, 128 * j: 128 * (j + 1)],
                                ident_bf[:, :], maskGE[:, :],
                                start=False, stop=False)
                            nrep = bass.AP(nm2, 0, [[128, 2], [0, j + 1], [1, 128]])
                            return t.matmul(
                                pb[bT][:, 0:128 * (j + 1)],
                                sel8[0:2, 128 * j: 128 * (j + 1)],
                                nrep, start=False, stop=True)
                        stm = P.step("pe", mm_st, deps=[stnm, P.use_bank(bT)],
                                     tag=f"A2t{wi}{j}")
                        stexp = P.step("act", lambda s, j=j, bT=bT: s.activation(
                            PTA[:, 0:128 * (j + 1)], pb[bT][:, 0:128 * (j + 1)],
                            AF.Exp), deps=[stm, pta_reader], tag=f"A2e{wi}{j}")
                        P.set_bank(bT, stexp)
                        bU = 7

                        def mm_u(t, j=j, bU=bU):
                            last = None
                            for kc in range(j + 1):
                                last = t.matmul(
                                    pb[bU][0:1, 0:128], ones_col[:, :],
                                    PTA[:, 128 * kc: 128 * (kc + 1)],
                                    start=(kc == 0), stop=(kc == j))
                            return last
                        stu = P.step("pe", mm_u, deps=[stexp, P.use_bank(bU)],
                                     tag=f"A2u{wi}{j}")
                        strc = P.step("dve", lambda e, bU=bU: e.reciprocal(
                            rrow[0:1, 0:128], pb[bU][0:1, 0:128]), deps=[stu],
                            tag=f"A2rc{wi}{j}")
                        P.set_bank(bU, strc)
                        bR = 5
                        strb = P.step("pe", lambda t, bR=bR: t.matmul(
                            pb[bR][:, 0:128], ones_row32[:, :], rrow[0:1, 0:128],
                            start=True, stop=True),
                            deps=[strc, P.use_bank(bR)], tag=f"A2rb{wi}{j}")
                        strbe = P.step("act", lambda s, bR=bR: s.copy(
                            rb32[:, 0:128], pb[bR][:, 0:128]), deps=[strb, rb_reader],
                            tag=f"A2rbe{wi}{j}")
                        P.set_bank(bR, strbe)

                        def mm_pv(t, wi=wi, j=j):
                            last = None
                            for oc in range(8):
                                bV = oc // 4
                                for kc in range(j + 1):
                                    tc = 2 * wi + kc
                                    last = t.matmul(
                                        pb[bV][:, 128 * (oc % 4):
                                               128 * (oc % 4 + 1)],
                                        vA[:, 1024 * tc + 128 * oc:
                                           1024 * tc + 128 * (oc + 1)],
                                        PTA[:, 128 * kc: 128 * (kc + 1)],
                                        start=(kc == 0), stop=(kc == j))
                            return last
                        stpv = P.step("pe", mm_pv,
                                      deps=[stexp, P.use_bank(0), P.use_bank(1)],
                                      tag=f"A2pv{wi}{j}")

                        def ev_o(e, wi=wi, j=j, qoff=qoff):
                            last = None
                            for oc in range(8):
                                bV = oc // 4
                                last = e.tensor_tensor(
                                    o_fmA[:, 512 * oc + qoff:
                                          512 * oc + qoff + 128],
                                    pb[bV][:, 128 * (oc % 4): 128 * (oc % 4 + 1)],
                                    rb32[:, 0:128], ALU.mult)
                            return last
                        stoe = P.step("dve", ev_o, deps=[stpv, strbe],
                                      tag=f"A2oe{wi}{j}")
                        P.set_bank(0, stoe)
                        P.set_bank(1, stoe)
                        pta_reader = stpv
                        rb_reader = stoe
                        a2pe_last[tb] = stpv
                    if wi == 1:
                        P.flush()

                zst = []
                for oc in range(8):
                    b = 4 + oc % 2

                    def mm_wo(t, oc=oc, b=b):
                        for ic in range(8):
                            t.matmul(
                                pb[b][:, 0:512],
                                wA["o"][:, 1024 * ic + 128 * oc:
                                        1024 * ic + 128 * (oc + 1)],
                                o_fmA[:, 512 * ic: 512 * (ic + 1)],
                                start=(ic == 0), stop=False)
                        return t.matmul(
                            pb[b][:, 0:512],
                            borow[0:1, 128 * oc: 128 * (oc + 1)],
                            ones_row[0:1, 0:512], start=False, stop=True)
                    stm = P.step("pe", mm_wo,
                                 deps=[st_brows, wsteps["o"], rb_reader,
                                       P.use_bank(b)],
                                 tag=f"A3wo{oc}")

                    def ev(e, oc=oc, b=b, sl=sl):
                        return e.scalar_tensor_tensor(
                            zbuf[:, zc(oc): zc(oc) + 512],
                            xroll[:, xrs(sl, oc) + 1: xrs(sl, oc) + 513],
                            cst[:, C_TWO:C_TWO + 1],
                            pb[b][:, 0:512], ALU.mult, ALU.add)
                    ste = P.step("dve", ev,
                                 deps=[stm, st_x32, prev_spill] + P.ln_zr,
                                 tag=f"A3z{oc}")
                    P.set_bank(b, ste)
                    zst.append(ste)
                a3_last[tb] = zst[-1]

                s7, s1s = ln_stats(zst, lambda oc: zbuf[:, zc(oc): zc(oc) + 512])
                appst = []
                for oc in range(8):
                    src = zbuf[:, zc(oc): zc(oc) + 512]

                    def app(g, src=src):
                        g.tensor_sub(src, src, Bb32[:, :])
                        return g.tensor_mul(src, src, Ab32[:, :])
                    appst.append(P.step("pool", app, deps=[s7], tag="lnappA"))
                P.ln_ab = list(appst)
                P.ln_zr = s1s + appst
                tr = [(dram3(xa, TB * tb),
                       bass.AP(zbuf, 1, [[8 * 513, 128], [513, 8], [1, 512]]))]
                if upto == "A":
                    tr.append((dram3(out, TB * tb),
                               bass.AP(zbuf, 1, [[8 * 513, 128], [513, 8], [1, 512]])))
                prev_spill = P.dma("sq", tr, deps=appst, tag=f"y0spill{tb}")

            P.wait_only("pool", [prev_spill], tag="drainA")
            P.flush()

        if upto == "A":
            with nc.Block() as block:
                @block.sync
                def _(s):
                    s.wait_ge(P.sem["sq"], P.cnt["sq"])
                    s.wait_ge(P.sem["gq"], P.cnt["gq"])
            return nc

        def emit_layer(l, sin, sout, want_out):
            pqn, pqsh = params[(l, "qs")], params[(l, "qssh")]
            pkn, pksh = params[(l, "ks")], params[(l, "kssh")]
            L1 = ExitStack()
            x1_spills = [None] * NTB
            y_spills = [None] * NTB
            with L1:
                lsb = lambda name, shape, dt: L1.enter_context(
                    nc.sbuf_tensor(name, shape, dt))
                tabs = {n: lsb(f"tab{n}{l}", [128, T], BF16)
                        for n in ["cq", "sq", "ck", "sk"]}
                wqkvB = lsb(f"wqkv{l}", [128, 8 * 1536], BF16)
                wshB = lsb(f"wsh{l}", [128, 8 * 1024], BF16)
                woB = lsb(f"wo{l}", [128, 4 * 1024], BF16)
                xbfB = lsb(f"xbf{l}", [128, 2 * 8 * 516], BF16)
                xbs2 = lambda s, ic: 516 * (8 * s + ic)
                q_rot = lsb(f"qrot{l}", [128, 4 * 512], BF16)
                k_rot = lsb(f"krot{l}", [128, 2 * 4 * 512], BF16)
                v_aug = lsb(f"vaug{l}", [128, 2 * 4 * 520], BF16)
                o_fm = lsb(f"ofm{l}", [128, 4 * 512], BF16)
                PT = lsb(f"PT{l}", [128, 2 * 640], BF16)
                ybuf = lsb(f"ybuf{l}", [128, 8 * 512], BF16)
                ybs = lambda s, ic: 512 * ic
                qsqb = lsb(f"qsq{l}", [128, 2 * 512], BF16)
                etmp = lsb(f"etmp{l}", [128, 512], F32)
                e1b = lsb(f"e1{l}", [128, 2 * 512], F32)
                e2b = lsb(f"e2{l}", [128, 2 * 512], F32)

                st_tab = P.dma("gq", [(tabs[n][:, :], E["T" + n].ap())
                                      for n in ["cq", "sq", "ck", "sk"]],
                               tag=f"tabs{l}")
                st_wqkv = P.dma("gq", [(
                    wqkvB.ap().rearrange("p (ic n) -> p ic n", ic=8),
                    bass.AP(E[f"B{l}_wqkv"], 0,
                            [[1536, 128], [128 * 1536, 8], [1, 1536]]))],
                    tag=f"wqkv{l}")
                st_wsh = P.dma("gq", [(
                    wshB.ap().rearrange("p (ic n) -> p ic n", ic=8),
                    bass.AP(E[f"B{l}_wsh"], 0,
                            [[1024, 128], [128 * 1024, 8], [1, 1024]]))],
                    tag=f"wsh{l}")
                st_wo = P.dma("gq", [(
                    woB.ap().rearrange("p (ic n) -> p ic n", ic=4),
                    bass.AP(E[f"B{l}_wo"], 0,
                            [[1024, 128], [128 * 1024, 4], [1, 1024]]))],
                    tag=f"woL{l}")
                st_vone = P.step("pool", lambda g: g.memset(
                    bass.AP(v_aug, 64, [[2 * 4 * 520, 128], [65, 64], [1, 1]]),
                    1.0), tag=f"vone{l}")

                guard_prev = None
                ptw = {0: None, 1: None}
                rbw = None
                rrow_w = None
                prev_fb = None
                qk_last = None
                b3_last = [None] * NTB
                pre_xbf = {}
                pre_x32 = {}
                b1pe_last = [None] * NTB
                b2pe_last = [None] * NTB
                prev_yapp3 = None

                def mk_xbfB(tb, deps=()):
                    sl = tb % 2
                    if tb == 0:
                        stg = P.step("pool", lambda g: g.memset(
                            bass.AP(xbfB, xbs2(0, 0),
                                    [[2 * 8 * 516, 128], [516, 8], [1, 1]]), 0.0),
                            tag=f"xbfg{l}")
                        return P.dma("gq", [(
                            bass.AP(xbfB, xbs2(sl, 0) + 1,
                                    [[2 * 8 * 516, 128], [516, 8], [1, 512]]),
                            dram3(sin, 0))], deps=[stg], tag=f"xbfB{l}_0")
                    return P.dma("gq", [(
                        bass.AP(xbfB, xbs2(sl, 0),
                                [[2 * 8 * 516, 128], [516, 8], [1, 513]]),
                        dram3(sin, TB * tb - 1, 513))], deps=list(deps),
                        tag=f"xbfB{l}_{tb}")

                def mk_x32B(tb, deps=()):
                    sl = tb % 2
                    if tb == 0:
                        stgx = P.step("pool", lambda g: g.memset(
                            bass.AP(xroll, xrs(0, 0),
                                    [[2 * 8 * 516, 128], [516, 8], [1, 1]]), 0.0),
                            tag=f"xg{l}")
                        return P.dma("sq", [(
                            bass.AP(xroll, xrs(sl, 0) + 1,
                                    [[2 * 8 * 516, 128], [516, 8], [1, 512]]),
                            dram3(sin, 0))], deps=[stgx] + list(deps),
                            tag=f"x32B{l}_{tb}")
                    return P.dma("sq", [(
                        bass.AP(xroll, xrs(sl, 0),
                                [[2 * 8 * 516, 128], [516, 8], [1, 513]]),
                        dram3(sin, TB * tb - 1, 513))], deps=deps,
                        tag=f"x32B{l}_{tb}")

                for tb in range(NTB):
                    sl = tb % 2
                    w = tb
                    st_xbf = pre_xbf.pop(tb) if tb in pre_xbf else mk_xbfB(tb)
                    st_x32 = pre_x32.pop(tb) if tb in pre_x32 else mk_x32B(tb)
                    if tb + 1 < NTB:
                        pre_xbf[tb + 1] = mk_xbfB(
                            tb + 1, deps=[b1pe_last[tb - 1]] if tb >= 1 else [])
                        pre_x32[tb + 1] = mk_x32B(
                            tb + 1, deps=[b3_last[tb - 1]] if tb >= 1 else [])

                    def hsrc(ic, t0, width, sl=sl):
                        off = 1 if ic < 4 else 0
                        return xbfB[:, xbs2(sl, ic) + off + t0:
                                    xbs2(sl, ic) + off + t0 + width]

                    for cx in range(8):
                        isq = cx < 4
                        c = cx % 4
                        colbase = (0 if isq else 512) + 128 * c
                        bq_, bsh_, bss_, bfb_ = [(0, 1, 2, 3), (4, 5, 2, 3)][cx % 2]
                        Tc = tabs["cq"] if isq else tabs["ck"]
                        Ts = tabs["sq"] if isq else tabs["sk"]
                        ppn = pqn if isq else pkn
                        ppsh = pqsh if isq else pksh
                        dstt = q_rot if isq else k_rot
                        dstoff = 512 * c if isq else 512 * (4 * sl + c)

                        def mm_q(t, colbase=colbase, bq_=bq_):
                            last = None
                            for ic in range(8):
                                last = t.matmul(
                                    pb[bq_][:, 0:512],
                                    wqkvB[:, 1536 * ic + colbase:
                                          1536 * ic + colbase + 128],
                                    hsrc(ic, 0, 512),
                                    start=(ic == 0), stop=(ic == 7))
                            return last
                        smq = P.step("pe", mm_q,
                                     deps=[st_xbf, st_wqkv, P.use_bank(bq_)],
                                     tag=f"B1q{l}{tb}{cx}")

                        def mm_sh(t, colbase=colbase, bsh_=bsh_):
                            last = None
                            for ic in range(8):
                                last = t.matmul(
                                    pb[bsh_][:, 0:512],
                                    wshB[:, 1024 * ic + colbase:
                                         1024 * ic + colbase + 128],
                                    hsrc(ic, 0, 512),
                                    start=(ic == 0), stop=(ic == 7))
                            return last
                        smsh = P.step("pe", mm_sh,
                                      deps=[st_xbf, st_wsh, P.use_bank(bsh_)],
                                      tag=f"B1s{l}{tb}{cx}")
                        ssq = P.step("act", lambda s, bq_=bq_, cx=cx: s.activation(
                            qsqb[:, 512 * (cx % 2): 512 * (cx % 2 + 1)],
                            pb[bq_][:, 0:512], AF.Square),
                            deps=[smq], tag=f"B1sq{l}{tb}{cx}")
                        sss = P.step("pe", lambda t, bss_=bss_, cx=cx: t.matmul(
                            pb[bss_][0:2, 0:512], ind2[:, :],
                            qsqb[:, 512 * (cx % 2): 512 * (cx % 2 + 1)],
                            start=True, stop=True),
                            deps=[ssq, st_ms, P.use_bank(bss_)],
                            tag=f"B1ss{l}{tb}{cx}")

                        def fca(s, bss_=bss_, isq=isq):
                            s.activation(sspair[:, :], pb[bss_][0:2, 0:512],
                                         AF.Ln, bias=cst[0:2, C_TINY:C_TINY + 1])
                            return s.activation(
                                fpair[:, :], sspair[:, :], AF.Exp, scale=-0.5,
                                bias=(cst[0:2, C_LN8:C_LN8 + 1] if isq else 0.0))
                        sfc = P.step("act", fca, deps=[sss, prev_fb],
                                     tag=f"B1f{l}{tb}{cx}")
                        P.set_bank(bss_, sfc)
                        sfb = P.step("pe", lambda t, bfb_=bfb_: t.matmul(
                            pb[bfb_][:, 0:512], ind2t[:, :], fpair[:, :],
                            start=True, stop=True),
                            deps=[sfc, P.use_bank(bfb_)], tag=f"B1fb{l}{tb}{cx}")
                        prev_fb = sfb
                        se1 = P.step("dve", lambda e, bq_=bq_, cx=cx, Tc=Tc,
                                     ppn=ppn, tb=tb: e.scalar_tensor_tensor(
                                         e1b[:, 512 * (cx % 2): 512 * (cx % 2 + 1)],
                                         pb[bq_][:, 0:512], ppn[:, :],
                                         Tc[:, TB * tb: TB * (tb + 1)],
                                         ALU.mult, ALU.mult),
                                     deps=[smq, ssq, st_tab, st_cf],
                                     tag=f"B1e1{l}{tb}{cx}")
                        P.set_bank(bq_, [ssq, se1])
                        se2 = P.step("dve", lambda e, bsh_=bsh_, cx=cx, Ts=Ts,
                                     ppsh=ppsh, tb=tb: e.scalar_tensor_tensor(
                                         e2b[:, 512 * (cx % 2): 512 * (cx % 2 + 1)],
                                         pb[bsh_][:, 0:512], ppsh[:, :],
                                         Ts[:, TB * tb: TB * (tb + 1)],
                                         ALU.mult, ALU.mult),
                                     deps=[smsh, st_tab, st_cf],
                                     tag=f"B1e2{l}{tb}{cx}")
                        P.set_bank(bsh_, se2)
                        sadd = P.step("pool", lambda g, cx=cx: g.tensor_add(
                            e1b[:, 512 * (cx % 2): 512 * (cx % 2 + 1)],
                            e1b[:, 512 * (cx % 2): 512 * (cx % 2 + 1)],
                            e2b[:, 512 * (cx % 2): 512 * (cx % 2 + 1)]),
                            deps=[se1, se2], tag=f"B1a{l}{tb}{cx}")
                        sfm = P.step("dve", lambda e, cx=cx, dstt=dstt,
                                     dstoff=dstoff, bfb_=bfb_: e.tensor_tensor(
                                         dstt[:, dstoff: dstoff + 512],
                                         e1b[:, 512 * (cx % 2): 512 * (cx % 2 + 1)],
                                         pb[bfb_][:, 0:512], ALU.mult),
                                     deps=[sadd, sfb,
                                           b2pe_last[tb - 1] if tb >= 1 else None],
                                     tag=f"B1fm{l}{tb}{cx}")
                        P.set_bank(bfb_, sfm)
                        qk_last = sfm
                    last_vev = None
                    for tc in range(4):
                        b = (0, 1, 4, 5)[tc]

                        def mm_v(t, tc=tc, b=b):
                            last = None
                            for ic in range(8):
                                last = t.matmul(
                                    pb[b][:, 0:512],
                                    hsrc(ic, 128 * tc, 128),
                                    wqkvB[:, 1536 * ic + 1024: 1536 * ic + 1536],
                                    start=(ic == 0), stop=(ic == 7))
                            return last
                        smv = P.step("pe", mm_v,
                                     deps=[st_xbf, st_wqkv, P.use_bank(b)],
                                     tag=f"B1v{l}{tb}{tc}")
                        vdst = bass.AP(v_aug, 520 * (4 * sl + tc),
                                       [[2 * 4 * 520, 128], [65, 8], [1, 64]])
                        psrc = bass.AP(pbig, 512 * b, [[4096, 128], [64, 8], [1, 64]])
                        sev = P.step("act", lambda s, vdst=vdst, psrc=psrc: s.copy(
                            vdst, psrc),
                            deps=[smv, st_vone,
                                  b2pe_last[tb - 1] if tb >= 1 else None],
                            tag=f"B1ve{l}{tb}{tc}")
                        P.set_bank(b, sev)
                        last_vev = sev
                        b1pe_last[tb] = smv

                    nmev = [None] * 4
                    for qt in range(4):
                        redst = []
                        for hp in range(4):
                            lbspan = 512 - 128 * qt
                            curspan = 128 * (qt + 1)
                            hc = hp

                            def mm_stat2(t, hc=hc, qt=qt, lbspan=lbspan,
                                         curspan=curspan, sl=sl, w=w):
                                qs2 = [q_rot[64 * hh:64 * hh + 64,
                                             512 * hc + 128 * qt:
                                             512 * hc + 128 * (qt + 1)]
                                       for hh in range(2)]
                                if w > 0:
                                    kof = 512 * (4 * ((w - 1) % 2) + hc) + 128 * qt
                                    for hh in range(2):
                                        t.matmul(pb[hh][:, 0:lbspan], qs2[hh],
                                                 k_rot[64 * hh:64 * hh + 64,
                                                       kof: kof + lbspan],
                                                 start=True, stop=False)
                                kof = 512 * (4 * sl + hc)
                                for hh in range(2):
                                    t.matmul(pb[2 + hh][:, 0:curspan], qs2[hh],
                                             k_rot[64 * hh:64 * hh + 64,
                                                   kof: kof + curspan],
                                             start=True, stop=False)
                                if w > 0:
                                    for hh in range(2):
                                        t.matmul(pb[hh][:, 0:128], ident_bf[:, :],
                                                 maskGE[:, :], start=False,
                                                 stop=True)
                                last = None
                                for hh in range(2):
                                    last = t.matmul(
                                        pb[2 + hh][:, 128 * qt: 128 * (qt + 1)],
                                        ident_bf[:, :], maskLE[:, :],
                                        start=False, stop=True)
                                return last
                            sstat = P.step("pe", mm_stat2,
                                           deps=[last_vev, qk_last, st_cst,
                                                 P.use_bank(0), P.use_bank(1),
                                                 P.use_bank(2), P.use_bank(3)],
                                           tag=f"B2st{l}{tb}{qt}{hp}")

                            def red2(e, hp=hp, w=w, lbspan=lbspan,
                                     curspan=curspan):
                                last = None
                                for hh in range(2):
                                    h = 2 * hp + hh
                                    if w > 0:
                                        e.reduce_max(negtmp[:, 0:1],
                                                     pb[hh][:, 0:lbspan],
                                                     axis=AX.X, negate=True)
                                        e.reduce_max(negtmp[:, 1:2],
                                                     pb[2 + hh][:, 0:curspan],
                                                     axis=AX.X, negate=True)
                                        last = e.tensor_tensor(
                                            negmax8[:, h:h + 1], negtmp[:, 0:1],
                                            negtmp[:, 1:2], ALU.min)
                                    else:
                                        last = e.reduce_max(
                                            negmax8[:, h:h + 1],
                                            pb[2 + hh][:, 0:curspan],
                                            axis=AX.X, negate=True)
                                return last
                            sred = P.step("dve", red2, deps=[sstat],
                                          tag=f"B2r{l}{tb}{qt}{hp}")
                            for bb in range(4):
                                P.set_bank(bb, sred)
                            redst.append(sred)
                        sttp = P.step("pe", lambda t: t.transpose(
                            bass.AP(pbig, 512 * 7, [[4096, 8], [1, 128]]),
                            negmax8[:, :], ident[:, :]),
                            deps=redst + [P.use_bank(7)], tag=f"B2tp{l}{tb}{qt}")
                        stnm8 = P.step("act", lambda s, qt=qt: s.copy(
                            nm32[:, 128 * qt: 128 * (qt + 1)],
                            bass.AP(pbig, 512 * 7, [[4096, 8], [1, 128]])),
                            deps=[sttp], tag=f"B2nm{l}{tb}{qt}")
                        P.set_bank(7, stnm8)
                        nmev[qt] = stnm8

                    for qt in range(4):
                        stnm8 = nmev[qt]
                        for hp in range(4):
                            bV = 4 + hp % 2
                            pvst = []
                            chunks = []
                            if w > 0:
                                for kc in range(qt, 4):
                                    chunks.append(((w - 1) % 2, kc,
                                                   maskLE if kc == qt else None))
                            for kc in range(0, qt + 1):
                                chunks.append((sl, kc,
                                               maskGE if kc == qt else None))
                            nch = len(chunks)
                            hcp = hp

                            def mm_stq2(t, hcp=hcp, hp=hp, chunks=chunks, qt=qt):
                                last = None
                                nch = len(chunks)
                                n1 = min(nch, 4)
                                qs2 = [q_rot[64 * hh:64 * hh + 64,
                                             512 * hcp + 128 * qt:
                                             512 * hcp + 128 * (qt + 1)]
                                       for hh in range(2)]
                                for pos, (par, kc, msk) in enumerate(chunks):
                                    fo = 128 * (pos % 4)
                                    kof0 = 512 * (4 * par + hcp) + 128 * kc
                                    for hh in range(2):
                                        bT = 2 * hh if pos < 4 else 2 * hh + 1
                                        t.matmul(pb[bT][:, fo: fo + 128],
                                                 k_rot[64 * hh:64 * hh + 64,
                                                       kof0: kof0 + 128],
                                                 qs2[hh], start=(pos in (0, 4)),
                                                 stop=False)
                                for pos, (par, kc, msk) in enumerate(chunks):
                                    if msk is None:
                                        continue
                                    fo = 128 * (pos % 4)
                                    for hh in range(2):
                                        bT = 2 * hh if pos < 4 else 2 * hh + 1
                                        t.matmul(pb[bT][:, fo: fo + 128],
                                                 ident_bf[:, :], msk[:, :],
                                                 start=False, stop=False)
                                nrep1 = bass.AP(nm32, 128 * qt,
                                                [[512, 8], [0, n1], [1, 128]])
                                for hh in range(2):
                                    h = 2 * hp + hh
                                    last = t.matmul(
                                        pb[2 * hh][:, 0:128 * n1],
                                        sel8[:, 128 * h: 128 * (h + 1)],
                                        nrep1, start=False, stop=True)
                                if nch > 4:
                                    nrep2 = bass.AP(nm32, 128 * qt,
                                                    [[512, 8], [0, nch - 4],
                                                     [1, 128]])
                                    for hh in range(2):
                                        h = 2 * hp + hh
                                        last = t.matmul(
                                            pb[2 * hh + 1][:, 0:128 * (nch - 4)],
                                            sel8[:, 128 * h: 128 * (h + 1)],
                                            nrep2, start=False, stop=True)
                                return last
                            smTp = P.step("pe", mm_stq2,
                                          deps=[stnm8, P.use_bank(0),
                                                P.use_bank(1), P.use_bank(2),
                                                P.use_bank(3)],
                                          tag=f"B2t{l}{tb}{qt}{hp}")
                            for hh in range(2):
                                h = 2 * hp + hh
                                hc = h // 2
                                hr = 64 * hh
                                bT1 = 2 * hh
                                bT2 = 2 * hh + 1
                                smT = smTp

                                def expf(s, h=h, hh=hh, nch=nch, bT1=bT1):
                                    po = 640 * hh
                                    return s.activation(
                                        PT[:, po: po + 128 * nch],
                                        pbig.ap()[:, 512 * bT1:
                                                  512 * bT1 + 128 * nch], AF.Exp)
                                sexp = P.step("act", expf, deps=[smT, ptw[hh]],
                                              tag=f"B2e{l}{tb}{qt}{h}")
                                P.set_bank(bT1, sexp)
                                P.set_bank(bT2, sexp)

                                def mm_pv(t, h=h, hh=hh, chunks=chunks, bV=bV):
                                    last = None
                                    po = 640 * hh
                                    for pos, (par, kc, msk) in enumerate(chunks):
                                        vof = 520 * (4 * par + kc) + 65 * h
                                        fo = po + 128 * pos
                                        last = t.matmul(
                                            pb[bV][0:65, 128 * hh: 128 * hh + 128],
                                            v_aug[:, vof: vof + 65],
                                            PT[:, fo: fo + 128],
                                            start=(pos == 0),
                                            stop=(pos == len(chunks) - 1))
                                    return last
                                dpv = [sexp] + (P.use_bank(bV) if hh == 0 else [])
                                spv = P.step("pe", mm_pv, deps=dpv,
                                             tag=f"B2pv{l}{tb}{qt}{h}")
                                ptw[hh] = spv
                                pvst.append(spv)
                            def rcp(s, bV=bV):
                                ap = bass.AP(pbig, 64 * 4096 + 512 * bV,
                                             [[4096, 1], [1, 256]])
                                s.activation(rrow[0:1, 0:256], ap, AF.Ln)
                                return s.activation(rrow[0:1, 0:256],
                                                    rrow[0:1, 0:256], AF.Exp,
                                                    scale=-1.0)
                            src_ = P.step("act", rcp, deps=pvst + [rrow_w],
                                          tag=f"B2rc{l}{tb}{qt}{hp}")
                            bRB = 6
                            srb = P.step("pe", lambda t, bRB=bRB: t.matmul(
                                pb[bRB][0:64, 0:256], ones_row32[0:1, 0:64],
                                rrow[0:1, 0:256], start=True, stop=True),
                                deps=[src_, P.use_bank(bRB)],
                                tag=f"B2rb{l}{tb}{qt}{hp}")
                            rrow_w = srb
                            srbe = P.step("act", lambda s, bRB=bRB: s.copy(
                                rb32[0:64, 0:256], pb[bRB][0:64, 0:256]),
                                deps=[srb, rbw], tag=f"B2rbe{l}{tb}{qt}{hp}")
                            P.set_bank(bRB, srbe)
                            soes = []
                            for hh in range(2):
                                h = 2 * hp + hh
                                hc = h // 2
                                hr = 64 * hh
                                soe = P.step("dve", lambda e, hc=hc, hr=hr, qt=qt,
                                             hh=hh, bV=bV: e.tensor_tensor(
                                                 o_fm[hr:hr + 64,
                                                      512 * hc + 128 * qt:
                                                      512 * hc + 128 * (qt + 1)],
                                                 pb[bV][0:64,
                                                        128 * hh: 128 * hh + 128],
                                                 rb32[0:64,
                                                      128 * hh: 128 * hh + 128],
                                                 ALU.mult),
                                             deps=[pvst[hh], srbe],
                                             tag=f"B2oe{l}{tb}{qt}{h}")
                                soes.append(soe)
                            P.set_bank(bV, soes)
                            rbw = soes
                            b2pe_last[tb] = srb
                    P.flush()

                    if tb == 0:
                        stguard = P.step("pool", lambda g: g.memset(
                            bass.AP(zbuf, 0, [[8 * 513, 128], [513, 8], [1, 1]]),
                            0.0), tag=f"zg{l}")
                    else:
                        stguard = P.step("pool", lambda g: g.tensor_copy(
                            bass.AP(zbuf, 0, [[8 * 513, 128], [513, 8], [1, 1]]),
                            bass.AP(zbuf, 512, [[8 * 513, 128], [513, 8], [1, 1]])),
                            deps=[guard_prev] + P.ln_zr, tag=f"zg{l}{tb}")
                    zst = []
                    for oc in range(8):
                        b = 4 + oc % 2

                        def mm_wo2(t, oc=oc, b=b):
                            last = None
                            for icc in range(4):
                                last = t.matmul(
                                    pb[b][:, 0:512],
                                    woB[:, 1024 * icc + 128 * oc:
                                        1024 * icc + 128 * (oc + 1)],
                                    o_fm[:, 512 * icc: 512 * (icc + 1)],
                                    start=(icc == 0), stop=(icc == 3))
                            return last
                        stm = P.step("pe", mm_wo2,
                                     deps=[st_wo, rbw, P.use_bank(b)],
                                     tag=f"B3wo{l}{tb}{oc}")

                        def ev(e, oc=oc, b=b, sl=sl):
                            return e.tensor_tensor(
                                zbuf[:, zc(oc): zc(oc) + 512],
                                pb[b][:, 0:512],
                                xroll[:, xrs(sl, oc) + 1: xrs(sl, oc) + 513],
                                ALU.add)
                        ste = P.step("dve", ev,
                                     deps=[stm, st_x32, stguard,
                                           x1_spills[tb - 1] if tb >= 1 else None]
                                     + P.ln_zr,
                                     tag=f"B3z{l}{tb}{oc}")
                        P.set_bank(b, ste)
                        zst.append(ste)
                    guard_prev = zst[-1]
                    b3_last[tb] = zst[-1]
                    x1_spills[tb] = P.dma("sq", [(
                        dram3(sout, TB * tb),
                        bass.AP(zbuf, 1, [[8 * 513, 128], [513, 8], [1, 512]]))],
                        deps=zst, tag=f"x1sp{l}{tb}")

                    def zsrc(oc):
                        off = zc(oc) if oc < 4 else zc(oc) - 1
                        return zbuf[:, off: off + 512]
                    s7, s1s = ln_stats(zst, zsrc)
                    P.ln_ab = []
                    P.ln_zr = list(s1s)
                    yapp = []
                    for oc in range(8):
                        sa1 = P.step("pool", lambda g, oc=oc: g.tensor_sub(
                            etmp[:, :], zsrc(oc), Bb32[:, :]),
                            deps=[s7, prev_yapp3], tag="yapp1")
                        sa2 = P.step("pool", lambda g: g.tensor_mul(
                            etmp[:, :], etmp[:, :], Ab32[:, :]), deps=[sa1],
                            tag="yapp2")
                        sa3 = P.step("dve", lambda e, oc=oc, sl=sl: e.tensor_scalar(
                            ybuf[:, ybs(sl, oc): ybs(sl, oc) + 512],
                            etmp[:, :], params[(l, "lng")][:, oc:oc + 1],
                            params[(l, "lnb")][:, oc:oc + 1], ALU.mult, ALU.add),
                            deps=[sa2, y_spills[tb - 1] if tb >= 1 else None],
                            tag="yapp3")
                        prev_yapp3 = sa3
                        yapp.append(sa3)
                        P.ln_ab.append(sa1); P.ln_ab.append(sa2)
                        P.ln_zr.append(sa1)
                    y_spills[tb] = P.dma("sq", [(
                        bass.AP(yb, TB * tb, [[T, 128], [128 * T, 8], [1, 512]]),
                        bass.AP(ybuf, 0,
                                [[8 * 512, 128], [512, 8], [1, 512]]))],
                        deps=yapp, tag=f"ysp{l}{tb}")

                P.wait_only("pool", [x1_spills[NTB - 1], y_spills[NTB - 1]],
                            tag=f"drainB13{l}")
                P.flush()

            G2 = ExitStack()
            x2_spills = []
            with G2:
                gsb = lambda name, shape, dt: G2.enter_context(
                    nc.sbuf_tensor(name, shape, dt))
                w1B = gsb(f"w1B{l}", [128, 8 * 5460], BF16)
                w2g = gsb(f"w2g{l}", [128, 2 * 3 * 1024], BF16)
                actB = gsb(f"actB{l}", [128, NJ * 512], BF16)
                ybloc = gsb(f"ybloc{l}", [128, 8 * 512], BF16)
                gel32 = gsb(f"gel{l}", [128, 512], F32)

                JR = [(0, 6), (6, 11), (11, 17), (17, 22)]
                w1parts = []
                for (jlo, jhi) in JR:
                    c0 = 128 * jlo
                    c1 = min(128 * jhi, HID)
                    for base in (0, HID):
                        w1parts.append((jlo, jhi, P.dma("gq", [(
                            bass.AP(w1B, base + c0,
                                    [[8 * 5460, 128], [5460, 8], [1, c1 - c0]]),
                            bass.AP(E[f"B{l}_w1"], base + c0,
                                    [[5460, 128], [128 * 5460, 8], [1, c1 - c0]]))],
                            tag=f"w1_{l}_{base}_{jlo}")))

                def w1dep_for(j, gate):
                    res = [stp for (jlo, jhi, stp) in w1parts if jlo <= j < jhi]
                    return res[gate]

                pre4 = {}
                ev2_last = None
                act_carry = None
                sg_carry = None

                def mk_x1ld(tb, deps=()):
                    return P.dma("sq", [(
                        bass.AP(xroll, xrs(tb % 2, 0) + 1,
                                [[2 * 8 * 516, 128], [516, 8], [1, 512]]),
                        dram3(sout, TB * tb))],
                        deps=[x1_spills[tb]] + list(deps), tag=f"x1ld{l}{tb}")

                def mk_yld(tb, deps=()):
                    return P.dma("sq", [(
                        bass.AP(ybloc, 0, [[8 * 512, 128], [512, 8], [1, 512]]),
                        bass.AP(yb, TB * tb, [[T, 128], [128 * T, 8], [1, 512]]))],
                        deps=[y_spills[tb]] + list(deps), tag=f"yld{l}{tb}")

                for tb in range(NTB):
                    sl = tb % 2
                    if tb in pre4:
                        pp = pre4.pop(tb)
                        stx1 = pp[0]
                        sty = pp[1] if len(pp) > 1 else mk_yld(
                            tb, deps=[sg_carry] if sg_carry else [])
                    else:
                        stx1 = mk_x1ld(tb)
                        sty = mk_yld(tb)
                    if tb + 1 < NTB:
                        pre4[tb + 1] = [
                            mk_x1ld(tb + 1, deps=[ev2_last] if ev2_last else [])]

                    def w2load(g, deps, tb=tb):
                        j0 = 3 * g
                        njg = min(3, NJ - 3 * g)
                        trs = [(
                            bass.AP(w2g, 1024 * 3 * (g % 2),
                                    [[2 * 3 * 1024, 128], [1024, njg], [1, 1024]]),
                            bass.AP(E[f"B{l}_w2"], 1024 * 128 * j0,
                                    [[1024, 128], [128 * 1024, njg], [1, 1024]]))]
                        return P.dma("gq", trs, deps=deps, tag=f"w2g{l}{tb}{g}")

                    if tb == 0:
                        w2ld = [None] * 8
                        w2ld[0] = w2load(0, [])
                        w2ld[1] = w2load(1, [])
                    else:
                        w2ld = [None] * 8
                        w2ld[0] = nw2[0]
                        w2ld[1] = nw2[1]
                    nw2 = [None, None]

                    prev_act = [None, None]
                    actst = [None] * NJ
                    for j in range(NJ):
                        wj = 128 if j < NJ - 1 else HID - 128 * (NJ - 1)
                        ba_, bg_ = (0, 1) if j % 2 == 0 else (2, 3)
                        acol = 128 * j
                        gcol = HID + 128 * j

                        def mma(t, wj=wj, ba_=ba_, acol=acol):
                            last = None
                            for ic in range(8):
                                last = t.matmul(
                                    pb[ba_][0:wj, 0:512],
                                    w1B[:, 5460 * ic + acol: 5460 * ic + acol + wj],
                                    ybloc[:, 512 * ic: 512 * (ic + 1)],
                                    start=(ic == 0), stop=(ic == 7))
                            return last
                        sa_ = P.step("pe", mma,
                                     deps=[sty, w1dep_for(j, 0), P.use_bank(ba_)],
                                     tag=f"B4a{l}{tb}{j}")

                        def mmg(t, wj=wj, bg_=bg_, gcol=gcol):
                            last = None
                            for ic in range(8):
                                last = t.matmul(
                                    pb[bg_][0:wj, 0:512],
                                    w1B[:, 5460 * ic + gcol: 5460 * ic + gcol + wj],
                                    ybloc[:, 512 * ic: 512 * (ic + 1)],
                                    start=(ic == 0), stop=(ic == 7))
                            return last
                        sg_ = P.step("pe", mmg,
                                     deps=[sty, w1dep_for(j, 1), P.use_bank(bg_)],
                                     tag=f"B4g{l}{tb}{j}")
                        sgel = P.step("act", lambda s, j=j, wj=wj, bg_=bg_:
                                      s.activation(
                                          gel32[0:wj, 0:512],
                                          pb[bg_][0:wj, 0:512], AF.Gelu,
                                          bias=params[(l, "b1g")][0:wj, j:j + 1]),
                                      deps=[sg_, actst[j - 1] if j >= 1
                                            else act_carry],
                                      tag=f"B4gel{l}{tb}{j}")
                        P.set_bank(bg_, sgel)
                        sact = P.step("dve", lambda e, j=j, wj=wj, ba_=ba_:
                                      e.scalar_tensor_tensor(
                                          actB[0:wj, 512 * j: 512 * j + 512],
                                          pb[ba_][0:wj, 0:512],
                                          params[(l, "b1a")][0:wj, j:j + 1],
                                          gel32[0:wj, 0:512],
                                          ALU.add, ALU.mult),
                                      deps=[sa_, sgel], tag=f"B4act{l}{tb}{j}")
                        P.set_bank(ba_, sact)
                        prev_act[j % 2] = sact
                        actst[j] = sact
                    act_carry = actst[NJ - 1]
                    sg_carry = sg_
                    if tb + 1 < NTB and (tb + 1) in pre4:
                        pre4[tb + 1].append(mk_yld(tb + 1, deps=[sg_carry]))

                    mmsteps = []
                    for g in range(8):
                        j0 = 3 * g
                        njg = min(3, NJ - 3 * g)

                        def mmw2(t, g=g, j0=j0, njg=njg):
                            last = None
                            for jj in range(njg):
                                j = j0 + jj
                                wj = 128 if j < NJ - 1 else HID - 128 * (NJ - 1)
                                for oc in range(8):
                                    last = t.matmul(
                                        pb[oc][:, 0:512],
                                        w2g[0:wj,
                                            1024 * (3 * (g % 2) + jj) + 128 * oc:
                                            1024 * (3 * (g % 2) + jj) +
                                            128 * (oc + 1)],
                                        actB[0:wj, 512 * j: 512 * j + 512],
                                        start=(j == 0), stop=(j == NJ - 1))
                            return last
                        depsl = [w2ld[g], actst[min(j0 + njg, NJ) - 1]]
                        if g == 0:
                            depsl += [P.use_bank(bb) for bb in range(8)]
                        sm = P.step("pe", mmw2, deps=depsl, tag=f"B4w2{l}{tb}{g}")
                        mmsteps.append(sm)
                        if g + 2 < 8:
                            w2ld[g + 2] = w2load(g + 2, [sm])
                        elif tb + 1 < NTB:
                            nw2[g - 6] = w2load(g - 6, [sm], tb=tb + 1)
                    zst2 = []
                    for oc in range(8):
                        def ev2(e, oc=oc, sl=sl):
                            return e.scalar_tensor_tensor(
                                zbuf[:, zc(oc): zc(oc) + 512],
                                pb[oc][:, 0:512],
                                params[(l, "b2")][:, oc:oc + 1],
                                xroll[:, xrs(sl, oc) + 1: xrs(sl, oc) + 513],
                                ALU.add, ALU.add)
                        s_ = P.step("dve", ev2,
                                    deps=[mmsteps[-1], stx1,
                                          x2_spills[-1] if x2_spills else None],
                                    tag=f"B4z{l}{tb}{oc}")
                        P.set_bank(oc, s_)
                        zst2.append(s_)
                    ev2_last = zst2[-1]
                    trs = [(dram3(sout, TB * tb),
                            bass.AP(zbuf, 1, [[8 * 513, 128], [513, 8], [1, 512]]))]
                    if want_out:
                        trs.append((dram3(out, TB * tb),
                                    bass.AP(zbuf, 1,
                                            [[8 * 513, 128], [513, 8], [1, 512]])))
                    x2_spills.append(P.dma("sq", trs, deps=zst2,
                                           tag=f"x2sp{l}{tb}"))
                P.wait_only("pool", [x2_spills[-1]], tag=f"drainB4{l}")
                P.flush()
            return x2_spills[-1]

        last0 = emit_layer(0, xa, xb, want_out=(upto == "L0"))
        if upto == "L0":
            with nc.Block() as block:
                @block.sync
                def _(s):
                    s.wait_ge(P.sem["sq"], P.cnt["sq"])
                    s.wait_ge(P.sem["gq"], P.cnt["gq"])
            return nc
        last1 = emit_layer(1, xb, xc, want_out=False)

        GF = ExitStack()
        with GF:
            zbufF = GF.enter_context(nc.sbuf_tensor("zbufF", [128, 8 * 513], F32))
            xrollF = GF.enter_context(
                nc.sbuf_tensor("xrollF", [128, 2 * 8 * 516], F32))
            outsp = []
            fz_last = [None] * NTB
            prefin = {}

            def mk_fin(tb, deps=()):
                xr = xroll if tb % 2 == 0 else xrollF
                st0 = P.dma("sq", [(
                    bass.AP(xr, xrs(0, 0) + 1,
                            [[2 * 8 * 516, 128], [516, 8], [1, 512]]),
                    dram3(xa, TB * tb))], deps=[last1] + list(deps),
                    tag=f"fl0{tb}")
                st1 = P.dma("sq", [(
                    bass.AP(xr, xrs(1, 0) + 1,
                            [[2 * 8 * 516, 128], [516, 8], [1, 512]]),
                    dram3(xc, TB * tb))], deps=[last1] + list(deps),
                    tag=f"fl1{tb}")
                return (st0, st1)

            for tb in range(NTB):
                zb = zbuf if tb % 2 == 0 else zbufF
                xr = xroll if tb % 2 == 0 else xrollF
                if tb in prefin:
                    st0, st1 = prefin.pop(tb)
                else:
                    st0, st1 = mk_fin(tb)
                zst = []
                for oc in range(8):
                    s_ = P.step("dve", lambda e, oc=oc, zb=zb, xr=xr:
                                e.tensor_tensor(
                                    zb[:, zc(oc): zc(oc) + 512],
                                    xr[:, xrs(0, oc) + 1: xrs(0, oc) + 513],
                                    xr[:, xrs(1, oc) + 1: xrs(1, oc) + 513],
                                    ALU.add),
                                deps=[st0, st1,
                                      outsp[tb - 2] if tb >= 2 else None]
                                + P.ln_zr, tag=f"fz{tb}{oc}")
                    zst.append(s_)
                fz_last[tb] = zst[-1]
                if tb + 1 < NTB:
                    prefin[tb + 1] = mk_fin(
                        tb + 1, deps=[fz_last[tb - 1]] if tb >= 1 else [])
                s7, s1s = ln_stats(zst, lambda oc, zb=zb: zb[:, zc(oc): zc(oc) + 512])
                appst = []
                for oc in range(8):
                    src = zb[:, zc(oc): zc(oc) + 512]

                    def app(g, src=src):
                        g.tensor_sub(src, src, Bb32[:, :])
                        return g.tensor_mul(src, src, Ab32[:, :])
                    appst.append(P.step("pool", app, deps=[s7], tag="lnappF"))
                P.ln_ab = list(appst)
                P.ln_zr = s1s + appst
                outsp.append(P.dma("sq", [(
                    dram3(out, TB * tb),
                    bass.AP(zb, 1, [[8 * 513, 128], [513, 8], [1, 512]]))],
                    deps=appst, tag=f"outF{tb}"))
            P.flush()

        with nc.Block() as block:
            @block.sync
            def _(s):
                s.wait_ge(P.sem["sq"], P.cnt["sq"])
                s.wait_ge(P.sem["gq"], P.cnt["gq"])
    return nc


def kernel(**inputs):
    consts = host_consts()
    shared = host_layout(inputs)
    shared.update(consts)
    nc = build_nc("full")
    x = np.asarray(inputs["L0_attn_x"])
    in_maps = []
    for b in range(8):
        m = dict(shared)
        m["xT"] = np.ascontiguousarray(x[b].T)
        in_maps.append(m)
    res = run_bass_kernel_spmd(nc, in_maps, core_ids=list(range(8)))
    outs = [np.ascontiguousarray(r["yT"].T) for r in res.results]
    return np.stack(outs, axis=0)
